# version 05
# baseline (speedup 1.0000x reference)
import math
import numpy as np
import ml_dtypes
from contextlib import ExitStack

import concourse.bass as bass
import concourse.mybir as mybir
from concourse.bass_utils import run_bass_kernel_spmd

F32 = mybir.dt.float32
BF16 = mybir.dt.bfloat16
AF = mybir.ActivationFunctionType
ALU = mybir.AluOpType

NCORE = 8
B, S, D, H, E = 2, 4096, 2048, 16, 128
HL = 4
TC = 1024
DC = D // 128
HID = 5461
HIDP = 5504
NHC = HIDP // 128
HID_SH = HIDP // NCORE
D_SH = D // NCORE
NQ = S // 512
QSCL = float(1.0 / math.sqrt(128.0))
W_N = float(HID * D)
LN_EPS = 1e-5

PB_SZ = 70656
OFF_QT = 0
OFF_KT = 8192
OFF_VT = 16384
OFF_WQKV = 24576
OFF_LNCH = 36864
OFF_Z = 53248
OFF_LN2XT = 0
OFF_ATTN = 16384
OFF_WO = 32768
OFF_P2STG = 40960
OFF_SQB = 45056
OFF_GT = 16384
OFF_FBUF = 38400
OFF_SLAB = 38400
OFF_FSTG = 60416
OFF_SIL2 = 68608

PF_SZ = 6656
PFO_XA, PFO_XB = 0, 1024
PFO_SCR = 2048
PFO_TMP = 3072
PFO_QW = 4096
PFO_X1A, PFO_X1B = 5472, 5984


def _ts(i, n):
    return slice(i * n, (i + 1) * n)


class Ctx:
    def __init__(self):
        self.h = {}
        self.n = {}

    def add(self, name, handle):
        self.h[name] = handle
        self.n[name] = 0

    def inc(self, instr, name, k=1):
        instr.then_inc(self.h[name], k)
        self.n[name] += k
        return self.n[name]

    def val(self, name):
        return self.n[name]

    def wait(self, eng, name, v=None):
        vv = self.n[name] if v is None else v
        if vv > 0:
            eng.wait_ge(self.h[name], vv)


def build_program(nc, taps=()):
    taps = set(taps)
    tap_t = {}

    xT = nc.dram_tensor("xT", [D, TC], F32, kind="ExternalInput").ap()
    wqkv_in = nc.dram_tensor("wqkv_in", [128, 3, HL, DC, 128], F32, kind="ExternalInput").ap()
    wo_in = nc.dram_tensor("wo_in", [128, HL, DC, 128], F32, kind="ExternalInput").ap()
    w1T = nc.dram_tensor("w1T", [D_SH, HIDP], F32, kind="ExternalInput").ap()
    w3T = nc.dram_tensor("w3T", [D_SH, HIDP], F32, kind="ExternalInput").ap()
    w2T = nc.dram_tensor("w2T", [HID_SH, D], F32, kind="ExternalInput").ap()
    masks = nc.dram_tensor("masks", [4, 128, 512], BF16, kind="ExternalInput").ap()
    ident = nc.dram_tensor("ident", [128, 128], BF16, kind="ExternalInput").ap()
    outT = nc.dram_tensor("outT", [D, TC], F32, kind="ExternalOutput").ap()

    lnx_in = nc.dram_tensor("lnx_in", [D, TC], BF16)
    lnx_out = nc.dram_tensor("lnx_out", [4 * D, TC], BF16)
    ar_in = nc.dram_tensor("ar_in", [1, 8], F32)
    ar_out = nc.dram_tensor("ar_out", [1, 8], F32)
    t1_in = nc.dram_tensor("t1_in", [D_SH, HIDP], BF16)
    t1_out = nc.dram_tensor("t1_out", [D, HIDP], BF16)
    t3_in = nc.dram_tensor("t3_in", [D_SH, HIDP], BF16)
    t3_out = nc.dram_tensor("t3_out", [D, HIDP], BF16)
    t2_in = nc.dram_tensor("t2_in", [HID_SH, D], BF16)
    t2_out = nc.dram_tensor("t2_out", [HIDP, D], BF16)
    rs_in = nc.dram_tensor("rs_in", [4, D, TC], BF16)
    rs_out = nc.dram_tensor("rs_out", [D, TC], BF16)
    x1_scr = nc.dram_tensor("x1_scr", [D, TC], F32)

    def tapt(name, shape, dtype):
        if name in taps and name not in tap_t:
            tap_t[name] = nc.dram_tensor("tap_" + name, shape, dtype, kind="ExternalOutput").ap()
        return tap_t.get(name)

    grp4 = [[0, 1, 2, 3], [4, 5, 6, 7]]
    grp8 = [list(range(8))]

    with ExitStack() as ectx:
        pb = ectx.enter_context(nc.sbuf_tensor("pb", [128, PB_SZ], BF16))
        pf = ectx.enter_context(nc.sbuf_tensor("pf", [128, PF_SZ], F32))
        ones_fc = ectx.enter_context(nc.sbuf_tensor("ones_fc", [128, 1], F32))
        ones_fr = ectx.enter_context(nc.sbuf_tensor("ones_fr", [1, 128], F32))
        ones_bc = ectx.enter_context(nc.sbuf_tensor("ones_bc", [128, 1], BF16))
        mask_s = ectx.enter_context(nc.sbuf_tensor("mask_s", [128, 4, 512], BF16))
        id_s = ectx.enter_context(nc.sbuf_tensor("id_s", [128, 128], BF16))
        rows = ectx.enter_context(nc.sbuf_tensor("rows", [1, 4608], F32))
        st1 = ectx.enter_context(nc.sbuf_tensor("st1", [128, 16], F32))
        exp_s = ectx.enter_context(nc.sbuf_tensor("exp_s", [128, 2, 512], BF16))
        expm_s = ectx.enter_context(nc.sbuf_tensor("expm_s", [128, 2, 512], BF16))
        vtmp_s = ectx.enter_context(nc.sbuf_tensor("vtmp_s", [128, 512], BF16))
        inv_s = ectx.enter_context(nc.sbuf_tensor("inv_s", [1, 512], F32))
        sil_s = ectx.enter_context(nc.sbuf_tensor("sil_s", [128, 512], BF16))
        qsg_s = ectx.enter_context(nc.sbuf_tensor("qsg_s", [128, 3, 688], BF16))
        qts_s = ectx.enter_context(nc.sbuf_tensor("qts_s", [128, 2, 688], BF16))

        PS = [ectx.enter_context(nc.psum_tensor("ps%d" % i, [128, 512], F32)) for i in range(6)]
        PT = [ectx.enter_context(nc.psum_tensor("pt%d" % i, [128, 1024], BF16)) for i in range(2)]

        C = Ctx()
        for nm in ("dS", "dS2", "dW", "dQ", "dQ2", "dG", "dG2", "dG3", "tE", "vE", "sE", "cC"):
            C.add(nm, ectx.enter_context(nc.semaphore(nm)))
        block = ectx.enter_context(nc.Block())

        lnxown = pb[:, 0:16384].rearrange("p (c t) -> p c t", c=16)
        qt = pb[:, OFF_QT:OFF_QT + 8192].rearrange("p (h t) -> p h t", h=2)
        kt = pb[:, OFF_KT:OFF_KT + 8192].rearrange("p (h t) -> p h t", h=2)
        vt = pb[:, OFF_VT:OFF_VT + 8192].rearrange("p (h k e) -> p h k e", h=2, k=32)
        wqkv = pb[:, OFF_WQKV:OFF_WQKV + 12288].rearrange("p (m h c e) -> p m h c e", m=3, h=2, c=DC)
        lnch = pb[:, OFF_LNCH:OFF_LNCH + 16384].rearrange("p (b c t) -> p b c t", b=2, c=16)
        z_v = pb[:, OFF_Z:OFF_Z + 16384].rearrange("p (h t) -> p h t", h=HL)
        ln2xt = pb[:, OFF_LN2XT:OFF_LN2XT + 16384].rearrange("p (c t) -> p c t", c=16)
        attn_t = pb[:, OFF_ATTN:OFF_ATTN + 16384].rearrange("p (c t) -> p c t", c=16)
        wo_v = pb[:, OFF_WO:OFF_WO + 8192].rearrange("p (h c e) -> p h c e", h=HL, c=DC)
        p2stg = pb[:, OFF_P2STG:OFF_P2STG + 4096].rearrange("p (t c) -> p t c", t=8)
        sqb = pb[:, OFF_SQB:OFF_SQB + 1024]
        gt_v = pb[:, OFF_GT:OFF_GT + NHC * 512].rearrange("p (h t) -> p h t", h=NHC)
        fbuf = pb[:, OFF_FBUF:OFF_FBUF + 16384].rearrange("p (b c h) -> p b c h", b=4, c=16)
        slab = pb[:, OFF_SLAB:OFF_SLAB + NHC * 512].rearrange("p (h d) -> p h d", h=NHC)
        fstg = pb[:, OFF_FSTG:OFF_FSTG + 8192].rearrange("p (c t) -> p c t", c=16)
        fsq_v = pb[:, OFF_SIL2:OFF_SIL2 + 512]

        xa = pf[:, PFO_XA:PFO_XA + 1024]
        xb = pf[:, PFO_XB:PFO_XB + 1024]
        fscr = pf[:, PFO_SCR:PFO_SCR + 1024]
        ftmp = pf[:, PFO_TMP:PFO_TMP + 1024]
        qw = [pf[:, PFO_QW:PFO_QW + 688], pf[:, PFO_QW + 688:PFO_QW + 1376]]
        qx = pf[:, PFO_QW:PFO_QW + 1024]
        x1pb = [pf[:, PFO_X1A:PFO_X1A + 512], pf[:, PFO_X1B:PFO_X1B + 512]]
        bcz_s = pf[:, PFO_X1A:PFO_X1A + 512]

        r_sum = rows[:, 0:1024]
        r_var = rows[:, 1024:2048]
        r_m = rows[:, 2048:3072]
        r_istd = rows[:, 3072:4096]
        r_misc = rows[:, 4096:4592]
        r_svec = rows[:, 4592:4600]
        r_sld = rows[:, 4600:4608]

        def T(f): block.tensor(f)
        def V(f): block.vector(f)
        def Sc(f): block.scalar(f)
        def Sy(f): block.sync(f)
        def G(f): block.gpsimd(f)

        dG_wqkv = {}

        @G
        def _(g):
            g.memset(ones_fc[:, :], 1.0)
            g.memset(ones_fr[:, :], 1.0)
            g.memset(ones_bc[:, :], 1.0)
            g.memset(r_svec[:, :], 0.0)
            i = g.dma_start(out=mask_s[:, :, :], in_=masks.rearrange("j p t -> p j t"))
            C.inc(i, "dG", 16)
            i = g.dma_start(out=id_s[:, :], in_=ident[:, :])
            C.inc(i, "dG", 16)
            for m in range(3):
                i = g.dma_start(out=wqkv[:, m, :, :, :], in_=wqkv_in[:, m, 0:2, :, :])
                C.inc(i, "dG", 16)
            dG_wqkv[0] = C.val("dG")
        dG_init = C.val("dG")

        sE_sq = []
        tE_x = []

        for c in range(DC):
            def s_p0(s, c=c):
                C.wait(s, "dS" if c % 2 == 0 else "dS2", 16 * (c // 2 + 1))
                if c >= 1:
                    C.wait(s, "tE", tE_x[c - 1])
                i = s.activation(fscr[:, :], (xa if c % 2 == 0 else xb), AF.Square)
                sE_sq.append(C.inc(i, "sE", 1))
            Sc(s_p0)

            def t_p0(t, c=c):
                if c == 0:
                    C.wait(t, "dG", dG_init)
                xbuf = xa if c % 2 == 0 else xb
                C.wait(t, "dS" if c % 2 == 0 else "dS2", 16 * (c // 2 + 1))
                t.matmul(PS[4][0:1, :], ones_fc[:, :], xbuf[:, 0:512], start=(c == 0), stop=(c == DC - 1))
                t.matmul(PS[5][0:1, :], ones_fc[:, :], xbuf[:, 512:1024], start=(c == 0), stop=(c == DC - 1))
                C.wait(t, "sE", sE_sq[c])
                t.matmul(PS[0][0:1, :], ones_fc[:, :], fscr[:, 0:512], start=(c == 0), stop=(c == DC - 1))
                i = t.matmul(PS[1][0:1, :], ones_fc[:, :], fscr[:, 512:1024], start=(c == 0), stop=(c == DC - 1))
                tE_x.append(C.inc(i, "tE", 1))
            T(t_p0)

        @Sy
        def _(sy):
            for c in range(DC):
                if c >= 2:
                    C.wait(sy, "tE", tE_x[c - 2])
                i = sy.dma_start(out=(xa if c % 2 == 0 else xb), in_=xT[_ts(c, 128), :])
                C.inc(i, "dS" if c % 2 == 0 else "dS2", 16)

        def emit_rowchain(sum_srcs, sq_srcs, width):
            @V
            def _(v):
                C.wait(v, "tE", C.val("tE"))
                i = None
                for j, (bk, sl) in enumerate(sum_srcs):
                    i = v.tensor_copy(r_sum[:, _ts(j, 512)], bk[0:1, sl])
                for j, (bk, sl) in enumerate(sq_srcs):
                    i = v.tensor_copy(r_var[:, _ts(j, 512)], bk[0:1, sl])
                C.inc(i, "vE", 1)

            @Sc
            def _(s):
                C.wait(s, "vE", C.val("vE"))
                s.mul(r_m[:, 0:width], r_sum[:, 0:width], 1.0 / D)
                i = s.mul(r_istd[:, 0:width], r_var[:, 0:width], 1.0 / D)
                C.inc(i, "sE", 1)

            @V
            def _(v):
                C.wait(v, "sE", C.val("sE"))
                v.tensor_tensor(r_sum[:, 0:width], r_m[:, 0:width], r_m[:, 0:width], ALU.mult)
                v.tensor_tensor(r_var[:, 0:width], r_istd[:, 0:width], r_sum[:, 0:width], ALU.subtract)
                i = v.tensor_scalar_add(r_istd[:, 0:width], r_var[:, 0:width], LN_EPS)
                C.inc(i, "vE", 1)

            @Sc
            def _(s):
                C.wait(s, "vE", C.val("vE"))
                i = s.activation(r_sum[:, 0:width], r_istd[:, 0:width], AF.Ln)
                C.inc(i, "sE", 1)
                C.wait(s, "sE", C.val("sE"))
                i = s.activation(r_istd[:, 0:width], r_sum[:, 0:width], AF.Exp, scale=-0.5)
                C.inc(i, "sE", 1)

        emit_rowchain([(PS[4], slice(0, 512)), (PS[5], slice(0, 512))],
                      [(PS[0], slice(0, 512)), (PS[1], slice(0, 512))], 1024)

        def emit_bcast_mi():
            @T
            def _(t):
                C.wait(t, "sE", C.val("sE"))
                t.matmul(PS[0][:, :], ones_fr[:, :], r_m[0:1, 0:512], start=True, stop=True)
                t.matmul(PS[1][:, :], ones_fr[:, :], r_m[0:1, 512:1024], start=True, stop=True)
                t.matmul(PS[2][:, :], ones_fr[:, :], r_istd[0:1, 0:512], start=True, stop=True)
                i = t.matmul(PS[3][:, :], ones_fr[:, :], r_istd[0:1, 512:1024], start=True, stop=True)
                C.inc(i, "tE", 1)

        emit_bcast_mi()
        tE_ln1bc = C.val("tE")

        vE_app = []

        @V
        def _(v):
            C.wait(v, "tE", tE_ln1bc)
            for c in range(DC):
                xbuf = xa if c % 2 == 0 else xb
                C.wait(v, "dS" if c % 2 == 0 else "dS2", 16 * (8 + c // 2 + 1))
                i = None
                for hf in range(2):
                    v.tensor_tensor(ftmp[:, _ts(hf, 512)], xbuf[:, _ts(hf, 512)], PS[hf][:, :], ALU.subtract)
                    i = v.tensor_tensor(lnxown[:, c, _ts(hf, 512)], ftmp[:, _ts(hf, 512)], PS[2 + hf][:, :], ALU.mult)
                vE_app.append(C.inc(i, "vE", 1))

        @Sy
        def _(sy):
            for c in range(DC):
                if c >= 2:
                    C.wait(sy, "vE", vE_app[c - 2])
                else:
                    C.wait(sy, "tE", tE_x[DC - 2 + c])
                i = sy.dma_start(out=(xa if c % 2 == 0 else xb), in_=xT[_ts(c, 128), :])
                C.inc(i, "dS" if c % 2 == 0 else "dS2", 16)

        @G
        def _(g):
            C.wait(g, "vE", vE_app[-1])
            i = g.dma_start(out=lnx_in.rearrange("(c p) t -> p c t", p=128), in_=lnxown[:, :, :])
            C.inc(i, "dG", 16)
            C.wait(g, "dG", C.val("dG"))
            i = g.collective_compute("AllGather", ALU.bypass, replica_groups=grp4,
                                     ins=[lnx_in.ap()], outs=[lnx_out.ap()])
            C.inc(i, "cC", 1)
        cC_lnx = C.val("cC")

        qchunks = []
        for mi, wsrc in ((0, w1T), (1, w3T), (2, w2T)):
            if mi < 2:
                for r in range(2):
                    for cc in range(8):
                        qchunks.append((wsrc, r * 128, 128, cc * 688, 688, mi))
            else:
                for r in range(6):
                    nr = 128 if r < 5 else HID_SH - 640
                    for (c0, ncol) in ((0, 688), (688, 688), (1376, 672)):
                        qchunks.append((wsrc, r * 128, nr, c0, ncol, mi))
        NQCH = len(qchunks)
        dQ_q1p = [C.val("dQ"), C.val("dQ2")]

        QEMIT = []
        QA_END = [0]
        vE_qred = []
        qacc_par = {0: 0, 1: 0, 2: 0}
        _first = {0: True, 1: True, 2: True}
        _par = {0: 0, 1: 0, 2: 0}
        sE_qs = {}
        vE_qt = {}
        cC_t = {}
        dG23 = {}

        for j, ch in enumerate(qchunks):
            def q_ld1(sc, j=j, ch=ch):
                wsrc, r0, nr, c0, ncol, mi = ch
                if j >= 2:
                    C.wait(sc, "vE", vE_qred[j - 2])
                i = sc.dma_start(out=qw[j % 2][0:nr, 0:ncol], in_=wsrc[r0:r0 + nr, c0:c0 + ncol])
                C.inc(i, "dQ" if j % 2 == 0 else "dQ2", 16)
            QEMIT.append((Sc, q_ld1))

            def q_red(v, j=j, ch=ch):
                wsrc, r0, nr, c0, ncol, mi = ch
                C.wait(v, "dQ" if j % 2 == 0 else "dQ2", dQ_q1p[j % 2] + 16 * (j // 2 + 1))
                buf = qw[j % 2]
                if _first[mi]:
                    i = v.reduce_sum(st1[:, mi * 2:mi * 2 + 1], buf[0:nr, 0:ncol],
                                     axis=mybir.AxisListType.X, apply_absolute_value=True)
                    _first[mi] = False
                else:
                    v.reduce_sum(st1[0:nr, 6:7], buf[0:nr, 0:ncol],
                                 axis=mybir.AxisListType.X, apply_absolute_value=True)
                    o = _par[mi]
                    _par[mi] ^= 1
                    n2 = _par[mi]
                    v.tensor_copy(st1[:, mi * 2 + n2:mi * 2 + n2 + 1], st1[:, mi * 2 + o:mi * 2 + o + 1])
                    i = v.tensor_tensor(st1[0:nr, mi * 2 + n2:mi * 2 + n2 + 1],
                                        st1[0:nr, mi * 2 + o:mi * 2 + o + 1],
                                        st1[0:nr, 6:7], ALU.add)
                vE_qred.append(C.inc(i, "vE", 1))
                qacc_par[mi] = _par[mi]
            QEMIT.append((V, q_red))

        def q_sum(t):
            C.wait(t, "vE", vE_qred[-1])
            i = None
            for mi in range(3):
                acc = st1[:, mi * 2 + qacc_par[mi]:mi * 2 + qacc_par[mi] + 1]
                i = t.matmul(PS[4][0:1, mi:mi + 1], ones_fc[:, :], acc, start=True, stop=True)
            C.inc(i, "tE", 1)
        QEMIT.append((T, q_sum))

        def q_cp(v):
            C.wait(v, "tE", C.val("tE"))
            i = v.tensor_copy(r_svec[:, 0:3], PS[4][0:1, 0:3])
            C.inc(i, "vE", 1)
        QEMIT.append((V, q_cp))

        def q_ar(g):
            C.wait(g, "vE", C.val("vE"))
            i = g.dma_start(out=ar_in[:, :], in_=r_svec[:, :])
            C.inc(i, "dG", 16)
            C.wait(g, "dG", C.val("dG"))
            i = g.collective_compute("AllReduce", ALU.add, replica_groups=grp8,
                                     ins=[ar_in.ap()], outs=[ar_out.ap()])
            C.inc(i, "cC", 1)
            C.wait(g, "cC", C.val("cC"))
            i = g.dma_start(out=r_sld[:, :], in_=ar_out[:, :])
            C.inc(i, "dG", 16)
        QEMIT.append((G, q_ar))

        def q_smul(sc):
            C.wait(sc, "dG", C.val("dG"))
            i = sc.mul(r_misc[:, 0:3], r_sld[:, 0:3], 0.5 / W_N)
            C.inc(i, "sE", 1)
        QEMIT.append((Sc, q_smul))

        def q_bc(t):
            C.wait(t, "sE", C.val("sE"))
            i = t.matmul(PS[5][:, 0:3], ones_fr[:, :], r_misc[0:1, 0:3], start=True, stop=True)
            C.inc(i, "tE", 1)
        QEMIT.append((T, q_bc))

        vE_sh_h = [0]

        def q_sh(v):
            C.wait(v, "tE", C.val("tE"))
            v.tensor_copy(st1[:, 8:11], PS[5][:, 0:3])
            i = v.tensor_scalar_mul(st1[:, 11:14], PS[5][:, 0:3], -1.0)
            vE_sh_h[0] = C.inc(i, "vE", 1)
        QEMIT.append((V, q_sh))
        QA_END[0] = len(QEMIT)

        dQ_q2p = [None, None]

        def q_p2init(sc):
            dQ_q2p[0] = C.val("dQ")
            dQ_q2p[1] = C.val("dQ2")
            C.wait(sc, "vE", vE_sh_h[0])
            for j in range(2):
                wsrc, r0, nr, c0, ncol, mi = qchunks[j]
                i = sc.dma_start(out=qw[j % 2][0:nr, 0:ncol], in_=wsrc[r0:r0 + nr, c0:c0 + ncol])
                C.inc(i, "dQ" if j % 2 == 0 else "dQ2", 16)
        QEMIT.append((Sc, q_p2init))

        dsts = {0: t1_in, 1: t3_in, 2: t2_in}
        lastj = {0: 15, 1: 31, 2: NQCH - 1}

        _written = set()

        def emit_write(sc, j):
            if j in _written:
                return
            _written.add(j)
            wsrc, r0, nr, c0, ncol, mi = qchunks[j]
            C.wait(sc, "vE", vE_qt[j])
            sem = "dG2" if j % 2 == 0 else "dG3"
            i = sc.dma_start(out=dsts[mi][r0:r0 + nr, c0:c0 + ncol],
                             in_=qts_s[0:nr, j % 2, 0:ncol])
            dG23[j] = (sem, C.inc(i, sem, 16))

        for j, ch in enumerate(qchunks):
            def q_sgn(sc, j=j, ch=ch):
                wsrc, r0, nr, c0, ncol, mi = ch
                if j >= 1:
                    emit_write(sc, j - 1)
                C.wait(sc, "dQ" if j % 2 == 0 else "dQ2", dQ_q2p[j % 2] + 16 * (j // 2 + 1))
                if j >= 1:
                    C.wait(sc, "vE", vE_qt[j - 1])
                sc.activation(qsg_s[0:nr, 0, 0:ncol], qw[j % 2][0:nr, 0:ncol], AF.Sign,
                              bias=st1[0:nr, 11 + mi:12 + mi], scale=1.0)
                i = sc.activation(qsg_s[0:nr, 1, 0:ncol], qw[j % 2][0:nr, 0:ncol], AF.Sign,
                                  bias=st1[0:nr, 8 + mi:9 + mi], scale=1.0)
                sE_qs[j] = C.inc(i, "sE", 1)
                if j + 2 < NQCH:
                    C.wait(sc, "sE", sE_qs[j])
                    w2s, r02, nr2, c02, nc2, _ = qchunks[j + 2]
                    i = sc.dma_start(out=qw[j % 2][0:nr2, 0:nc2], in_=w2s[r02:r02 + nr2, c02:c02 + nc2])
                    C.inc(i, "dQ" if j % 2 == 0 else "dQ2", 16)
            QEMIT.append((Sc, q_sgn))

            def q_tt(v, j=j, ch=ch):
                wsrc, r0, nr, c0, ncol, mi = ch
                C.wait(v, "sE", sE_qs[j])
                if j >= 2:
                    C.wait(v, dG23[j - 2][0], dG23[j - 2][1])
                v.tensor_tensor(qsg_s[0:nr, 2, 0:ncol], qsg_s[0:nr, 0, 0:ncol],
                                qsg_s[0:nr, 1, 0:ncol], ALU.add)
                i = v.tensor_scalar_mul(qts_s[0:nr, j % 2, 0:ncol], qsg_s[0:nr, 2, 0:ncol],
                                        st1[0:nr, 8 + mi:9 + mi])
                vE_qt[j] = C.inc(i, "vE", 1)
            QEMIT.append((V, q_tt))

            if j == lastj[0] or j == lastj[1] or j == lastj[2]:
                def q_ag(g, j=j):
                    mi = {lastj[0]: 0, lastj[1]: 1, lastj[2]: 2}[j]
                    C.wait(g, "dG2", C.val("dG2"))
                    C.wait(g, "dG3", C.val("dG3"))
                    src, dst = {0: (t1_in, t1_out), 1: (t3_in, t3_out), 2: (t2_in, t2_out)}[mi]
                    i = g.collective_compute("AllGather", ALU.bypass, replica_groups=grp8,
                                             ins=[src.ap()], outs=[dst.ap()])
                    cC_t[mi] = C.inc(i, "cC", 1)

                def q_wlast(sc, j=j):
                    emit_write(sc, j)
                QEMIT.append((Sc, q_wlast))
                QEMIT.append((G, q_ag))

        _emitted_last = set()

        def pop_quant(k):
            for _ in range(k):
                if not QEMIT:
                    return
                reg, fn = QEMIT.pop(0)
                QA_END[0] = max(0, QA_END[0] - 1)
                reg(fn)

        def drain_quant_psum_part():
            while QA_END[0] > 0:
                pop_quant(1)

        evac_done = {}
        qkv_done = {}
        dS_chunk = {}
        last_exp_use = {0: 0, 1: 0}
        last_pv = {0: 0, 1: 0}
        last_z_use = {0: 0, 1: 0}
        last_den = [0]
        last_bc = [0]
        vt_use = {0: 0, 1: 0}
        z_done = [0]
        GROUPS = [(0, 0), (1, 0), (2, 0), (0, 1), (1, 1), (2, 1)]

        for pk in range(2):
            for j in range(8):
                jg = pk * 8 + j

                def sy_mini(sy, j=j, jg=jg):
                    if jg == 0:
                        C.wait(sy, "cC", cC_lnx)
                    if jg >= 2:
                        C.wait(sy, "tE", qkv_done[jg - 2])
                    blkr = (j // 2) * D
                    c0 = (j % 2) * 512
                    src = lnx_out[blkr:blkr + D, c0:c0 + 512].rearrange("(c p) t -> p c t", p=128)
                    sem = "dS" if jg % 2 == 0 else "dS2"
                    for q in range(4):
                        i = sy.dma_start(out=lnch[:, j % 2, _ts(q, 4), :], in_=src[:, _ts(q, 4), :])
                        C.inc(i, sem, 16)
                    dS_chunk[jg] = (sem, C.val(sem))
                Sy(sy_mini)

                for g_i, (m, h) in enumerate(GROUPS):
                    Gg = jg * 6 + g_i

                    def t_mini(t, m=m, h=h, j=j, jg=jg, Gg=Gg, pk=pk):
                        bank = PS[Gg % 2]
                        C.wait(t, dS_chunk[jg][0], dS_chunk[jg][1])
                        C.wait(t, "dG", dG_wqkv[pk])
                        if Gg >= 2 and (Gg - 2) in evac_done:
                            C.wait(t, "vE", evac_done[Gg - 2])
                        i = None
                        for cc in range(DC):
                            i = t.matmul(bank[:, :], wqkv[:, m, h, cc, :], lnch[:, j % 2, cc, :],
                                         start=(cc == 0), stop=(cc == DC - 1))
                        qkv_done[jg] = C.inc(i, "tE", 1)
                    T(t_mini)

                    if m < 2:
                        def v_mini(v, m=m, h=h, j=j, Gg=Gg):
                            bank = PS[Gg % 2]
                            C.wait(v, "tE", C.val("tE"))
                            i = v.tensor_copy((qt if m == 0 else kt)[:, h, _ts(j, 512)], bank[:, :])
                            evac_done[Gg] = C.inc(i, "vE", 1)
                        V(v_mini)
                        pop_quant(2)
                    else:
                        def v_mini1(v, Gg=Gg):
                            bank = PS[Gg % 2]
                            C.wait(v, "tE", C.val("tE"))
                            i = v.tensor_copy(vtmp_s[:, :], bank[:, :])
                            C.inc(i, "vE", 1)
                        V(v_mini1)

                        vpar = jg % 2

                        def t_mini2(t, vpar=vpar, jg=jg):
                            C.wait(t, "vE", C.val("vE"))
                            if vt_use[vpar]:
                                C.wait(t, "vE", vt_use[vpar])
                            i = None
                            for q in range(4):
                                i = t.transpose(PT[vpar][:, _ts(q, 128)], vtmp_s[:, _ts(q, 128)], id_s[:, :])
                            qkv_done[jg] = C.inc(i, "tE", 1)
                        T(t_mini2)

                        def v_mini2(v, h=h, j=j, Gg=Gg, vpar=vpar):
                            C.wait(v, "tE", C.val("tE"))
                            dst = vt[:, h, 4 * j:4 * j + 4, :].rearrange("p k e -> p (k e)")
                            i = v.tensor_copy(dst, PT[vpar][:, 0:512])
                            vt_use[vpar] = C.inc(i, "vE", 1)
                            evac_done[Gg] = C.val("vE")
                        V(v_mini2)
                        pop_quant(2)

            if pk == 0:
                def g_wq1(g):
                    C.wait(g, "tE", qkv_done[7])
                    for m in range(3):
                        i = g.dma_start(out=wqkv[:, m, :, :, :], in_=wqkv_in[:, m, 2:4, :, :])
                        C.inc(i, "dG", 16)
                    dG_wqkv[1] = C.val("dG")
                G(g_wq1)

            drain_quant_psum_part()
            vE_pass_evacs = evac_done[(pk * 8 + 7) * 6 + 5]

            for hl in range(2):
                hglob = 2 * pk + hl
                for qc in range(NQ):
                    nk = 4 * qc + 4
                    tS = {}

                    def t_pre(t, hl=hl, qc=qc, nk=nk, tS=tS, vE_pass_evacs=vE_pass_evacs):
                        C.wait(t, "vE", vE_pass_evacs)
                        if last_den[0]:
                            C.wait(t, "vE", last_den[0])
                        if last_z_use[qc % 2]:
                            C.wait(t, "vE", last_z_use[qc % 2])
                        for kc in range(min(2, nk)):
                            if last_exp_use[kc % 2]:
                                C.wait(t, "sE", last_exp_use[kc % 2])
                            i = t.matmul(PS[kc % 2][:, :], kt[:, hl, _ts(kc, 128)],
                                         qt[:, hl, _ts(qc, 512)], start=True, stop=True)
                            tS[kc] = C.inc(i, "tE", 1)
                    T(t_pre)

                    for kc in range(nk):
                        diag = kc >= nk - 4
                        eK = [0]
                        mK = [0]

                        def s_exp(s, kc=kc, tS=tS, eK=eK):
                            C.wait(s, "tE", tS[kc])
                            if last_pv[kc % 2]:
                                C.wait(s, "tE", last_pv[kc % 2])
                            i = s.activation(exp_s[:, kc % 2, :], PS[kc % 2][:, :], AF.Exp, scale=QSCL)
                            eK[0] = C.inc(i, "sE", 1)
                            last_exp_use[kc % 2] = eK[0]
                        Sc(s_exp)

                        if diag:
                            def v_mask(v, kc=kc, nk=nk, eK=eK, mK=mK):
                                C.wait(v, "sE", eK[0])
                                if last_pv[kc % 2]:
                                    C.wait(v, "tE", last_pv[kc % 2])
                                jj = kc - (nk - 4)
                                i = v.tensor_tensor(expm_s[:, kc % 2, :], exp_s[:, kc % 2, :],
                                                    mask_s[:, jj, :], ALU.mult)
                                mK[0] = C.inc(i, "vE", 1)
                            V(v_mask)

                        def t_pv(t, hl=hl, qc=qc, kc=kc, nk=nk, diag=diag, eK=eK, mK=mK, tS=tS):
                            if diag:
                                C.wait(t, "vE", mK[0])
                                src = expm_s[:, kc % 2, :]
                            else:
                                C.wait(t, "sE", eK[0])
                                src = exp_s[:, kc % 2, :]
                            i = t.matmul(PS[2 + qc % 2][:, :], vt[:, hl, kc, :], src,
                                         start=(kc == 0), stop=(kc == nk - 1))
                            C.inc(i, "tE", 1)
                            i = t.matmul(PS[4][0:1, :], ones_bc[:, :], src,
                                         start=(kc == 0), stop=(kc == nk - 1))
                            last_pv[kc % 2] = C.inc(i, "tE", 1)
                            if kc + 2 < nk:
                                C.wait(t, "sE", last_exp_use[kc % 2])
                                i = t.matmul(PS[kc % 2][:, :], kt[:, hl, _ts(kc + 2, 128)],
                                             qt[:, hl, _ts(qc, 512)], start=True, stop=True)
                                tS[kc + 2] = C.inc(i, "tE", 1)
                        T(t_pv)

                    def v_recip(v):
                        C.wait(v, "tE", C.val("tE"))
                        i = v.reciprocal(inv_s[:, :], PS[4][0:1, :])
                        last_den[0] = C.inc(i, "vE", 1)
                    V(v_recip)

                    def t_bc(t):
                        C.wait(t, "vE", last_den[0])
                        if last_bc[0]:
                            C.wait(t, "vE", last_bc[0])
                        i = t.matmul(PS[5][:, :], ones_fr[:, :], inv_s[0:1, :], start=True, stop=True)
                        C.inc(i, "tE", 1)
                    T(t_bc)

                    def v_zmul(v, hglob=hglob, qc=qc):
                        C.wait(v, "tE", C.val("tE"))
                        v.tensor_copy(bcz_s[:, :], PS[5][:, :])
                        i = v.tensor_tensor(z_v[:, hglob, _ts(qc, 512)], PS[2 + qc % 2][:, :],
                                            bcz_s[:, :], ALU.mult)
                        last_z_use[qc % 2] = C.inc(i, "vE", 1)
                        last_bc[0] = C.val("vE")
                        z_done[0] = C.val("vE")
                    V(v_zmul)
                    pop_quant(3)

        vE_zall = z_done[0]

        pop_quant(10 ** 9)

        dG_wo = [0]

        @G
        def _(g):
            C.wait(g, "tE", qkv_done[15])
            i = g.dma_start(out=wo_v[:, :, :, :], in_=wo_in[:, :, :, :])
            dG_wo[0] = C.inc(i, "dG", 16)

        p2_evac = {}
        dS_rs = [0]

        for dc in range(DC):
            for grp, t5s in ((0, (0, 1, 2, 3, 4, 5)), (1, (6, 7))):
                def t_p2(t, dc=dc, grp=grp, t5s=t5s):
                    C.wait(t, "dG", dG_wo[0])
                    C.wait(t, "vE", vE_zall)
                    prev = (dc, 0) if grp == 1 else (dc - 1, 1)
                    if prev in p2_evac:
                        C.wait(t, "vE", p2_evac[prev])
                    i = None
                    for he in range(4):
                        for bi, t5 in enumerate(t5s):
                            i = t.matmul(PS[bi][:, :], wo_v[:, he, dc, :],
                                         z_v[:, he, _ts(t5, 512)], start=(he == 0), stop=(he == 3))
                    C.inc(i, "tE", 1)
                T(t_p2)

                def v_p2(v, dc=dc, grp=grp, t5s=t5s):
                    C.wait(v, "tE", C.val("tE"))
                    if grp == 0 and dc >= 1:
                        C.wait(v, "dW", dS_rs[0])
                    i = None
                    for bi, t5 in enumerate(t5s):
                        i = v.tensor_copy(p2stg[:, t5, :], PS[bi][:, :])
                    p2_evac[(dc, grp)] = C.inc(i, "vE", 1)
                V(v_p2)

            def sy_p2(sy, dc=dc):
                C.wait(sy, "vE", p2_evac[(dc, 1)])
                rsv = rs_in.rearrange("s (c p) (h t) -> p c s h t", p=128, t=512)
                i = sy.dma_start(out=rsv[:, dc, :, :, :],
                                 in_=p2stg.rearrange("p (s h) t -> p s h t", s=4))
                dS_rs[0] = C.inc(i, "dW", 16)
            Sy(sy_p2)

        @G
        def _(g):
            C.wait(g, "dW", dS_rs[0])
            i = g.collective_compute("ReduceScatter", ALU.add, replica_groups=grp4,
                                     ins=[rs_in.ap()], outs=[rs_out.ap()])
            C.inc(i, "cC", 1)
        cC_rs = C.val("cC")

        dS_at = {}

        @Sy
        def _(sy):
            C.wait(sy, "cC", cC_rs)
            for c in range(DC):
                i = sy.dma_start(out=attn_t[:, c, :], in_=rs_out[_ts(c, 128), :])
                C.inc(i, "dS", 16)
            for c in range(DC):
                dS_at[c] = C.val("dS")

        tE_asq = {}
        for c in range(DC):
            def s_asq(s, c=c):
                C.wait(s, "dS", dS_at[c])
                if c >= 1:
                    C.wait(s, "tE", tE_asq[c - 1])
                i = s.activation(sqb[:, :], attn_t[:, c, :], AF.Square)
                C.inc(i, "sE", 1)
            Sc(s_asq)

            def t_asq(t, c=c):
                C.wait(t, "dS", dS_at[c])
                t.matmul(PS[4][0:1, :], ones_bc[:, :], attn_t[:, c, 0:512], start=(c == 0), stop=(c == DC - 1))
                t.matmul(PS[5][0:1, :], ones_bc[:, :], attn_t[:, c, 512:1024], start=(c == 0), stop=(c == DC - 1))
                C.wait(t, "sE", C.val("sE"))
                t.matmul(PS[0][0:1, :], ones_bc[:, :], sqb[:, 0:512], start=(c == 0), stop=(c == DC - 1))
                i = t.matmul(PS[1][0:1, :], ones_bc[:, :], sqb[:, 512:1024], start=(c == 0), stop=(c == DC - 1))
                tE_asq[c] = C.inc(i, "tE", 1)
            T(t_asq)

        emit_rowchain([(PS[4], slice(0, 512)), (PS[5], slice(0, 512))],
                      [(PS[0], slice(0, 512)), (PS[1], slice(0, 512))], 1024)
        emit_bcast_mi()
        tE_abc = C.val("tE")


        x2ld = {}

        @Sc
        def _(s):
            C.wait(s, "vE", vE_app[-1])
            for c in range(2):
                sem = "dQ" if c % 2 == 0 else "dQ2"
                i = s.dma_start(out=(xa if c % 2 == 0 else xb), in_=xT[_ts(c, 128), :])
                x2ld[c] = (sem, C.inc(i, sem, 16))

        vE_x1 = {}
        dS_x1w = {}
        for c in range(DC):
            def v_x1(v, c=c):
                xbuf = xa if c % 2 == 0 else xb
                C.wait(v, "tE", tE_abc)
                C.wait(v, "dS", dS_at[c])
                C.wait(v, x2ld[c][0], x2ld[c][1])
                if c >= 1:
                    C.wait(v, "dW", dS_x1w[c - 1])
                i = None
                for hf in range(2):
                    v.tensor_tensor(ftmp[:, _ts(hf, 512)], attn_t[:, c, _ts(hf, 512)], PS[hf][:, :], ALU.subtract)
                    v.tensor_tensor(qx[:, _ts(hf, 512)], ftmp[:, _ts(hf, 512)], PS[2 + hf][:, :], ALU.mult)
                    i = v.tensor_tensor(fscr[:, _ts(hf, 512)], qx[:, _ts(hf, 512)],
                                        xbuf[:, _ts(hf, 512)], ALU.add)
                vE_x1[c] = C.inc(i, "vE", 1)
            V(v_x1)

            def sy_x1(sy, c=c):
                C.wait(sy, "vE", vE_x1[c])
                i = sy.dma_start(out=x1_scr[_ts(c, 128), :], in_=fscr[:, :])
                dS_x1w[c] = C.inc(i, "dW", 16)
            Sy(sy_x1)

            def s_x1(s, c=c):
                if c + 2 < DC:
                    C.wait(s, "vE", vE_x1[c])
                    sem = "dQ" if c % 2 == 0 else "dQ2"
                    i = s.dma_start(out=(xa if c % 2 == 0 else xb), in_=xT[_ts(c + 2, 128), :])
                    x2ld[c + 2] = (sem, C.inc(i, sem, 16))
            Sc(s_x1)

        stld = {}

        @Sc
        def _(s):
            C.wait(s, "vE", vE_x1[DC - 1])
            for c in range(2):
                C.wait(s, "dW", dS_x1w[c])
                sem = "dQ" if c % 2 == 0 else "dQ2"
                i = s.dma_start(out=(xa if c % 2 == 0 else xb), in_=x1_scr[_ts(c, 128), :])
                stld[c] = (sem, C.inc(i, sem, 16))

        sE_st = {}
        tE_st = {}
        for c in range(DC):
            def s_stA(s, c=c):
                C.wait(s, stld[c][0], stld[c][1])
                if c >= 1:
                    C.wait(s, "tE", tE_st[c - 1])
                i = s.activation(qx[:, :], (xa if c % 2 == 0 else xb), AF.Square)
                sE_st[c] = C.inc(i, "sE", 1)
            Sc(s_stA)

            def t_st(t, c=c):
                xbuf = xa if c % 2 == 0 else xb
                if c == 0:
                    C.wait(t, "vE", vE_x1[DC - 1])
                C.wait(t, stld[c][0], stld[c][1])
                t.matmul(PS[4][0:1, :], ones_fc[:, :], xbuf[:, 0:512], start=(c == 0), stop=(c == DC - 1))
                t.matmul(PS[5][0:1, :], ones_fc[:, :], xbuf[:, 512:1024], start=(c == 0), stop=(c == DC - 1))
                C.wait(t, "sE", sE_st[c])
                t.matmul(PS[0][0:1, :], ones_fc[:, :], qx[:, 0:512], start=(c == 0), stop=(c == DC - 1))
                i = t.matmul(PS[1][0:1, :], ones_fc[:, :], qx[:, 512:1024], start=(c == 0), stop=(c == DC - 1))
                tE_st[c] = C.inc(i, "tE", 1)
            T(t_st)

            def s_stB(s, c=c):
                if c + 2 < DC:
                    C.wait(s, "tE", tE_st[c])
                    C.wait(s, "dW", dS_x1w[c + 2])
                    sem = "dQ" if c % 2 == 0 else "dQ2"
                    i = s.dma_start(out=(xa if c % 2 == 0 else xb), in_=x1_scr[_ts(c + 2, 128), :])
                    stld[c + 2] = (sem, C.inc(i, sem, 16))
            Sc(s_stB)

        emit_rowchain([(PS[4], slice(0, 512)), (PS[5], slice(0, 512))],
                      [(PS[0], slice(0, 512)), (PS[1], slice(0, 512))], 1024)
        emit_bcast_mi()
        tE_l2bc = C.val("tE")

        l2ld = {}
        tE_pb = C.val("tE")

        @Sc
        def _(s):
            C.wait(s, "tE", tE_pb)
            for c in range(2):
                sem = "dQ" if c % 2 == 0 else "dQ2"
                i = s.dma_start(out=(xa if c % 2 == 0 else xb), in_=x1_scr[_ts(c, 128), :])
                l2ld[c] = (sem, C.inc(i, sem, 16))

        vE_l2 = {}
        for c in range(DC):
            def v_l2(v, c=c):
                xbuf = xa if c % 2 == 0 else xb
                C.wait(v, "tE", tE_l2bc)
                C.wait(v, l2ld[c][0], l2ld[c][1])
                i = None
                for hf in range(2):
                    v.tensor_tensor(fscr[:, _ts(hf, 512)], xbuf[:, _ts(hf, 512)], PS[hf][:, :], ALU.subtract)
                    i = v.tensor_tensor(ln2xt[:, c, _ts(hf, 512)], fscr[:, _ts(hf, 512)], PS[2 + hf][:, :], ALU.mult)
                vE_l2[c] = C.inc(i, "vE", 1)
            V(v_l2)

            def s_l2(s, c=c):
                if c + 2 < DC:
                    C.wait(s, "vE", vE_l2[c])
                    sem = "dQ" if c % 2 == 0 else "dQ2"
                    i = s.dma_start(out=(xa if c % 2 == 0 else xb), in_=x1_scr[_ts(c + 2, 128), :])
                    l2ld[c + 2] = (sem, C.inc(i, sem, 16))
            Sc(s_l2)
        vE_ln2 = C.val("vE")

        HCP = [(2 * i, 2) for i in range(21)] + [(42, 1)]
        out_store = [0]
        vE_out_prev = [0]
        gt_tap_gate = [0]
        gt_done = {}
        sE_sil = {}
        h_mm = {}
        w2_done = {}
        fstg_evac = {}

        def megabarrier():
            for eng, name in ((T, "tensor"),):
                def bar_t(t):
                    C.wait(t, "vE", C.val("vE"))
                    C.wait(t, "sE", C.val("sE"))
                    C.wait(t, "dS", C.val("dS"))
                    C.wait(t, "dQ", C.val("dQ"))
                    C.wait(t, "dG", C.val("dG"))
                    i = t.matmul(PS[5][0:1, 0:1], ones_fc[0:1, :], ones_fc[0:1, 0:1], start=True, stop=True)
                    C.inc(i, "tE", 1)
                T(bar_t)
            def bar_v(v):
                C.wait(v, "tE", C.val("tE"))
                i = v.tensor_copy(st1[:, 15:16], st1[:, 14:15])
                C.inc(i, "vE", 1)
            V(bar_v)
            def bar_s(s):
                C.wait(s, "vE", C.val("vE"))
                i = s.mul(st1[:, 15:16], st1[:, 15:16], 1.0)
                C.inc(i, "sE", 1)
            Sc(bar_s)
            def bar_sy(sy):
                C.wait(sy, "sE", C.val("sE"))
            Sy(bar_sy)
            def bar_g(g):
                C.wait(g, "sE", C.val("sE"))
            G(bar_g)

        MEGABAR = False
        for th in range(2):
            tsl = _ts(th, 512)
            if MEGABAR:
                megabarrier()
            dS_fb = {}
            for pi, (hc0, nn) in enumerate(HCP):
                def sy_fb(sy, pi=pi, hc0=hc0, nn=nn, th=th):
                    C.wait(sy, "cC", cC_t[1])
                    if th == 0 and pi < 2:
                        C.wait(sy, "vE", vE_ln2)
                    if pi >= 2:
                        C.wait(sy, "tE", h_mm[(th, pi - 2, 1)])
                    elif th == 1 and pi < 2:
                        C.wait(sy, "tE", h_mm[(0, len(HCP) - 2 + pi, 1)])
                        C.wait(sy, "tE", w2_done[(0, 3)])
                    sem = "dS" if pi % 2 == 0 else "dS2"
                    for m, tsrc in ((0, t1_out), (1, t3_out)):
                        src = tsrc[:, hc0 * 128:(hc0 + nn) * 128].rearrange("(c p) h -> p c h", p=128)
                        for q in range(4):
                            i = sy.dma_start(out=fbuf[:, m * 2 + pi % 2, _ts(q, 4), 0:nn * 128],
                                             in_=src[:, _ts(q, 4), :])
                            C.inc(i, sem, 16)
                    dS_fb[pi] = (sem, C.val(sem))
                Sy(sy_fb)

                for li in range(nn):
                    hc = hc0 + li

                    def t_h(t, pi=pi, li=li, hc=hc, th=th):
                        C.wait(t, dS_fb[pi][0], dS_fb[pi][1])
                        if th == 1 and hc < 2 and gt_tap_gate[0]:
                            C.wait(t, "dG", gt_tap_gate[0])
                        if th == 0 and hc < 2:
                            C.wait(t, "vE", vE_ln2)
                        if th == 1 and hc < 2 and (0, 3) in fstg_evac:
                            C.wait(t, "vE", fstg_evac[(0, 3)])
                        if hc >= 2:
                            C.wait(t, "vE", gt_done[(th, hc - 2)])
                        elif th == 1:
                            C.wait(t, "vE", gt_done[(0, NHC - 2 + hc)])
                        i1 = None
                        i3 = None
                        for cc in range(DC):
                            i1 = t.matmul(PS[hc % 2][:, :], fbuf[:, 0 + pi % 2, cc, _ts(li, 128)],
                                          ln2xt[:, cc, _ts(th, 512)], start=(cc == 0), stop=(cc == DC - 1))
                        h_mm[(th, pi, 0)] = C.inc(i1, "tE", 1)
                        for cc in range(DC):
                            i3 = t.matmul(PS[2 + hc % 2][:, :], fbuf[:, 2 + pi % 2, cc, _ts(li, 128)],
                                          ln2xt[:, cc, _ts(th, 512)], start=(cc == 0), stop=(cc == DC - 1))
                        h_mm[(th, pi, 1)] = C.inc(i3, "tE", 1)
                    T(t_h)

                    def s_sil(s, pi=pi, hc=hc, th=th):
                        C.wait(s, "tE", h_mm[(th, pi, 1)])
                        if hc >= 1:
                            C.wait(s, "vE", gt_done[(th, hc - 1)])
                        elif th == 1:
                            C.wait(s, "vE", gt_done[(0, NHC - 1)])
                        i = s.activation(sil_s[:, :], PS[hc % 2][:, :], AF.Silu)
                        sE_sil[(th, hc)] = C.inc(i, "sE", 1)
                    Sc(s_sil)

                    def v_g(v, hc=hc, th=th):
                        C.wait(v, "sE", sE_sil[(th, hc)])
                        C.wait(v, "tE", C.val("tE"))
                        i = v.tensor_tensor(gt_v[:, hc, :], sil_s[:, :], PS[2 + hc % 2][:, :], ALU.mult)
                        gt_done[(th, hc)] = C.inc(i, "vE", 1)
                    V(v_g)

            if th == 0 and "gt" in taps:
                tpg = tapt("gt", [128, NHC, 512], BF16)

                @G
                def _(g):
                    C.wait(g, "vE", gt_done[(0, NHC - 1)])
                    i = g.dma_start(out=tpg[:, :, :], in_=gt_v[:, :, :])
                    gt_tap_gate[0] = C.inc(i, "dG", 16)
            if th == 0 and "fb" in taps:
                tpf = tapt("fb", [128, 4, 16, 256], BF16)

                @G
                def _(g):
                    C.wait(g, "vE", gt_done[(0, NHC - 1)])
                    i = g.dma_start(out=tpf[:, :, :, :], in_=fbuf[:, :, :, :])
                    gt_tap_gate[0] = C.inc(i, "dG", 16)

            if MEGABAR:
                megabarrier()
            dQ_sl = {}
            for dcg in range(4):
                def s_slab(s, dcg=dcg, th=th):
                    C.wait(s, "cC", cC_t[2])
                    if dcg == 0:
                        if th == 0:
                            C.wait(s, "vE", vE_ln2)
                        C.wait(s, "tE", h_mm[(th, len(HCP) - 1, 1)])
                    if (th, dcg - 1) in w2_done:
                        C.wait(s, "tE", w2_done[(th, dcg - 1)])
                    elif th == 1 and dcg == 0:
                        C.wait(s, "tE", w2_done[(0, 3)])
                    src = t2_out[:, _ts(dcg, 512)]
                    sem = "dQ" if dcg % 2 == 0 else "dQ2"
                    for q in range(5):
                        h0 = q * 9
                        h1 = min(NHC, h0 + 9)
                        if h0 >= NHC:
                            break
                        i = s.dma_start(out=slab[:, h0:h1, :],
                                        in_=src[h0 * 128:h1 * 128, :].rearrange("(h p) d -> p h d", p=128))
                        C.inc(i, sem, 16)
                    dQ_sl[dcg] = (sem, C.val(sem))
                Sc(s_slab)

                def t_w2(t, dcg=dcg, th=th):
                    C.wait(t, dQ_sl[dcg][0], dQ_sl[dcg][1])
                    C.wait(t, "vE", gt_done[(th, NHC - 1)])
                    if th == 1 and dcg == 0 and vE_out_prev[0]:
                        C.wait(t, "vE", vE_out_prev[0])
                    if (th, dcg - 1) in fstg_evac:
                        C.wait(t, "vE", fstg_evac[(th, dcg - 1)])
                    elif th == 1 and dcg == 0:
                        C.wait(t, "vE", fstg_evac[(0, 3)])
                    i = None
                    for hc in range(NHC):
                        for dcl in range(4):
                            i = t.matmul(PS[dcl][:, :], slab[:, hc, _ts(dcl, 128)],
                                         gt_v[:, hc, :], start=(hc == 0), stop=(hc == NHC - 1))
                    w2_done[(th, dcg)] = C.inc(i, "tE", 1)
                T(t_w2)

                def v_w2(v, dcg=dcg, th=th):
                    C.wait(v, "tE", w2_done[(th, dcg)])
                    i = None
                    for dcl in range(4):
                        i = v.tensor_copy(fstg[:, dcg * 4 + dcl, :], PS[dcl][:, :])
                    fstg_evac[(th, dcg)] = C.inc(i, "vE", 1)
                V(v_w2)

            tE_fsq = {}
            for c in range(DC):
                def s_fsq(s, c=c, th=th):
                    if th == 1 and c == 0 and vE_out_prev[0]:
                        C.wait(s, "vE", vE_out_prev[0])
                    C.wait(s, "vE", fstg_evac[(th, c // 4)])
                    if c >= 1:
                        C.wait(s, "tE", tE_fsq[c - 1])
                    i = s.activation(fsq_v[:, :], fstg[:, c, :], AF.Square)
                    C.inc(i, "sE", 1)
                Sc(s_fsq)

                def t_fsq(t, c=c, th=th):
                    if th == 1 and c == 0 and vE_out_prev[0]:
                        C.wait(t, "vE", vE_out_prev[0])
                    C.wait(t, "vE", fstg_evac[(th, c // 4)])
                    t.matmul(PS[4][0:1, :], ones_bc[:, :], fstg[:, c, :], start=(c == 0), stop=(c == DC - 1))
                    C.wait(t, "sE", C.val("sE"))
                    i = t.matmul(PS[5][0:1, :], ones_bc[:, :], fsq_v[:, :], start=(c == 0), stop=(c == DC - 1))
                    tE_fsq[c] = C.inc(i, "tE", 1)
                T(t_fsq)

            emit_rowchain([(PS[4], slice(0, 512))], [(PS[5], slice(0, 512))], 512)

            def t_fbc(t):
                C.wait(t, "sE", C.val("sE"))
                t.matmul(PS[4][:, :], ones_fr[:, :], r_m[0:1, 0:512], start=True, stop=True)
                i = t.matmul(PS[5][:, :], ones_fr[:, :], r_istd[0:1, 0:512], start=True, stop=True)
                C.inc(i, "tE", 1)
            T(t_fbc)
            tE_fbc = C.val("tE")

            x1ld = {}

            @Sc
            def _(s, th=th):
                if th == 1 and vE_out_prev[0]:
                    C.wait(s, "vE", vE_out_prev[0])
                C.wait(s, "tE", w2_done[(th, 3)])
                for c in range(2):
                    sem = "dQ" if c % 2 == 0 else "dQ2"
                    i = s.dma_start(out=x1pb[c % 2], in_=x1_scr[_ts(c, 128), _ts(th, 512)])
                    x1ld[c] = (sem, C.inc(i, sem, 16))

            vE_o = {}
            dS_o = {}
            for c in range(DC):
                def v_out(v, c=c, th=th):
                    C.wait(v, "tE", tE_fbc)
                    C.wait(v, x1ld[c][0], x1ld[c][1])
                    if c >= 1:
                        C.wait(v, "dW", dS_o[c - 1])
                    v.tensor_tensor(fscr[:, 0:512], fstg[:, c, :], PS[4][:, :], ALU.subtract)
                    v.tensor_tensor(ftmp[:, 0:512], fscr[:, 0:512], PS[5][:, :], ALU.mult)
                    i = v.tensor_tensor(fscr[:, 0:512], ftmp[:, 0:512], x1pb[c % 2], ALU.add)
                    vE_o[c] = C.inc(i, "vE", 1)
                V(v_out)

                def sy_out(sy, c=c, th=th):
                    C.wait(sy, "vE", vE_o[c])
                    i = sy.dma_start(out=outT[_ts(c, 128), _ts(th, 512)], in_=fscr[:, 0:512])
                    dS_o[c] = C.inc(i, "dW", 16)
                    out_store[0] = dS_o[c]
                Sy(sy_out)

                def s_xr(s, c=c, th=th):
                    if c + 2 < DC:
                        C.wait(s, "vE", vE_o[c])
                        sem = "dQ" if c % 2 == 0 else "dQ2"
                        i = s.dma_start(out=x1pb[c % 2], in_=x1_scr[_ts(c + 2, 128), _ts(th, 512)])
                        x1ld[c + 2] = (sem, C.inc(i, sem, 16))
                Sc(s_xr)

            vE_out_prev[0] = vE_o[DC - 1]

        if "lnx" in taps:
            tp = tapt("lnx", [4 * D, TC], BF16)

            @G
            def _(g):
                C.wait(g, "cC", cC_lnx)
                i = g.dma_start(out=tp[:, :], in_=lnx_out.ap())
                C.inc(i, "dG", 16)
        if "t1" in taps:
            tp = tapt("t1", [D, HIDP], BF16)

            @G
            def _(g):
                C.wait(g, "cC", cC_t[0])
                i = g.dma_start(out=tp[:, :], in_=t1_out.ap())
                C.inc(i, "dG", 16)
        if "t3" in taps:
            tp = tapt("t3", [D, HIDP], BF16)

            @G
            def _(g):
                C.wait(g, "cC", cC_t[1])
                i = g.dma_start(out=tp[:, :], in_=t3_out.ap())
                C.inc(i, "dG", 16)
        if "t3in" in taps:
            tp = tapt("t3in", [D_SH, HIDP], BF16)

            @G
            def _(g):
                C.wait(g, "cC", cC_t[1])
                i = g.dma_start(out=tp[:, :], in_=t3_in.ap())
                C.inc(i, "dG", 16)
        if "t2" in taps:
            tp = tapt("t2", [HIDP, D], BF16)

            @G
            def _(g):
                C.wait(g, "cC", cC_t[2])
                i = g.dma_start(out=tp[:, :], in_=t2_out.ap())
                C.inc(i, "dG", 16)
        if "ln2" in taps:
            tp = tapt("ln2", [128, 16, TC], BF16)

            @G
            def _(g):
                C.wait(g, "vE", vE_ln2)
                i = g.dma_start(out=tp[:, :, :], in_=ln2xt[:, :, :])
                C.inc(i, "dG", 16)
        if "z" in taps:
            tp = tapt("z", [128, HL, S], BF16)

            @G
            def _(g):
                C.wait(g, "vE", vE_zall)
                i = g.dma_start(out=tp[:, :, :], in_=z_v[:, :, :])
                C.inc(i, "dG", 16)
        if "rs" in taps:
            tp = tapt("rs", [D, TC], BF16)

            @G
            def _(g):
                C.wait(g, "cC", cC_rs)
                i = g.dma_start(out=tp[:, :], in_=rs_out.ap())
                C.inc(i, "dG", 16)
        if "x1" in taps:
            tp = tapt("x1", [D, TC], F32)

            @G
            def _(g):
                C.wait(g, "dW", dS_x1w[DC - 1])
                i = g.dma_start(out=tp[:, :], in_=x1_scr.ap())
                C.inc(i, "dG", 16)
        if "qkv" in taps:
            tpq = tapt("qkv", [128, 3, 2, S], BF16)

            @G
            def _(g):
                C.wait(g, "vE", evac_done[47])
                i = g.dma_start(out=tpq[:, 0, :, :], in_=qt[:, :, :])
                C.inc(i, "dG", 16)
                i = g.dma_start(out=tpq[:, 1, :, :], in_=kt[:, :, :])
                C.inc(i, "dG", 16)
                i = g.dma_start(out=tpq[:, 2, :, :],
                                in_=vt.rearrange("p h k e -> p h (k e)"))
                C.inc(i, "dG", 16)

        if "t1b" in taps:
            tp = tapt("t1b", [D, HIDP], BF16)

            @G
            def _(g):
                C.wait(g, "dW", out_store[0])
                i = g.dma_start(out=tp[:, :], in_=t1_out.ap())
                C.inc(i, "dG", 16)
        if "ln2b" in taps:
            tp = tapt("ln2b", [128, 16, TC], BF16)

            @G
            def _(g):
                C.wait(g, "dW", out_store[0])
                i = g.dma_start(out=tp[:, :, :], in_=ln2xt[:, :, :])
                C.inc(i, "dG", 16)

        @Sy
        def _(sy):
            for nm in ("dS", "dS2", "dW", "dQ", "dQ2", "dG", "dG2", "dG3"):
                C.wait(sy, nm, C.val(nm))

    return tap_t



def _mk_masks():
    kl = np.arange(128)[:, None]
    ql = np.arange(512)[None, :]
    return np.stack([(128 * j + kl <= ql) for j in range(4)]).astype(ml_dtypes.bfloat16)


def make_core_inputs(inp):
    x = np.asarray(inp["L0_attn_x"], dtype=np.float32).reshape(B * S, D)
    wq = np.asarray(inp["L0_attn_W_Q"], dtype=np.float32)
    wk = np.asarray(inp["L0_attn_W_K"], dtype=np.float32)
    wv = np.asarray(inp["L0_attn_W_V"], dtype=np.float32)
    wo = np.asarray(inp["L0_attn_W_O"], dtype=np.float32)
    w1 = np.asarray(inp["L0_ffn_w1"], dtype=np.float32)
    w3 = np.asarray(inp["L0_ffn_w3"], dtype=np.float32)
    w2 = np.asarray(inp["L0_ffn_w2"], dtype=np.float32)

    w1p = np.zeros((HIDP, D), np.float32); w1p[:HID] = w1
    w3p = np.zeros((HIDP, D), np.float32); w3p[:HID] = w3
    w2p = np.zeros((D, HIDP), np.float32); w2p[:, :HID] = w2
    w1Tf = np.ascontiguousarray(w1p.T)
    w3Tf = np.ascontiguousarray(w3p.T)
    w2Tf = np.ascontiguousarray(w2p.T)
    masks = _mk_masks()
    ident = np.eye(128, dtype=ml_dtypes.bfloat16)

    maps = []
    for c in range(NCORE):
        h0 = 4 * (c % 4)
        wqkv_host = np.empty((128, 3, HL, DC, 128), np.float32)
        for m, w in enumerate((wq, wk, wv)):
            for h in range(HL):
                wqkv_host[:, m, h] = w[h0 + h].T.reshape(DC, 128, 128).transpose(1, 0, 2)
        wo_host = np.empty((128, HL, DC, 128), np.float32)
        for h in range(HL):
            wo_host[:, h] = wo[h0 + h].T.reshape(128, DC, 128)
        maps.append({
            "xT": np.ascontiguousarray(x[c * TC:(c + 1) * TC, :].T),
            "wqkv_in": wqkv_host,
            "wo_in": wo_host,
            "w1T": np.ascontiguousarray(w1Tf[c * D_SH:(c + 1) * D_SH, :]),
            "w3T": np.ascontiguousarray(w3Tf[c * D_SH:(c + 1) * D_SH, :]),
            "w2T": np.ascontiguousarray(w2Tf[c * HID_SH:(c + 1) * HID_SH, :]),
            "masks": masks,
            "ident": ident,
        })
    return maps


def kernel(**inputs):
    nc = bass.Bass("TRN2", target_bir_lowering=False)
    build_program(nc)
    maps = make_core_inputs(inputs)
    res = run_bass_kernel_spmd(nc, maps, core_ids=list(range(NCORE)))
    parts = [np.ascontiguousarray(res.results[c]["outT"].T) for c in range(NCORE)]
    return np.concatenate(parts, axis=0).reshape(B, S, D).astype(np.float32)


# version 06
# speedup vs baseline: 1.0190x; 1.0190x over previous
import math
import numpy as np
import ml_dtypes
from contextlib import ExitStack

import concourse.bass as bass
import concourse.mybir as mybir
from concourse.bass_utils import run_bass_kernel_spmd

F32 = mybir.dt.float32
BF16 = mybir.dt.bfloat16
AF = mybir.ActivationFunctionType
ALU = mybir.AluOpType

NCORE = 8
B, S, D, H, E = 2, 4096, 2048, 16, 128
HL = 4
TC = 1024
DC = D // 128
HID = 5461
HIDP = 5504
NHC = HIDP // 128
HID_SH = HIDP // NCORE
D_SH = D // NCORE
NQ = S // 512
QSCL = float(1.0 / math.sqrt(128.0))
W_N = float(HID * D)
LN_EPS = 1e-5

PB_SZ = 70656
OFF_QT = 0
OFF_KT = 8192
OFF_VT = 16384
OFF_WQKV = 24576
OFF_LNCH = 36864
OFF_Z = 53248
OFF_LN2XT = 0
OFF_ATTN = 16384
OFF_WO = 32768
OFF_P2STG = 40960
OFF_SQB = 45056
OFF_GT = 16384
OFF_FBUF = 38400
OFF_SLAB = 38400
OFF_FSTG = 60416
OFF_SIL2 = 68608

PF_SZ = 6656
PFO_XA, PFO_XB = 0, 1024
PFO_SCR = 2048
PFO_TMP = 3072
PFO_QW = 4096
PFO_X1A, PFO_X1B = 5472, 5984


def _ts(i, n):
    return slice(i * n, (i + 1) * n)


class Ctx:
    def __init__(self):
        self.h = {}
        self.n = {}

    def add(self, name, handle):
        self.h[name] = handle
        self.n[name] = 0

    def inc(self, instr, name, k=1):
        instr.then_inc(self.h[name], k)
        self.n[name] += k
        return self.n[name]

    def val(self, name):
        return self.n[name]

    def wait(self, eng, name, v=None):
        vv = self.n[name] if v is None else v
        if vv > 0:
            eng.wait_ge(self.h[name], vv)


def build_program(nc, taps=()):
    taps = set(taps)
    tap_t = {}

    xT = nc.dram_tensor("xT", [D, TC], F32, kind="ExternalInput").ap()
    wqkv_in = nc.dram_tensor("wqkv_in", [128, 3, HL, DC, 128], F32, kind="ExternalInput").ap()
    wo_in = nc.dram_tensor("wo_in", [128, HL, DC, 128], F32, kind="ExternalInput").ap()
    w1T = nc.dram_tensor("w1T", [D_SH, HIDP], F32, kind="ExternalInput").ap()
    w3T = nc.dram_tensor("w3T", [D_SH, HIDP], F32, kind="ExternalInput").ap()
    w2T = nc.dram_tensor("w2T", [HID_SH, D], F32, kind="ExternalInput").ap()
    masks = nc.dram_tensor("masks", [4, 128, 512], BF16, kind="ExternalInput").ap()
    ident = nc.dram_tensor("ident", [128, 128], BF16, kind="ExternalInput").ap()
    outT = nc.dram_tensor("outT", [D, TC], F32, kind="ExternalOutput").ap()

    lnx_in = nc.dram_tensor("lnx_in", [D, TC], BF16)
    lnx_out = nc.dram_tensor("lnx_out", [4 * D, TC], BF16)
    ar_in = nc.dram_tensor("ar_in", [1, 8], F32)
    ar_out = nc.dram_tensor("ar_out", [1, 8], F32)
    t1_in = nc.dram_tensor("t1_in", [D_SH, HIDP], BF16)
    t1_out = nc.dram_tensor("t1_out", [D, HIDP], BF16, addr_space="Shared")
    t3_in = nc.dram_tensor("t3_in", [D_SH, HIDP], BF16)
    t3_out = nc.dram_tensor("t3_out", [D, HIDP], BF16, addr_space="Shared")
    t2_in = nc.dram_tensor("t2_in", [HID_SH, D], BF16)
    t2_out = nc.dram_tensor("t2_out", [HIDP, D], BF16, addr_space="Shared")
    rs_in = nc.dram_tensor("rs_in", [4, D, TC], BF16)
    rs_out = nc.dram_tensor("rs_out", [D, TC], BF16)
    x1_scr = nc.dram_tensor("x1_scr", [D, TC], F32)
    warm_in = nc.dram_tensor("warm_in", [1, 16], F32)
    warm_out = nc.dram_tensor("warm_out", [4, 16], F32)
    warm_out8 = nc.dram_tensor("warm_out8", [8, 16], F32, addr_space="Shared")

    def tapt(name, shape, dtype):
        if name in taps and name not in tap_t:
            tap_t[name] = nc.dram_tensor("tap_" + name, shape, dtype, kind="ExternalOutput").ap()
        return tap_t.get(name)

    grp4 = [[0, 1, 2, 3], [4, 5, 6, 7]]
    grp8 = [list(range(8))]

    with ExitStack() as ectx:
        pb = ectx.enter_context(nc.sbuf_tensor("pb", [128, PB_SZ], BF16))
        pf = ectx.enter_context(nc.sbuf_tensor("pf", [128, PF_SZ], F32))
        ones_fc = ectx.enter_context(nc.sbuf_tensor("ones_fc", [128, 1], F32))
        ones_fr = ectx.enter_context(nc.sbuf_tensor("ones_fr", [1, 128], F32))
        ones_bc = ectx.enter_context(nc.sbuf_tensor("ones_bc", [128, 1], BF16))
        mask_s = ectx.enter_context(nc.sbuf_tensor("mask_s", [128, 4, 512], BF16))
        id_s = ectx.enter_context(nc.sbuf_tensor("id_s", [128, 128], BF16))
        rows = ectx.enter_context(nc.sbuf_tensor("rows", [1, 4608], F32))
        st1 = ectx.enter_context(nc.sbuf_tensor("st1", [128, 16], F32))
        exp_s = ectx.enter_context(nc.sbuf_tensor("exp_s", [128, 2, 512], BF16))
        expm_s = ectx.enter_context(nc.sbuf_tensor("expm_s", [128, 2, 512], BF16))
        vtmp_s = ectx.enter_context(nc.sbuf_tensor("vtmp_s", [128, 512], BF16))
        inv_s = ectx.enter_context(nc.sbuf_tensor("inv_s", [1, 512], F32))
        sil_s = ectx.enter_context(nc.sbuf_tensor("sil_s", [128, 512], BF16))
        qsg_s = ectx.enter_context(nc.sbuf_tensor("qsg_s", [128, 3, 688], BF16))
        qts_s = ectx.enter_context(nc.sbuf_tensor("qts_s", [128, 2, 688], BF16))

        PS = [ectx.enter_context(nc.psum_tensor("ps%d" % i, [128, 512], F32)) for i in range(6)]
        PT = [ectx.enter_context(nc.psum_tensor("pt%d" % i, [128, 1024], BF16)) for i in range(2)]

        C = Ctx()
        for nm in ("dS", "dS2", "dW", "dQ", "dQ2", "dG", "dG2", "dG3", "tE", "vE", "sE", "cC"):
            C.add(nm, ectx.enter_context(nc.semaphore(nm)))
        block = ectx.enter_context(nc.Block())

        lnxown = pb[:, 0:16384].rearrange("p (c t) -> p c t", c=16)
        qt = pb[:, OFF_QT:OFF_QT + 8192].rearrange("p (h t) -> p h t", h=2)
        kt = pb[:, OFF_KT:OFF_KT + 8192].rearrange("p (h t) -> p h t", h=2)
        vt = pb[:, OFF_VT:OFF_VT + 8192].rearrange("p (h k e) -> p h k e", h=2, k=32)
        wqkv = pb[:, OFF_WQKV:OFF_WQKV + 12288].rearrange("p (m h c e) -> p m h c e", m=3, h=2, c=DC)
        lnch = pb[:, OFF_LNCH:OFF_LNCH + 16384].rearrange("p (b c t) -> p b c t", b=2, c=16)
        z_v = pb[:, OFF_Z:OFF_Z + 16384].rearrange("p (h t) -> p h t", h=HL)
        ln2xt = pb[:, OFF_LN2XT:OFF_LN2XT + 16384].rearrange("p (c t) -> p c t", c=16)
        attn_t = pb[:, OFF_ATTN:OFF_ATTN + 16384].rearrange("p (c t) -> p c t", c=16)
        wo_v = pb[:, OFF_WO:OFF_WO + 8192].rearrange("p (h c e) -> p h c e", h=HL, c=DC)
        p2stg = pb[:, OFF_P2STG:OFF_P2STG + 4096].rearrange("p (t c) -> p t c", t=8)
        sqb = pb[:, OFF_SQB:OFF_SQB + 1024]
        gt_v = pb[:, OFF_GT:OFF_GT + NHC * 512].rearrange("p (h t) -> p h t", h=NHC)
        fbuf = pb[:, OFF_FBUF:OFF_FBUF + 16384].rearrange("p (b c h) -> p b c h", b=4, c=16)
        slab = pb[:, OFF_SLAB:OFF_SLAB + NHC * 512].rearrange("p (h d) -> p h d", h=NHC)
        fstg = pb[:, OFF_FSTG:OFF_FSTG + 8192].rearrange("p (c t) -> p c t", c=16)
        fsq_v = pb[:, OFF_SIL2:OFF_SIL2 + 512]

        xa = pf[:, PFO_XA:PFO_XA + 1024]
        xb = pf[:, PFO_XB:PFO_XB + 1024]
        fscr = pf[:, PFO_SCR:PFO_SCR + 1024]
        ftmp = pf[:, PFO_TMP:PFO_TMP + 1024]
        qw = [pf[:, PFO_QW:PFO_QW + 688], pf[:, PFO_QW + 688:PFO_QW + 1376]]
        qx = pf[:, PFO_QW:PFO_QW + 1024]
        x1pb = [pf[:, PFO_X1A:PFO_X1A + 512], pf[:, PFO_X1B:PFO_X1B + 512]]
        bcz_s = pf[:, PFO_X1A:PFO_X1A + 512]

        r_sum = rows[:, 0:1024]
        r_var = rows[:, 1024:2048]
        r_m = rows[:, 2048:3072]
        r_istd = rows[:, 3072:4096]
        r_misc = rows[:, 4096:4592]
        r_svec = rows[:, 4592:4600]
        r_sld = rows[:, 4600:4608]

        def T(f): block.tensor(f)
        def V(f): block.vector(f)
        def Sc(f): block.scalar(f)
        def Sy(f): block.sync(f)
        def G(f): block.gpsimd(f)

        dG_wqkv = {}

        @G
        def _(g):
            i = g.collective_compute("AllGather", ALU.bypass, replica_groups=grp4,
                                     ins=[warm_in.ap()], outs=[warm_out.ap()])
            C.inc(i, "cC", 1)
            i = g.collective_compute("AllGather", ALU.bypass, replica_groups=grp8,
                                     ins=[warm_in.ap()], outs=[warm_out8.ap()])
            C.inc(i, "cC", 1)
            g.memset(ones_fc[:, :], 1.0)
            g.memset(ones_fr[:, :], 1.0)
            g.memset(ones_bc[:, :], 1.0)
            g.memset(r_svec[:, :], 0.0)
            i = g.dma_start(out=mask_s[:, :, :], in_=masks.rearrange("j p t -> p j t"))
            C.inc(i, "dG", 16)
            i = g.dma_start(out=id_s[:, :], in_=ident[:, :])
            C.inc(i, "dG", 16)
            for m in range(3):
                i = g.dma_start(out=wqkv[:, m, :, :, :], in_=wqkv_in[:, m, 0:2, :, :])
                C.inc(i, "dG", 16)
            dG_wqkv[0] = C.val("dG")
        dG_init = C.val("dG")

        sE_sq = []
        tE_x = []

        for c in range(DC):
            def s_p0(s, c=c):
                C.wait(s, "dS" if c % 2 == 0 else "dS2", 16 * (c // 2 + 1))
                if c >= 1:
                    C.wait(s, "tE", tE_x[c - 1])
                i = s.activation(fscr[:, :], (xa if c % 2 == 0 else xb), AF.Square)
                sE_sq.append(C.inc(i, "sE", 1))
            Sc(s_p0)

            def t_p0(t, c=c):
                if c == 0:
                    C.wait(t, "dG", dG_init)
                xbuf = xa if c % 2 == 0 else xb
                C.wait(t, "dS" if c % 2 == 0 else "dS2", 16 * (c // 2 + 1))
                t.matmul(PS[4][0:1, :], ones_fc[:, :], xbuf[:, 0:512], start=(c == 0), stop=(c == DC - 1))
                t.matmul(PS[5][0:1, :], ones_fc[:, :], xbuf[:, 512:1024], start=(c == 0), stop=(c == DC - 1))
                C.wait(t, "sE", sE_sq[c])
                t.matmul(PS[0][0:1, :], ones_fc[:, :], fscr[:, 0:512], start=(c == 0), stop=(c == DC - 1))
                i = t.matmul(PS[1][0:1, :], ones_fc[:, :], fscr[:, 512:1024], start=(c == 0), stop=(c == DC - 1))
                tE_x.append(C.inc(i, "tE", 1))
            T(t_p0)

        @Sy
        def _(sy):
            for c in range(DC):
                if c >= 2:
                    C.wait(sy, "tE", tE_x[c - 2])
                i = sy.dma_start(out=(xa if c % 2 == 0 else xb), in_=xT[_ts(c, 128), :])
                C.inc(i, "dS" if c % 2 == 0 else "dS2", 16)

        def emit_rowchain(sum_srcs, sq_srcs, width):
            @V
            def _(v):
                C.wait(v, "tE", C.val("tE"))
                i = None
                for j, (bk, sl) in enumerate(sum_srcs):
                    i = v.tensor_copy(r_sum[:, _ts(j, 512)], bk[0:1, sl])
                for j, (bk, sl) in enumerate(sq_srcs):
                    i = v.tensor_copy(r_var[:, _ts(j, 512)], bk[0:1, sl])
                C.inc(i, "vE", 1)

            @Sc
            def _(s):
                C.wait(s, "vE", C.val("vE"))
                s.mul(r_m[:, 0:width], r_sum[:, 0:width], 1.0 / D)
                i = s.mul(r_istd[:, 0:width], r_var[:, 0:width], 1.0 / D)
                C.inc(i, "sE", 1)

            @V
            def _(v):
                C.wait(v, "sE", C.val("sE"))
                v.tensor_tensor(r_sum[:, 0:width], r_m[:, 0:width], r_m[:, 0:width], ALU.mult)
                v.tensor_tensor(r_var[:, 0:width], r_istd[:, 0:width], r_sum[:, 0:width], ALU.subtract)
                i = v.tensor_scalar_add(r_istd[:, 0:width], r_var[:, 0:width], LN_EPS)
                C.inc(i, "vE", 1)

            @Sc
            def _(s):
                C.wait(s, "vE", C.val("vE"))
                i = s.activation(r_sum[:, 0:width], r_istd[:, 0:width], AF.Ln)
                C.inc(i, "sE", 1)
                C.wait(s, "sE", C.val("sE"))
                i = s.activation(r_istd[:, 0:width], r_sum[:, 0:width], AF.Exp, scale=-0.5)
                C.inc(i, "sE", 1)

        emit_rowchain([(PS[4], slice(0, 512)), (PS[5], slice(0, 512))],
                      [(PS[0], slice(0, 512)), (PS[1], slice(0, 512))], 1024)

        def emit_bcast_mi():
            @T
            def _(t):
                C.wait(t, "sE", C.val("sE"))
                t.matmul(PS[0][:, :], ones_fr[:, :], r_m[0:1, 0:512], start=True, stop=True)
                t.matmul(PS[1][:, :], ones_fr[:, :], r_m[0:1, 512:1024], start=True, stop=True)
                t.matmul(PS[2][:, :], ones_fr[:, :], r_istd[0:1, 0:512], start=True, stop=True)
                i = t.matmul(PS[3][:, :], ones_fr[:, :], r_istd[0:1, 512:1024], start=True, stop=True)
                C.inc(i, "tE", 1)

        emit_bcast_mi()
        tE_ln1bc = C.val("tE")

        vE_app = []

        @V
        def _(v):
            C.wait(v, "tE", tE_ln1bc)
            for c in range(DC):
                xbuf = xa if c % 2 == 0 else xb
                C.wait(v, "dS" if c % 2 == 0 else "dS2", 16 * (8 + c // 2 + 1))
                i = None
                for hf in range(2):
                    v.tensor_tensor(ftmp[:, _ts(hf, 512)], xbuf[:, _ts(hf, 512)], PS[hf][:, :], ALU.subtract)
                    i = v.tensor_tensor(lnxown[:, c, _ts(hf, 512)], ftmp[:, _ts(hf, 512)], PS[2 + hf][:, :], ALU.mult)
                vE_app.append(C.inc(i, "vE", 1))

        @Sy
        def _(sy):
            for c in range(DC):
                if c >= 2:
                    C.wait(sy, "vE", vE_app[c - 2])
                else:
                    C.wait(sy, "tE", tE_x[DC - 2 + c])
                i = sy.dma_start(out=(xa if c % 2 == 0 else xb), in_=xT[_ts(c, 128), :])
                C.inc(i, "dS" if c % 2 == 0 else "dS2", 16)

        @G
        def _(g):
            C.wait(g, "vE", vE_app[-1])
            i = g.dma_start(out=lnx_in.rearrange("(c p) t -> p c t", p=128), in_=lnxown[:, :, :])
            C.inc(i, "dG", 16)
            C.wait(g, "dG", C.val("dG"))
            i = g.collective_compute("AllGather", ALU.bypass, replica_groups=grp4,
                                     ins=[lnx_in.ap()], outs=[lnx_out.ap()])
            C.inc(i, "cC", 1)
        cC_lnx = C.val("cC")

        qchunks = []
        for mi, wsrc in ((0, w1T), (1, w3T), (2, w2T)):
            if mi < 2:
                for r in range(2):
                    for cc in range(8):
                        qchunks.append((wsrc, r * 128, 128, cc * 688, 688, mi))
            else:
                for r in range(6):
                    nr = 128 if r < 5 else HID_SH - 640
                    for (c0, ncol) in ((0, 688), (688, 688), (1376, 672)):
                        qchunks.append((wsrc, r * 128, nr, c0, ncol, mi))
        NQCH = len(qchunks)
        dQ_q1p = [C.val("dQ"), C.val("dQ2")]

        QEMIT = []
        QA_END = [0]
        vE_qred = []
        qacc_par = {0: 0, 1: 0, 2: 0}
        _first = {0: True, 1: True, 2: True}
        _par = {0: 0, 1: 0, 2: 0}
        sE_qs = {}
        vE_qt = {}
        cC_t = {}
        dG23 = {}

        for j, ch in enumerate(qchunks):
            def q_ld1(sc, j=j, ch=ch):
                wsrc, r0, nr, c0, ncol, mi = ch
                if j >= 2:
                    C.wait(sc, "vE", vE_qred[j - 2])
                i = sc.dma_start(out=qw[j % 2][0:nr, 0:ncol], in_=wsrc[r0:r0 + nr, c0:c0 + ncol])
                C.inc(i, "dQ" if j % 2 == 0 else "dQ2", 16)
            QEMIT.append((Sc, q_ld1))

            def q_red(v, j=j, ch=ch):
                wsrc, r0, nr, c0, ncol, mi = ch
                C.wait(v, "dQ" if j % 2 == 0 else "dQ2", dQ_q1p[j % 2] + 16 * (j // 2 + 1))
                buf = qw[j % 2]
                if _first[mi]:
                    i = v.reduce_sum(st1[:, mi * 2:mi * 2 + 1], buf[0:nr, 0:ncol],
                                     axis=mybir.AxisListType.X, apply_absolute_value=True)
                    _first[mi] = False
                else:
                    v.reduce_sum(st1[0:nr, 6:7], buf[0:nr, 0:ncol],
                                 axis=mybir.AxisListType.X, apply_absolute_value=True)
                    o = _par[mi]
                    _par[mi] ^= 1
                    n2 = _par[mi]
                    v.tensor_copy(st1[:, mi * 2 + n2:mi * 2 + n2 + 1], st1[:, mi * 2 + o:mi * 2 + o + 1])
                    i = v.tensor_tensor(st1[0:nr, mi * 2 + n2:mi * 2 + n2 + 1],
                                        st1[0:nr, mi * 2 + o:mi * 2 + o + 1],
                                        st1[0:nr, 6:7], ALU.add)
                vE_qred.append(C.inc(i, "vE", 1))
                qacc_par[mi] = _par[mi]
            QEMIT.append((V, q_red))

        def q_sum(t):
            C.wait(t, "vE", vE_qred[-1])
            i = None
            for mi in range(3):
                acc = st1[:, mi * 2 + qacc_par[mi]:mi * 2 + qacc_par[mi] + 1]
                i = t.matmul(PS[4][0:1, mi:mi + 1], ones_fc[:, :], acc, start=True, stop=True)
            C.inc(i, "tE", 1)
        QEMIT.append((T, q_sum))

        def q_cp(v):
            C.wait(v, "tE", C.val("tE"))
            i = v.tensor_copy(r_svec[:, 0:3], PS[4][0:1, 0:3])
            C.inc(i, "vE", 1)
        QEMIT.append((V, q_cp))

        def q_ar(g):
            C.wait(g, "vE", C.val("vE"))
            i = g.dma_start(out=ar_in[:, :], in_=r_svec[:, :])
            C.inc(i, "dG", 16)
            C.wait(g, "dG", C.val("dG"))
            i = g.collective_compute("AllReduce", ALU.add, replica_groups=grp8,
                                     ins=[ar_in.ap()], outs=[ar_out.ap()])
            C.inc(i, "cC", 1)
            C.wait(g, "cC", C.val("cC"))
            i = g.dma_start(out=r_sld[:, :], in_=ar_out[:, :])
            C.inc(i, "dG", 16)
        QEMIT.append((G, q_ar))

        def q_smul(sc):
            C.wait(sc, "dG", C.val("dG"))
            i = sc.mul(r_misc[:, 0:3], r_sld[:, 0:3], 0.5 / W_N)
            C.inc(i, "sE", 1)
        QEMIT.append((Sc, q_smul))

        def q_bc(t):
            C.wait(t, "sE", C.val("sE"))
            i = t.matmul(PS[5][:, 0:3], ones_fr[:, :], r_misc[0:1, 0:3], start=True, stop=True)
            C.inc(i, "tE", 1)
        QEMIT.append((T, q_bc))

        vE_sh_h = [0]

        def q_sh(v):
            C.wait(v, "tE", C.val("tE"))
            v.tensor_copy(st1[:, 8:11], PS[5][:, 0:3])
            i = v.tensor_scalar_mul(st1[:, 11:14], PS[5][:, 0:3], -1.0)
            vE_sh_h[0] = C.inc(i, "vE", 1)
        QEMIT.append((V, q_sh))
        QA_END[0] = len(QEMIT)

        dQ_q2p = [None, None]

        def q_p2init(sc):
            dQ_q2p[0] = C.val("dQ")
            dQ_q2p[1] = C.val("dQ2")
            C.wait(sc, "vE", vE_sh_h[0])
            for j in range(2):
                wsrc, r0, nr, c0, ncol, mi = qchunks[j]
                i = sc.dma_start(out=qw[j % 2][0:nr, 0:ncol], in_=wsrc[r0:r0 + nr, c0:c0 + ncol])
                C.inc(i, "dQ" if j % 2 == 0 else "dQ2", 16)
        QEMIT.append((Sc, q_p2init))

        dsts = {0: t1_in, 1: t3_in, 2: t2_in}
        lastj = {0: 15, 1: 31, 2: NQCH - 1}

        _written = set()

        def emit_write(sc, j):
            if j in _written:
                return
            _written.add(j)
            wsrc, r0, nr, c0, ncol, mi = qchunks[j]
            C.wait(sc, "vE", vE_qt[j])
            sem = "dG2" if j % 2 == 0 else "dG3"
            i = sc.dma_start(out=dsts[mi][r0:r0 + nr, c0:c0 + ncol],
                             in_=qts_s[0:nr, j % 2, 0:ncol])
            dG23[j] = (sem, C.inc(i, sem, 16))

        for j, ch in enumerate(qchunks):
            def q_sgn(sc, j=j, ch=ch):
                wsrc, r0, nr, c0, ncol, mi = ch
                if j >= 1:
                    emit_write(sc, j - 1)
                C.wait(sc, "dQ" if j % 2 == 0 else "dQ2", dQ_q2p[j % 2] + 16 * (j // 2 + 1))
                if j >= 1:
                    C.wait(sc, "vE", vE_qt[j - 1])
                sc.activation(qsg_s[0:nr, 0, 0:ncol], qw[j % 2][0:nr, 0:ncol], AF.Sign,
                              bias=st1[0:nr, 11 + mi:12 + mi], scale=1.0)
                i = sc.activation(qsg_s[0:nr, 1, 0:ncol], qw[j % 2][0:nr, 0:ncol], AF.Sign,
                                  bias=st1[0:nr, 8 + mi:9 + mi], scale=1.0)
                sE_qs[j] = C.inc(i, "sE", 1)
                if j + 2 < NQCH:
                    C.wait(sc, "sE", sE_qs[j])
                    w2s, r02, nr2, c02, nc2, _ = qchunks[j + 2]
                    i = sc.dma_start(out=qw[j % 2][0:nr2, 0:nc2], in_=w2s[r02:r02 + nr2, c02:c02 + nc2])
                    C.inc(i, "dQ" if j % 2 == 0 else "dQ2", 16)
            QEMIT.append((Sc, q_sgn))

            def q_tt(v, j=j, ch=ch):
                wsrc, r0, nr, c0, ncol, mi = ch
                C.wait(v, "sE", sE_qs[j])
                if j >= 2:
                    C.wait(v, dG23[j - 2][0], dG23[j - 2][1])
                v.tensor_tensor(qsg_s[0:nr, 2, 0:ncol], qsg_s[0:nr, 0, 0:ncol],
                                qsg_s[0:nr, 1, 0:ncol], ALU.add)
                i = v.tensor_scalar_mul(qts_s[0:nr, j % 2, 0:ncol], qsg_s[0:nr, 2, 0:ncol],
                                        st1[0:nr, 8 + mi:9 + mi])
                vE_qt[j] = C.inc(i, "vE", 1)
            QEMIT.append((V, q_tt))

            if j == lastj[0] or j == lastj[1] or j == lastj[2]:
                def q_ag(g, j=j):
                    mi = {lastj[0]: 0, lastj[1]: 1, lastj[2]: 2}[j]
                    C.wait(g, "dG2", C.val("dG2"))
                    C.wait(g, "dG3", C.val("dG3"))
                    src, dst = {0: (t1_in, t1_out), 1: (t3_in, t3_out), 2: (t2_in, t2_out)}[mi]
                    i = g.collective_compute("AllGather", ALU.bypass, replica_groups=grp8,
                                             ins=[src.ap()], outs=[dst.ap()])
                    cC_t[mi] = C.inc(i, "cC", 1)

                def q_wlast(sc, j=j):
                    emit_write(sc, j)
                QEMIT.append((Sc, q_wlast))
                QEMIT.append((G, q_ag))

        _emitted_last = set()

        def pop_quant(k):
            for _ in range(k):
                if not QEMIT:
                    return
                reg, fn = QEMIT.pop(0)
                QA_END[0] = max(0, QA_END[0] - 1)
                reg(fn)

        def drain_quant_psum_part():
            while QA_END[0] > 0:
                pop_quant(1)

        evac_done = {}
        qkv_done = {}
        dS_chunk = {}
        last_exp_use = {0: 0, 1: 0}
        last_pv = {0: 0, 1: 0}
        last_z_use = {0: 0, 1: 0}
        last_den = [0]
        last_bc = [0]
        vt_use = {0: 0, 1: 0}
        z_done = [0]
        GROUPS = [(0, 0), (1, 0), (2, 0), (0, 1), (1, 1), (2, 1)]

        for pk in range(2):
            for j in range(8):
                jg = pk * 8 + j

                def sy_mini(sy, j=j, jg=jg):
                    if jg == 0:
                        C.wait(sy, "cC", cC_lnx)
                    if jg >= 2:
                        C.wait(sy, "tE", qkv_done[jg - 2])
                    blkr = (j // 2) * D
                    c0 = (j % 2) * 512
                    src = lnx_out[blkr:blkr + D, c0:c0 + 512].rearrange("(c p) t -> p c t", p=128)
                    sem = "dS" if jg % 2 == 0 else "dS2"
                    for q in range(4):
                        i = sy.dma_start(out=lnch[:, j % 2, _ts(q, 4), :], in_=src[:, _ts(q, 4), :])
                        C.inc(i, sem, 16)
                    dS_chunk[jg] = (sem, C.val(sem))
                Sy(sy_mini)

                for g_i, (m, h) in enumerate(GROUPS):
                    Gg = jg * 6 + g_i

                    def t_mini(t, m=m, h=h, j=j, jg=jg, Gg=Gg, pk=pk):
                        bank = PS[Gg % 2]
                        C.wait(t, dS_chunk[jg][0], dS_chunk[jg][1])
                        C.wait(t, "dG", dG_wqkv[pk])
                        if Gg >= 2 and (Gg - 2) in evac_done:
                            C.wait(t, "vE", evac_done[Gg - 2])
                        i = None
                        for cc in range(DC):
                            i = t.matmul(bank[:, :], wqkv[:, m, h, cc, :], lnch[:, j % 2, cc, :],
                                         start=(cc == 0), stop=(cc == DC - 1))
                        qkv_done[jg] = C.inc(i, "tE", 1)
                    T(t_mini)

                    if m < 2:
                        def v_mini(v, m=m, h=h, j=j, Gg=Gg):
                            bank = PS[Gg % 2]
                            C.wait(v, "tE", C.val("tE"))
                            i = v.tensor_copy((qt if m == 0 else kt)[:, h, _ts(j, 512)], bank[:, :])
                            evac_done[Gg] = C.inc(i, "vE", 1)
                        V(v_mini)
                        pop_quant(2)
                    else:
                        def v_mini1(v, Gg=Gg):
                            bank = PS[Gg % 2]
                            C.wait(v, "tE", C.val("tE"))
                            i = v.tensor_copy(vtmp_s[:, :], bank[:, :])
                            C.inc(i, "vE", 1)
                        V(v_mini1)

                        vpar = jg % 2

                        def t_mini2(t, vpar=vpar, jg=jg):
                            C.wait(t, "vE", C.val("vE"))
                            if vt_use[vpar]:
                                C.wait(t, "vE", vt_use[vpar])
                            i = None
                            for q in range(4):
                                i = t.transpose(PT[vpar][:, _ts(q, 128)], vtmp_s[:, _ts(q, 128)], id_s[:, :])
                            qkv_done[jg] = C.inc(i, "tE", 1)
                        T(t_mini2)

                        def v_mini2(v, h=h, j=j, Gg=Gg, vpar=vpar):
                            C.wait(v, "tE", C.val("tE"))
                            dst = vt[:, h, 4 * j:4 * j + 4, :].rearrange("p k e -> p (k e)")
                            i = v.tensor_copy(dst, PT[vpar][:, 0:512])
                            vt_use[vpar] = C.inc(i, "vE", 1)
                            evac_done[Gg] = C.val("vE")
                        V(v_mini2)
                        pop_quant(2)

            if pk == 0:
                def g_wq1(g):
                    C.wait(g, "tE", qkv_done[7])
                    for m in range(3):
                        i = g.dma_start(out=wqkv[:, m, :, :, :], in_=wqkv_in[:, m, 2:4, :, :])
                        C.inc(i, "dG", 16)
                    dG_wqkv[1] = C.val("dG")
                G(g_wq1)

            drain_quant_psum_part()
            vE_pass_evacs = evac_done[(pk * 8 + 7) * 6 + 5]

            for hl in range(2):
                hglob = 2 * pk + hl
                for qc in range(NQ):
                    nk = 4 * qc + 4
                    tS = {}

                    def t_pre(t, hl=hl, qc=qc, nk=nk, tS=tS, vE_pass_evacs=vE_pass_evacs):
                        C.wait(t, "vE", vE_pass_evacs)
                        if last_den[0]:
                            C.wait(t, "vE", last_den[0])
                        if last_z_use[qc % 2]:
                            C.wait(t, "vE", last_z_use[qc % 2])
                        for kc in range(min(2, nk)):
                            if last_exp_use[kc % 2]:
                                C.wait(t, "sE", last_exp_use[kc % 2])
                            i = t.matmul(PS[kc % 2][:, :], kt[:, hl, _ts(kc, 128)],
                                         qt[:, hl, _ts(qc, 512)], start=True, stop=True)
                            tS[kc] = C.inc(i, "tE", 1)
                    T(t_pre)

                    for kc in range(nk):
                        diag = kc >= nk - 4
                        eK = [0]
                        mK = [0]

                        def s_exp(s, kc=kc, tS=tS, eK=eK):
                            C.wait(s, "tE", tS[kc])
                            if last_pv[kc % 2]:
                                C.wait(s, "tE", last_pv[kc % 2])
                            i = s.activation(exp_s[:, kc % 2, :], PS[kc % 2][:, :], AF.Exp, scale=QSCL)
                            eK[0] = C.inc(i, "sE", 1)
                            last_exp_use[kc % 2] = eK[0]
                        Sc(s_exp)

                        if diag:
                            def v_mask(v, kc=kc, nk=nk, eK=eK, mK=mK):
                                C.wait(v, "sE", eK[0])
                                if last_pv[kc % 2]:
                                    C.wait(v, "tE", last_pv[kc % 2])
                                jj = kc - (nk - 4)
                                i = v.tensor_tensor(expm_s[:, kc % 2, :], exp_s[:, kc % 2, :],
                                                    mask_s[:, jj, :], ALU.mult)
                                mK[0] = C.inc(i, "vE", 1)
                            V(v_mask)

                        def t_pv(t, hl=hl, qc=qc, kc=kc, nk=nk, diag=diag, eK=eK, mK=mK, tS=tS):
                            if diag:
                                C.wait(t, "vE", mK[0])
                                src = expm_s[:, kc % 2, :]
                            else:
                                C.wait(t, "sE", eK[0])
                                src = exp_s[:, kc % 2, :]
                            i = t.matmul(PS[2 + qc % 2][:, :], vt[:, hl, kc, :], src,
                                         start=(kc == 0), stop=(kc == nk - 1))
                            C.inc(i, "tE", 1)
                            i = t.matmul(PS[4][0:1, :], ones_bc[:, :], src,
                                         start=(kc == 0), stop=(kc == nk - 1))
                            last_pv[kc % 2] = C.inc(i, "tE", 1)
                            if kc + 2 < nk:
                                C.wait(t, "sE", last_exp_use[kc % 2])
                                i = t.matmul(PS[kc % 2][:, :], kt[:, hl, _ts(kc + 2, 128)],
                                             qt[:, hl, _ts(qc, 512)], start=True, stop=True)
                                tS[kc + 2] = C.inc(i, "tE", 1)
                        T(t_pv)

                    def v_recip(v):
                        C.wait(v, "tE", C.val("tE"))
                        i = v.reciprocal(inv_s[:, :], PS[4][0:1, :])
                        last_den[0] = C.inc(i, "vE", 1)
                    V(v_recip)

                    def t_bc(t):
                        C.wait(t, "vE", last_den[0])
                        if last_bc[0]:
                            C.wait(t, "vE", last_bc[0])
                        i = t.matmul(PS[5][:, :], ones_fr[:, :], inv_s[0:1, :], start=True, stop=True)
                        C.inc(i, "tE", 1)
                    T(t_bc)

                    def v_zmul(v, hglob=hglob, qc=qc):
                        C.wait(v, "tE", C.val("tE"))
                        v.tensor_copy(bcz_s[:, :], PS[5][:, :])
                        i = v.tensor_tensor(z_v[:, hglob, _ts(qc, 512)], PS[2 + qc % 2][:, :],
                                            bcz_s[:, :], ALU.mult)
                        last_z_use[qc % 2] = C.inc(i, "vE", 1)
                        last_bc[0] = C.val("vE")
                        z_done[0] = C.val("vE")
                    V(v_zmul)
                    pop_quant(3)

        vE_zall = z_done[0]

        pop_quant(10 ** 9)

        dG_wo = [0]

        @G
        def _(g):
            C.wait(g, "tE", qkv_done[15])
            i = g.dma_start(out=wo_v[:, :, :, :], in_=wo_in[:, :, :, :])
            dG_wo[0] = C.inc(i, "dG", 16)

        p2_evac = {}
        dS_rs = [0]

        for dc in range(DC):
            for grp, t5s in ((0, (0, 1, 2, 3, 4, 5)), (1, (6, 7))):
                def t_p2(t, dc=dc, grp=grp, t5s=t5s):
                    C.wait(t, "dG", dG_wo[0])
                    C.wait(t, "vE", vE_zall)
                    prev = (dc, 0) if grp == 1 else (dc - 1, 1)
                    if prev in p2_evac:
                        C.wait(t, "vE", p2_evac[prev])
                    i = None
                    for he in range(4):
                        for bi, t5 in enumerate(t5s):
                            i = t.matmul(PS[bi][:, :], wo_v[:, he, dc, :],
                                         z_v[:, he, _ts(t5, 512)], start=(he == 0), stop=(he == 3))
                    C.inc(i, "tE", 1)
                T(t_p2)

                def v_p2(v, dc=dc, grp=grp, t5s=t5s):
                    C.wait(v, "tE", C.val("tE"))
                    if grp == 0 and dc >= 1:
                        C.wait(v, "dW", dS_rs[0])
                    i = None
                    for bi, t5 in enumerate(t5s):
                        i = v.tensor_copy(p2stg[:, t5, :], PS[bi][:, :])
                    p2_evac[(dc, grp)] = C.inc(i, "vE", 1)
                V(v_p2)

            def sy_p2(sy, dc=dc):
                C.wait(sy, "vE", p2_evac[(dc, 1)])
                rsv = rs_in.rearrange("s (c p) (h t) -> p c s h t", p=128, t=512)
                i = sy.dma_start(out=rsv[:, dc, :, :, :],
                                 in_=p2stg.rearrange("p (s h) t -> p s h t", s=4))
                dS_rs[0] = C.inc(i, "dW", 16)
            Sy(sy_p2)

        @G
        def _(g):
            C.wait(g, "dW", dS_rs[0])
            i = g.collective_compute("ReduceScatter", ALU.add, replica_groups=grp4,
                                     ins=[rs_in.ap()], outs=[rs_out.ap()])
            C.inc(i, "cC", 1)
        cC_rs = C.val("cC")

        dS_at = {}

        @Sy
        def _(sy):
            C.wait(sy, "cC", cC_rs)
            for c in range(DC):
                i = sy.dma_start(out=attn_t[:, c, :], in_=rs_out[_ts(c, 128), :])
                C.inc(i, "dS", 16)
            for c in range(DC):
                dS_at[c] = C.val("dS")

        tE_asq = {}
        for c in range(DC):
            def s_asq(s, c=c):
                C.wait(s, "dS", dS_at[c])
                if c >= 1:
                    C.wait(s, "tE", tE_asq[c - 1])
                i = s.activation(sqb[:, :], attn_t[:, c, :], AF.Square)
                C.inc(i, "sE", 1)
            Sc(s_asq)

            def t_asq(t, c=c):
                C.wait(t, "dS", dS_at[c])
                t.matmul(PS[4][0:1, :], ones_bc[:, :], attn_t[:, c, 0:512], start=(c == 0), stop=(c == DC - 1))
                t.matmul(PS[5][0:1, :], ones_bc[:, :], attn_t[:, c, 512:1024], start=(c == 0), stop=(c == DC - 1))
                C.wait(t, "sE", C.val("sE"))
                t.matmul(PS[0][0:1, :], ones_bc[:, :], sqb[:, 0:512], start=(c == 0), stop=(c == DC - 1))
                i = t.matmul(PS[1][0:1, :], ones_bc[:, :], sqb[:, 512:1024], start=(c == 0), stop=(c == DC - 1))
                tE_asq[c] = C.inc(i, "tE", 1)
            T(t_asq)

        emit_rowchain([(PS[4], slice(0, 512)), (PS[5], slice(0, 512))],
                      [(PS[0], slice(0, 512)), (PS[1], slice(0, 512))], 1024)
        emit_bcast_mi()
        tE_abc = C.val("tE")


        x2ld = {}

        @Sc
        def _(s):
            C.wait(s, "vE", vE_app[-1])
            for c in range(2):
                sem = "dQ" if c % 2 == 0 else "dQ2"
                i = s.dma_start(out=(xa if c % 2 == 0 else xb), in_=xT[_ts(c, 128), :])
                x2ld[c] = (sem, C.inc(i, sem, 16))

        vE_x1 = {}
        dS_x1w = {}
        for c in range(DC):
            def v_x1(v, c=c):
                xbuf = xa if c % 2 == 0 else xb
                C.wait(v, "tE", tE_abc)
                C.wait(v, "dS", dS_at[c])
                C.wait(v, x2ld[c][0], x2ld[c][1])
                if c >= 1:
                    C.wait(v, "dW", dS_x1w[c - 1])
                i = None
                for hf in range(2):
                    v.tensor_tensor(ftmp[:, _ts(hf, 512)], attn_t[:, c, _ts(hf, 512)], PS[hf][:, :], ALU.subtract)
                    v.tensor_tensor(qx[:, _ts(hf, 512)], ftmp[:, _ts(hf, 512)], PS[2 + hf][:, :], ALU.mult)
                    i = v.tensor_tensor(fscr[:, _ts(hf, 512)], qx[:, _ts(hf, 512)],
                                        xbuf[:, _ts(hf, 512)], ALU.add)
                vE_x1[c] = C.inc(i, "vE", 1)
            V(v_x1)

            def sy_x1(sy, c=c):
                C.wait(sy, "vE", vE_x1[c])
                i = sy.dma_start(out=x1_scr[_ts(c, 128), :], in_=fscr[:, :])
                dS_x1w[c] = C.inc(i, "dW", 16)
            Sy(sy_x1)

            def s_x1(s, c=c):
                if c + 2 < DC:
                    C.wait(s, "vE", vE_x1[c])
                    sem = "dQ" if c % 2 == 0 else "dQ2"
                    i = s.dma_start(out=(xa if c % 2 == 0 else xb), in_=xT[_ts(c + 2, 128), :])
                    x2ld[c + 2] = (sem, C.inc(i, sem, 16))
            Sc(s_x1)

        stld = {}

        @Sc
        def _(s):
            C.wait(s, "vE", vE_x1[DC - 1])
            for c in range(2):
                C.wait(s, "dW", dS_x1w[c])
                sem = "dQ" if c % 2 == 0 else "dQ2"
                i = s.dma_start(out=(xa if c % 2 == 0 else xb), in_=x1_scr[_ts(c, 128), :])
                stld[c] = (sem, C.inc(i, sem, 16))

        sE_st = {}
        tE_st = {}
        for c in range(DC):
            def s_stA(s, c=c):
                C.wait(s, stld[c][0], stld[c][1])
                if c >= 1:
                    C.wait(s, "tE", tE_st[c - 1])
                i = s.activation(qx[:, :], (xa if c % 2 == 0 else xb), AF.Square)
                sE_st[c] = C.inc(i, "sE", 1)
            Sc(s_stA)

            def t_st(t, c=c):
                xbuf = xa if c % 2 == 0 else xb
                if c == 0:
                    C.wait(t, "vE", vE_x1[DC - 1])
                C.wait(t, stld[c][0], stld[c][1])
                t.matmul(PS[4][0:1, :], ones_fc[:, :], xbuf[:, 0:512], start=(c == 0), stop=(c == DC - 1))
                t.matmul(PS[5][0:1, :], ones_fc[:, :], xbuf[:, 512:1024], start=(c == 0), stop=(c == DC - 1))
                C.wait(t, "sE", sE_st[c])
                t.matmul(PS[0][0:1, :], ones_fc[:, :], qx[:, 0:512], start=(c == 0), stop=(c == DC - 1))
                i = t.matmul(PS[1][0:1, :], ones_fc[:, :], qx[:, 512:1024], start=(c == 0), stop=(c == DC - 1))
                tE_st[c] = C.inc(i, "tE", 1)
            T(t_st)

            def s_stB(s, c=c):
                if c + 2 < DC:
                    C.wait(s, "tE", tE_st[c])
                    C.wait(s, "dW", dS_x1w[c + 2])
                    sem = "dQ" if c % 2 == 0 else "dQ2"
                    i = s.dma_start(out=(xa if c % 2 == 0 else xb), in_=x1_scr[_ts(c + 2, 128), :])
                    stld[c + 2] = (sem, C.inc(i, sem, 16))
            Sc(s_stB)

        emit_rowchain([(PS[4], slice(0, 512)), (PS[5], slice(0, 512))],
                      [(PS[0], slice(0, 512)), (PS[1], slice(0, 512))], 1024)
        emit_bcast_mi()
        tE_l2bc = C.val("tE")

        l2ld = {}
        tE_pb = C.val("tE")

        @Sc
        def _(s):
            C.wait(s, "tE", tE_pb)
            for c in range(2):
                sem = "dQ" if c % 2 == 0 else "dQ2"
                i = s.dma_start(out=(xa if c % 2 == 0 else xb), in_=x1_scr[_ts(c, 128), :])
                l2ld[c] = (sem, C.inc(i, sem, 16))

        vE_l2 = {}
        for c in range(DC):
            def v_l2(v, c=c):
                xbuf = xa if c % 2 == 0 else xb
                C.wait(v, "tE", tE_l2bc)
                C.wait(v, l2ld[c][0], l2ld[c][1])
                i = None
                for hf in range(2):
                    v.tensor_tensor(fscr[:, _ts(hf, 512)], xbuf[:, _ts(hf, 512)], PS[hf][:, :], ALU.subtract)
                    i = v.tensor_tensor(ln2xt[:, c, _ts(hf, 512)], fscr[:, _ts(hf, 512)], PS[2 + hf][:, :], ALU.mult)
                vE_l2[c] = C.inc(i, "vE", 1)
            V(v_l2)

            def s_l2(s, c=c):
                if c + 2 < DC:
                    C.wait(s, "vE", vE_l2[c])
                    sem = "dQ" if c % 2 == 0 else "dQ2"
                    i = s.dma_start(out=(xa if c % 2 == 0 else xb), in_=x1_scr[_ts(c + 2, 128), :])
                    l2ld[c + 2] = (sem, C.inc(i, sem, 16))
            Sc(s_l2)
        vE_ln2 = C.val("vE")

        HCP = [(2 * i, 2) for i in range(21)] + [(42, 1)]
        out_store = [0]
        vE_out_prev = [0]
        gt_tap_gate = [0]
        gt_done = {}
        sE_sil = {}
        h_mm = {}
        w2_done = {}
        fstg_evac = {}

        def megabarrier():
            for eng, name in ((T, "tensor"),):
                def bar_t(t):
                    C.wait(t, "vE", C.val("vE"))
                    C.wait(t, "sE", C.val("sE"))
                    C.wait(t, "dS", C.val("dS"))
                    C.wait(t, "dQ", C.val("dQ"))
                    C.wait(t, "dG", C.val("dG"))
                    i = t.matmul(PS[5][0:1, 0:1], ones_fc[0:1, :], ones_fc[0:1, 0:1], start=True, stop=True)
                    C.inc(i, "tE", 1)
                T(bar_t)
            def bar_v(v):
                C.wait(v, "tE", C.val("tE"))
                i = v.tensor_copy(st1[:, 15:16], st1[:, 14:15])
                C.inc(i, "vE", 1)
            V(bar_v)
            def bar_s(s):
                C.wait(s, "vE", C.val("vE"))
                i = s.mul(st1[:, 15:16], st1[:, 15:16], 1.0)
                C.inc(i, "sE", 1)
            Sc(bar_s)
            def bar_sy(sy):
                C.wait(sy, "sE", C.val("sE"))
            Sy(bar_sy)
            def bar_g(g):
                C.wait(g, "sE", C.val("sE"))
            G(bar_g)

        MEGABAR = False
        for th in range(2):
            tsl = _ts(th, 512)
            if MEGABAR:
                megabarrier()
            dS_fb = {}
            for pi, (hc0, nn) in enumerate(HCP):
                def sy_fb(sy, pi=pi, hc0=hc0, nn=nn, th=th):
                    C.wait(sy, "cC", cC_t[1])
                    if th == 0 and pi < 2:
                        C.wait(sy, "vE", vE_ln2)
                    if pi >= 2:
                        C.wait(sy, "tE", h_mm[(th, pi - 2, 1)])
                    elif th == 1 and pi < 2:
                        C.wait(sy, "tE", h_mm[(0, len(HCP) - 2 + pi, 1)])
                        C.wait(sy, "tE", w2_done[(0, 3)])
                    sem = "dS" if pi % 2 == 0 else "dS2"
                    for m, tsrc in ((0, t1_out), (1, t3_out)):
                        src = tsrc[:, hc0 * 128:(hc0 + nn) * 128].rearrange("(c p) h -> p c h", p=128)
                        for q in range(4):
                            i = sy.dma_start(out=fbuf[:, m * 2 + pi % 2, _ts(q, 4), 0:nn * 128],
                                             in_=src[:, _ts(q, 4), :])
                            C.inc(i, sem, 16)
                    dS_fb[pi] = (sem, C.val(sem))
                Sy(sy_fb)

                for li in range(nn):
                    hc = hc0 + li

                    def t_h(t, pi=pi, li=li, hc=hc, th=th):
                        C.wait(t, dS_fb[pi][0], dS_fb[pi][1])
                        if th == 1 and hc < 2 and gt_tap_gate[0]:
                            C.wait(t, "dG", gt_tap_gate[0])
                        if th == 0 and hc < 2:
                            C.wait(t, "vE", vE_ln2)
                        if th == 1 and hc < 2 and (0, 3) in fstg_evac:
                            C.wait(t, "vE", fstg_evac[(0, 3)])
                        if hc >= 2:
                            C.wait(t, "vE", gt_done[(th, hc - 2)])
                        elif th == 1:
                            C.wait(t, "vE", gt_done[(0, NHC - 2 + hc)])
                        i1 = None
                        i3 = None
                        for cc in range(DC):
                            i1 = t.matmul(PS[hc % 2][:, :], fbuf[:, 0 + pi % 2, cc, _ts(li, 128)],
                                          ln2xt[:, cc, _ts(th, 512)], start=(cc == 0), stop=(cc == DC - 1))
                        h_mm[(th, pi, 0)] = C.inc(i1, "tE", 1)
                        for cc in range(DC):
                            i3 = t.matmul(PS[2 + hc % 2][:, :], fbuf[:, 2 + pi % 2, cc, _ts(li, 128)],
                                          ln2xt[:, cc, _ts(th, 512)], start=(cc == 0), stop=(cc == DC - 1))
                        h_mm[(th, pi, 1)] = C.inc(i3, "tE", 1)
                    T(t_h)

                    def s_sil(s, pi=pi, hc=hc, th=th):
                        C.wait(s, "tE", h_mm[(th, pi, 1)])
                        if hc >= 1:
                            C.wait(s, "vE", gt_done[(th, hc - 1)])
                        elif th == 1:
                            C.wait(s, "vE", gt_done[(0, NHC - 1)])
                        i = s.activation(sil_s[:, :], PS[hc % 2][:, :], AF.Silu)
                        sE_sil[(th, hc)] = C.inc(i, "sE", 1)
                    Sc(s_sil)

                    def v_g(v, hc=hc, th=th):
                        C.wait(v, "sE", sE_sil[(th, hc)])
                        C.wait(v, "tE", C.val("tE"))
                        i = v.tensor_tensor(gt_v[:, hc, :], sil_s[:, :], PS[2 + hc % 2][:, :], ALU.mult)
                        gt_done[(th, hc)] = C.inc(i, "vE", 1)
                    V(v_g)

            if th == 0 and "gt" in taps:
                tpg = tapt("gt", [128, NHC, 512], BF16)

                @G
                def _(g):
                    C.wait(g, "vE", gt_done[(0, NHC - 1)])
                    i = g.dma_start(out=tpg[:, :, :], in_=gt_v[:, :, :])
                    gt_tap_gate[0] = C.inc(i, "dG", 16)
            if th == 0 and "fb" in taps:
                tpf = tapt("fb", [128, 4, 16, 256], BF16)

                @G
                def _(g):
                    C.wait(g, "vE", gt_done[(0, NHC - 1)])
                    i = g.dma_start(out=tpf[:, :, :, :], in_=fbuf[:, :, :, :])
                    gt_tap_gate[0] = C.inc(i, "dG", 16)

            if MEGABAR:
                megabarrier()
            dQ_sl = {}
            for dcg in range(4):
                def s_slab(s, dcg=dcg, th=th):
                    C.wait(s, "cC", cC_t[2])
                    if dcg == 0:
                        if th == 0:
                            C.wait(s, "vE", vE_ln2)
                        C.wait(s, "tE", h_mm[(th, len(HCP) - 1, 1)])
                    if (th, dcg - 1) in w2_done:
                        C.wait(s, "tE", w2_done[(th, dcg - 1)])
                    elif th == 1 and dcg == 0:
                        C.wait(s, "tE", w2_done[(0, 3)])
                    src = t2_out[:, _ts(dcg, 512)]
                    sem = "dQ" if dcg % 2 == 0 else "dQ2"
                    for q in range(5):
                        h0 = q * 9
                        h1 = min(NHC, h0 + 9)
                        if h0 >= NHC:
                            break
                        i = s.dma_start(out=slab[:, h0:h1, :],
                                        in_=src[h0 * 128:h1 * 128, :].rearrange("(h p) d -> p h d", p=128))
                        C.inc(i, sem, 16)
                    dQ_sl[dcg] = (sem, C.val(sem))
                Sc(s_slab)

                def t_w2(t, dcg=dcg, th=th):
                    C.wait(t, dQ_sl[dcg][0], dQ_sl[dcg][1])
                    C.wait(t, "vE", gt_done[(th, NHC - 1)])
                    if th == 1 and dcg == 0 and vE_out_prev[0]:
                        C.wait(t, "vE", vE_out_prev[0])
                    if (th, dcg - 1) in fstg_evac:
                        C.wait(t, "vE", fstg_evac[(th, dcg - 1)])
                    elif th == 1 and dcg == 0:
                        C.wait(t, "vE", fstg_evac[(0, 3)])
                    i = None
                    for hc in range(NHC):
                        for dcl in range(4):
                            i = t.matmul(PS[dcl][:, :], slab[:, hc, _ts(dcl, 128)],
                                         gt_v[:, hc, :], start=(hc == 0), stop=(hc == NHC - 1))
                    w2_done[(th, dcg)] = C.inc(i, "tE", 1)
                T(t_w2)

                def v_w2(v, dcg=dcg, th=th):
                    C.wait(v, "tE", w2_done[(th, dcg)])
                    i = None
                    for dcl in range(4):
                        i = v.tensor_copy(fstg[:, dcg * 4 + dcl, :], PS[dcl][:, :])
                    fstg_evac[(th, dcg)] = C.inc(i, "vE", 1)
                V(v_w2)

            tE_fsq = {}
            for c in range(DC):
                def s_fsq(s, c=c, th=th):
                    if th == 1 and c == 0 and vE_out_prev[0]:
                        C.wait(s, "vE", vE_out_prev[0])
                    C.wait(s, "vE", fstg_evac[(th, c // 4)])
                    if c >= 1:
                        C.wait(s, "tE", tE_fsq[c - 1])
                    i = s.activation(fsq_v[:, :], fstg[:, c, :], AF.Square)
                    C.inc(i, "sE", 1)
                Sc(s_fsq)

                def t_fsq(t, c=c, th=th):
                    if th == 1 and c == 0 and vE_out_prev[0]:
                        C.wait(t, "vE", vE_out_prev[0])
                    C.wait(t, "vE", fstg_evac[(th, c // 4)])
                    t.matmul(PS[4][0:1, :], ones_bc[:, :], fstg[:, c, :], start=(c == 0), stop=(c == DC - 1))
                    C.wait(t, "sE", C.val("sE"))
                    i = t.matmul(PS[5][0:1, :], ones_bc[:, :], fsq_v[:, :], start=(c == 0), stop=(c == DC - 1))
                    tE_fsq[c] = C.inc(i, "tE", 1)
                T(t_fsq)

            emit_rowchain([(PS[4], slice(0, 512))], [(PS[5], slice(0, 512))], 512)

            def t_fbc(t):
                C.wait(t, "sE", C.val("sE"))
                t.matmul(PS[4][:, :], ones_fr[:, :], r_m[0:1, 0:512], start=True, stop=True)
                i = t.matmul(PS[5][:, :], ones_fr[:, :], r_istd[0:1, 0:512], start=True, stop=True)
                C.inc(i, "tE", 1)
            T(t_fbc)
            tE_fbc = C.val("tE")

            x1ld = {}

            @Sc
            def _(s, th=th):
                if th == 1 and vE_out_prev[0]:
                    C.wait(s, "vE", vE_out_prev[0])
                C.wait(s, "tE", w2_done[(th, 3)])
                for c in range(2):
                    sem = "dQ" if c % 2 == 0 else "dQ2"
                    i = s.dma_start(out=x1pb[c % 2], in_=x1_scr[_ts(c, 128), _ts(th, 512)])
                    x1ld[c] = (sem, C.inc(i, sem, 16))

            vE_o = {}
            dS_o = {}
            for c in range(DC):
                def v_out(v, c=c, th=th):
                    C.wait(v, "tE", tE_fbc)
                    C.wait(v, x1ld[c][0], x1ld[c][1])
                    if c >= 1:
                        C.wait(v, "dW", dS_o[c - 1])
                    v.tensor_tensor(fscr[:, 0:512], fstg[:, c, :], PS[4][:, :], ALU.subtract)
                    v.tensor_tensor(ftmp[:, 0:512], fscr[:, 0:512], PS[5][:, :], ALU.mult)
                    i = v.tensor_tensor(fscr[:, 0:512], ftmp[:, 0:512], x1pb[c % 2], ALU.add)
                    vE_o[c] = C.inc(i, "vE", 1)
                V(v_out)

                def sy_out(sy, c=c, th=th):
                    C.wait(sy, "vE", vE_o[c])
                    i = sy.dma_start(out=outT[_ts(c, 128), _ts(th, 512)], in_=fscr[:, 0:512])
                    dS_o[c] = C.inc(i, "dW", 16)
                    out_store[0] = dS_o[c]
                Sy(sy_out)

                def s_xr(s, c=c, th=th):
                    if c + 2 < DC:
                        C.wait(s, "vE", vE_o[c])
                        sem = "dQ" if c % 2 == 0 else "dQ2"
                        i = s.dma_start(out=x1pb[c % 2], in_=x1_scr[_ts(c + 2, 128), _ts(th, 512)])
                        x1ld[c + 2] = (sem, C.inc(i, sem, 16))
                Sc(s_xr)

            vE_out_prev[0] = vE_o[DC - 1]

        if "lnx" in taps:
            tp = tapt("lnx", [4 * D, TC], BF16)

            @G
            def _(g):
                C.wait(g, "cC", cC_lnx)
                i = g.dma_start(out=tp[:, :], in_=lnx_out.ap())
                C.inc(i, "dG", 16)
        if "t1" in taps:
            tp = tapt("t1", [D, HIDP], BF16)

            @G
            def _(g):
                C.wait(g, "cC", cC_t[0])
                i = g.dma_start(out=tp[:, :], in_=t1_out.ap())
                C.inc(i, "dG", 16)
        if "t3" in taps:
            tp = tapt("t3", [D, HIDP], BF16)

            @G
            def _(g):
                C.wait(g, "cC", cC_t[1])
                i = g.dma_start(out=tp[:, :], in_=t3_out.ap())
                C.inc(i, "dG", 16)
        if "t3in" in taps:
            tp = tapt("t3in", [D_SH, HIDP], BF16)

            @G
            def _(g):
                C.wait(g, "cC", cC_t[1])
                i = g.dma_start(out=tp[:, :], in_=t3_in.ap())
                C.inc(i, "dG", 16)
        if "t2" in taps:
            tp = tapt("t2", [HIDP, D], BF16)

            @G
            def _(g):
                C.wait(g, "cC", cC_t[2])
                i = g.dma_start(out=tp[:, :], in_=t2_out.ap())
                C.inc(i, "dG", 16)
        if "ln2" in taps:
            tp = tapt("ln2", [128, 16, TC], BF16)

            @G
            def _(g):
                C.wait(g, "vE", vE_ln2)
                i = g.dma_start(out=tp[:, :, :], in_=ln2xt[:, :, :])
                C.inc(i, "dG", 16)
        if "z" in taps:
            tp = tapt("z", [128, HL, S], BF16)

            @G
            def _(g):
                C.wait(g, "vE", vE_zall)
                i = g.dma_start(out=tp[:, :, :], in_=z_v[:, :, :])
                C.inc(i, "dG", 16)
        if "rs" in taps:
            tp = tapt("rs", [D, TC], BF16)

            @G
            def _(g):
                C.wait(g, "cC", cC_rs)
                i = g.dma_start(out=tp[:, :], in_=rs_out.ap())
                C.inc(i, "dG", 16)
        if "x1" in taps:
            tp = tapt("x1", [D, TC], F32)

            @G
            def _(g):
                C.wait(g, "dW", dS_x1w[DC - 1])
                i = g.dma_start(out=tp[:, :], in_=x1_scr.ap())
                C.inc(i, "dG", 16)
        if "qkv" in taps:
            tpq = tapt("qkv", [128, 3, 2, S], BF16)

            @G
            def _(g):
                C.wait(g, "vE", evac_done[47])
                i = g.dma_start(out=tpq[:, 0, :, :], in_=qt[:, :, :])
                C.inc(i, "dG", 16)
                i = g.dma_start(out=tpq[:, 1, :, :], in_=kt[:, :, :])
                C.inc(i, "dG", 16)
                i = g.dma_start(out=tpq[:, 2, :, :],
                                in_=vt.rearrange("p h k e -> p h (k e)"))
                C.inc(i, "dG", 16)

        if "t1b" in taps:
            tp = tapt("t1b", [D, HIDP], BF16)

            @G
            def _(g):
                C.wait(g, "dW", out_store[0])
                i = g.dma_start(out=tp[:, :], in_=t1_out.ap())
                C.inc(i, "dG", 16)
        if "ln2b" in taps:
            tp = tapt("ln2b", [128, 16, TC], BF16)

            @G
            def _(g):
                C.wait(g, "dW", out_store[0])
                i = g.dma_start(out=tp[:, :, :], in_=ln2xt[:, :, :])
                C.inc(i, "dG", 16)

        @Sy
        def _(sy):
            for nm in ("dS", "dS2", "dW", "dQ", "dQ2", "dG", "dG2", "dG3"):
                C.wait(sy, nm, C.val(nm))

    return tap_t



def _mk_masks():
    kl = np.arange(128)[:, None]
    ql = np.arange(512)[None, :]
    return np.stack([(128 * j + kl <= ql) for j in range(4)]).astype(ml_dtypes.bfloat16)


def make_core_inputs(inp):
    x = np.asarray(inp["L0_attn_x"], dtype=np.float32).reshape(B * S, D)
    wq = np.asarray(inp["L0_attn_W_Q"], dtype=np.float32)
    wk = np.asarray(inp["L0_attn_W_K"], dtype=np.float32)
    wv = np.asarray(inp["L0_attn_W_V"], dtype=np.float32)
    wo = np.asarray(inp["L0_attn_W_O"], dtype=np.float32)
    w1 = np.asarray(inp["L0_ffn_w1"], dtype=np.float32)
    w3 = np.asarray(inp["L0_ffn_w3"], dtype=np.float32)
    w2 = np.asarray(inp["L0_ffn_w2"], dtype=np.float32)

    w1p = np.zeros((HIDP, D), np.float32); w1p[:HID] = w1
    w3p = np.zeros((HIDP, D), np.float32); w3p[:HID] = w3
    w2p = np.zeros((D, HIDP), np.float32); w2p[:, :HID] = w2
    w1Tf = np.ascontiguousarray(w1p.T)
    w3Tf = np.ascontiguousarray(w3p.T)
    w2Tf = np.ascontiguousarray(w2p.T)
    masks = _mk_masks()
    ident = np.eye(128, dtype=ml_dtypes.bfloat16)

    maps = []
    for c in range(NCORE):
        h0 = 4 * (c % 4)
        wqkv_host = np.empty((128, 3, HL, DC, 128), np.float32)
        for m, w in enumerate((wq, wk, wv)):
            for h in range(HL):
                wqkv_host[:, m, h] = w[h0 + h].T.reshape(DC, 128, 128).transpose(1, 0, 2)
        wo_host = np.empty((128, HL, DC, 128), np.float32)
        for h in range(HL):
            wo_host[:, h] = wo[h0 + h].T.reshape(128, DC, 128)
        maps.append({
            "xT": np.ascontiguousarray(x[c * TC:(c + 1) * TC, :].T),
            "wqkv_in": wqkv_host,
            "wo_in": wo_host,
            "w1T": np.ascontiguousarray(w1Tf[c * D_SH:(c + 1) * D_SH, :]),
            "w3T": np.ascontiguousarray(w3Tf[c * D_SH:(c + 1) * D_SH, :]),
            "w2T": np.ascontiguousarray(w2Tf[c * HID_SH:(c + 1) * HID_SH, :]),
            "masks": masks,
            "ident": ident,
        })
    return maps


def kernel(**inputs):
    nc = bass.Bass("TRN2", target_bir_lowering=False)
    build_program(nc)
    maps = make_core_inputs(inputs)
    res = run_bass_kernel_spmd(nc, maps, core_ids=list(range(NCORE)))
    parts = [np.ascontiguousarray(res.results[c]["outT"].T) for c in range(NCORE)]
    return np.concatenate(parts, axis=0).reshape(B, S, D).astype(np.float32)


# version 07
# speedup vs baseline: 1.0502x; 1.0306x over previous
import math
import numpy as np
import ml_dtypes
from contextlib import ExitStack

import concourse.bass as bass
import concourse.mybir as mybir
from concourse.bass_utils import run_bass_kernel_spmd

F32 = mybir.dt.float32
BF16 = mybir.dt.bfloat16
AF = mybir.ActivationFunctionType
ALU = mybir.AluOpType

NCORE = 8
B, S, D, H, E = 2, 4096, 2048, 16, 128
HL = 4
TC = 1024
DC = D // 128
HID = 5461
HIDP = 5504
NHC = HIDP // 128
HID_SH = HIDP // NCORE
D_SH = D // NCORE
NQ = S // 512
QSCL = float(1.0 / math.sqrt(128.0))
W_N = float(HID * D)
LN_EPS = 1e-5

PB_SZ = 70656
OFF_QT = 0
OFF_KT = 8192
OFF_VT = 16384
OFF_WQKV = 24576
OFF_LNCH = 36864
OFF_Z = 53248
OFF_LN2XT = 0
OFF_ATTN = 16384
OFF_WO = 32768
OFF_P2STG = 40960
OFF_SQB = 45056
OFF_GT = 16384
OFF_FBUF = 38400
OFF_SLAB = 38400
OFF_FSTG = 60416
OFF_SIL2 = 68608

PF_SZ = 6656
PFO_XA, PFO_XB = 0, 1024
PFO_SCR = 2048
PFO_TMP = 3072
PFO_QW = 4096
PFO_X1A, PFO_X1B = 5472, 5984


def _ts(i, n):
    return slice(i * n, (i + 1) * n)


class Ctx:
    def __init__(self):
        self.h = {}
        self.n = {}

    def add(self, name, handle):
        self.h[name] = handle
        self.n[name] = 0

    def inc(self, instr, name, k=1):
        instr.then_inc(self.h[name], k)
        self.n[name] += k
        return self.n[name]

    def val(self, name):
        return self.n[name]

    def wait(self, eng, name, v=None):
        vv = self.n[name] if v is None else v
        if vv > 0:
            eng.wait_ge(self.h[name], vv)


def build_program(nc, taps=()):
    taps = set(taps)
    tap_t = {}

    xT = nc.dram_tensor("xT", [D, TC], F32, kind="ExternalInput").ap()
    wqkv_in = nc.dram_tensor("wqkv_in", [128, 3, HL, DC, 128], F32, kind="ExternalInput").ap()
    wo_in = nc.dram_tensor("wo_in", [128, HL, DC, 128], F32, kind="ExternalInput").ap()
    w1T = nc.dram_tensor("w1T", [D_SH, HIDP], F32, kind="ExternalInput").ap()
    w3T = nc.dram_tensor("w3T", [D_SH, HIDP], F32, kind="ExternalInput").ap()
    w2T = nc.dram_tensor("w2T", [HID_SH, D], F32, kind="ExternalInput").ap()
    masks = nc.dram_tensor("masks", [4, 128, 512], BF16, kind="ExternalInput").ap()
    ident = nc.dram_tensor("ident", [128, 128], BF16, kind="ExternalInput").ap()
    outT = nc.dram_tensor("outT", [D, TC], F32, kind="ExternalOutput").ap()

    lnx_in4 = [nc.dram_tensor("lnx_in%d" % q, [D, 256], BF16) for q in range(4)]
    lnx_out4 = [nc.dram_tensor("lnx_out%d" % q, [4 * D, 256], BF16) for q in range(4)]
    ar_in = nc.dram_tensor("ar_in", [1, 8], F32)
    ar_out = nc.dram_tensor("ar_out", [1, 8], F32)
    t1_in = nc.dram_tensor("t1_in", [D_SH, HIDP], BF16)
    t1_out = nc.dram_tensor("t1_out", [D, HIDP], BF16, addr_space="Shared")
    t3_in = nc.dram_tensor("t3_in", [D_SH, HIDP], BF16)
    t3_out = nc.dram_tensor("t3_out", [D, HIDP], BF16, addr_space="Shared")
    t2_in = nc.dram_tensor("t2_in", [HID_SH, D], BF16)
    t2_out = nc.dram_tensor("t2_out", [HIDP, D], BF16, addr_space="Shared")
    rs_in8 = [nc.dram_tensor("rs_in%d" % k, [4, 256, TC], BF16) for k in range(8)]
    rs_out8 = [nc.dram_tensor("rs_out%d" % k, [256, TC], BF16) for k in range(8)]
    x1_scr = nc.dram_tensor("x1_scr", [D, TC], F32)
    warm_in = nc.dram_tensor("warm_in", [1, 16], F32)
    warm_out = nc.dram_tensor("warm_out", [4, 16], F32)
    warm_out8 = nc.dram_tensor("warm_out8", [8, 16], F32, addr_space="Shared")

    def tapt(name, shape, dtype):
        if name in taps and name not in tap_t:
            tap_t[name] = nc.dram_tensor("tap_" + name, shape, dtype, kind="ExternalOutput").ap()
        return tap_t.get(name)

    grp4 = [[0, 1, 2, 3], [4, 5, 6, 7]]
    grp8 = [list(range(8))]

    with ExitStack() as ectx:
        pb = ectx.enter_context(nc.sbuf_tensor("pb", [128, PB_SZ], BF16))
        pf = ectx.enter_context(nc.sbuf_tensor("pf", [128, PF_SZ], F32))
        ones_fc = ectx.enter_context(nc.sbuf_tensor("ones_fc", [128, 1], F32))
        ones_fr = ectx.enter_context(nc.sbuf_tensor("ones_fr", [1, 128], F32))
        ones_bc = ectx.enter_context(nc.sbuf_tensor("ones_bc", [128, 1], BF16))
        mask_s = ectx.enter_context(nc.sbuf_tensor("mask_s", [128, 4, 512], BF16))
        id_s = ectx.enter_context(nc.sbuf_tensor("id_s", [128, 128], BF16))
        rows = ectx.enter_context(nc.sbuf_tensor("rows", [1, 4608], F32))
        st1 = ectx.enter_context(nc.sbuf_tensor("st1", [128, 16], F32))
        exp_s = ectx.enter_context(nc.sbuf_tensor("exp_s", [128, 2, 512], BF16))
        expm_s = ectx.enter_context(nc.sbuf_tensor("expm_s", [128, 2, 512], BF16))
        vtmp_s = ectx.enter_context(nc.sbuf_tensor("vtmp_s", [128, 512], BF16))
        inv_s = ectx.enter_context(nc.sbuf_tensor("inv_s", [1, 512], F32))
        sil_s = ectx.enter_context(nc.sbuf_tensor("sil_s", [128, 512], BF16))
        qsg_s = ectx.enter_context(nc.sbuf_tensor("qsg_s", [128, 3, 688], BF16))
        qts_s = ectx.enter_context(nc.sbuf_tensor("qts_s", [128, 2, 688], BF16))

        PS = [ectx.enter_context(nc.psum_tensor("ps%d" % i, [128, 512], F32)) for i in range(6)]
        PT = [ectx.enter_context(nc.psum_tensor("pt%d" % i, [128, 1024], BF16)) for i in range(2)]

        C = Ctx()
        for nm in ("dS", "dS2", "dW", "dQ", "dQ2", "dG", "dG2", "dG3", "dR", "dR2", "tE", "vE", "sE", "cC"):
            C.add(nm, ectx.enter_context(nc.semaphore(nm)))
        block = ectx.enter_context(nc.Block())

        lnxown = pb[:, 0:16384].rearrange("p (c t) -> p c t", c=16)
        qt = pb[:, OFF_QT:OFF_QT + 8192].rearrange("p (h t) -> p h t", h=2)
        kt = pb[:, OFF_KT:OFF_KT + 8192].rearrange("p (h t) -> p h t", h=2)
        vt = pb[:, OFF_VT:OFF_VT + 8192].rearrange("p (h k e) -> p h k e", h=2, k=32)
        wqkv = pb[:, OFF_WQKV:OFF_WQKV + 12288].rearrange("p (m h c e) -> p m h c e", m=3, h=2, c=DC)
        lnch = pb[:, OFF_LNCH:OFF_LNCH + 16384].rearrange("p (b c t) -> p b c t", b=2, c=16)
        z_v = pb[:, OFF_Z:OFF_Z + 16384].rearrange("p (h t) -> p h t", h=HL)
        ln2xt = pb[:, OFF_LN2XT:OFF_LN2XT + 16384].rearrange("p (c t) -> p c t", c=16)
        attn_t = pb[:, OFF_ATTN:OFF_ATTN + 16384].rearrange("p (c t) -> p c t", c=16)
        wo_v = pb[:, OFF_WO:OFF_WO + 8192].rearrange("p (h c e) -> p h c e", h=HL, c=DC)
        p2stg = pb[:, OFF_P2STG:OFF_P2STG + 4096].rearrange("p (t c) -> p t c", t=8)
        sqb = pb[:, OFF_SQB:OFF_SQB + 1024]
        gt_v = pb[:, OFF_GT:OFF_GT + NHC * 512].rearrange("p (h t) -> p h t", h=NHC)
        fbuf = pb[:, OFF_FBUF:OFF_FBUF + 16384].rearrange("p (b c h) -> p b c h", b=4, c=16)
        slab = pb[:, OFF_SLAB:OFF_SLAB + NHC * 512].rearrange("p (h d) -> p h d", h=NHC)
        fstg = pb[:, OFF_FSTG:OFF_FSTG + 8192].rearrange("p (c t) -> p c t", c=16)
        fsq_v = pb[:, OFF_SIL2:OFF_SIL2 + 512]

        xa = pf[:, PFO_XA:PFO_XA + 1024]
        xb = pf[:, PFO_XB:PFO_XB + 1024]
        fscr = pf[:, PFO_SCR:PFO_SCR + 1024]
        ftmp = pf[:, PFO_TMP:PFO_TMP + 1024]
        qw = [pf[:, PFO_QW:PFO_QW + 688], pf[:, PFO_QW + 688:PFO_QW + 1376]]
        qx = pf[:, PFO_QW:PFO_QW + 1024]
        x1pb = [pf[:, PFO_X1A:PFO_X1A + 512], pf[:, PFO_X1B:PFO_X1B + 512]]
        bcz_s = pf[:, PFO_X1A:PFO_X1A + 512]

        r_sum = rows[:, 0:1024]
        r_var = rows[:, 1024:2048]
        r_m = rows[:, 2048:3072]
        r_istd = rows[:, 3072:4096]
        r_misc = rows[:, 4096:4592]
        r_svec = rows[:, 4592:4600]
        r_sld = rows[:, 4600:4608]

        def T(f): block.tensor(f)
        def V(f): block.vector(f)
        def Sc(f): block.scalar(f)
        def Sy(f): block.sync(f)
        def G(f): block.gpsimd(f)

        dG_wqkv = {}

        @G
        def _(g):
            i = g.collective_compute("AllGather", ALU.bypass, replica_groups=grp4,
                                     ins=[warm_in.ap()], outs=[warm_out.ap()])
            C.inc(i, "cC", 1)
            i = g.collective_compute("AllGather", ALU.bypass, replica_groups=grp8,
                                     ins=[warm_in.ap()], outs=[warm_out8.ap()])
            C.inc(i, "cC", 1)
            g.memset(ones_fc[:, :], 1.0)
            g.memset(ones_fr[:, :], 1.0)
            g.memset(ones_bc[:, :], 1.0)
            g.memset(r_svec[:, :], 0.0)
            i = g.dma_start(out=mask_s[:, :, :], in_=masks.rearrange("j p t -> p j t"))
            C.inc(i, "dG", 16)
            i = g.dma_start(out=id_s[:, :], in_=ident[:, :])
            C.inc(i, "dG", 16)
            for m in range(3):
                i = g.dma_start(out=wqkv[:, m, :, :, :], in_=wqkv_in[:, m, 0:2, :, :])
                C.inc(i, "dG", 16)
            dG_wqkv[0] = C.val("dG")
        dG_init = C.val("dG")

        sE_sq = []
        tE_x = []

        for c in range(DC):
            def s_p0(s, c=c):
                C.wait(s, "dS" if c % 2 == 0 else "dS2", 16 * (c // 2 + 1))
                if c >= 1:
                    C.wait(s, "tE", tE_x[c - 1])
                i = s.activation(fscr[:, :], (xa if c % 2 == 0 else xb), AF.Square)
                sE_sq.append(C.inc(i, "sE", 1))
            Sc(s_p0)

            def t_p0(t, c=c):
                if c == 0:
                    C.wait(t, "dG", dG_init)
                xbuf = xa if c % 2 == 0 else xb
                C.wait(t, "dS" if c % 2 == 0 else "dS2", 16 * (c // 2 + 1))
                t.matmul(PS[4][0:1, :], ones_fc[:, :], xbuf[:, 0:512], start=(c == 0), stop=(c == DC - 1))
                t.matmul(PS[5][0:1, :], ones_fc[:, :], xbuf[:, 512:1024], start=(c == 0), stop=(c == DC - 1))
                C.wait(t, "sE", sE_sq[c])
                t.matmul(PS[0][0:1, :], ones_fc[:, :], fscr[:, 0:512], start=(c == 0), stop=(c == DC - 1))
                i = t.matmul(PS[1][0:1, :], ones_fc[:, :], fscr[:, 512:1024], start=(c == 0), stop=(c == DC - 1))
                tE_x.append(C.inc(i, "tE", 1))
            T(t_p0)

        @Sy
        def _(sy):
            for c in range(DC):
                if c >= 2:
                    C.wait(sy, "tE", tE_x[c - 2])
                i = sy.dma_start(out=(xa if c % 2 == 0 else xb), in_=xT[_ts(c, 128), :])
                C.inc(i, "dS" if c % 2 == 0 else "dS2", 16)

        def emit_rowchain(sum_srcs, sq_srcs, width):
            @V
            def _(v):
                C.wait(v, "tE", C.val("tE"))
                i = None
                for j, (bk, sl) in enumerate(sum_srcs):
                    i = v.tensor_copy(r_sum[:, _ts(j, 512)], bk[0:1, sl])
                for j, (bk, sl) in enumerate(sq_srcs):
                    i = v.tensor_copy(r_var[:, _ts(j, 512)], bk[0:1, sl])
                C.inc(i, "vE", 1)

            @Sc
            def _(s):
                C.wait(s, "vE", C.val("vE"))
                s.mul(r_m[:, 0:width], r_sum[:, 0:width], 1.0 / D)
                i = s.mul(r_istd[:, 0:width], r_var[:, 0:width], 1.0 / D)
                C.inc(i, "sE", 1)

            @V
            def _(v):
                C.wait(v, "sE", C.val("sE"))
                v.tensor_tensor(r_sum[:, 0:width], r_m[:, 0:width], r_m[:, 0:width], ALU.mult)
                v.tensor_tensor(r_var[:, 0:width], r_istd[:, 0:width], r_sum[:, 0:width], ALU.subtract)
                i = v.tensor_scalar_add(r_istd[:, 0:width], r_var[:, 0:width], LN_EPS)
                C.inc(i, "vE", 1)

            @Sc
            def _(s):
                C.wait(s, "vE", C.val("vE"))
                i = s.activation(r_sum[:, 0:width], r_istd[:, 0:width], AF.Ln)
                C.inc(i, "sE", 1)
                C.wait(s, "sE", C.val("sE"))
                i = s.activation(r_istd[:, 0:width], r_sum[:, 0:width], AF.Exp, scale=-0.5)
                C.inc(i, "sE", 1)

        emit_rowchain([(PS[4], slice(0, 512)), (PS[5], slice(0, 512))],
                      [(PS[0], slice(0, 512)), (PS[1], slice(0, 512))], 1024)

        def emit_bcast_mi():
            @T
            def _(t):
                C.wait(t, "sE", C.val("sE"))
                t.matmul(PS[0][:, :], ones_fr[:, :], r_m[0:1, 0:512], start=True, stop=True)
                t.matmul(PS[1][:, :], ones_fr[:, :], r_m[0:1, 512:1024], start=True, stop=True)
                t.matmul(PS[2][:, :], ones_fr[:, :], r_istd[0:1, 0:512], start=True, stop=True)
                i = t.matmul(PS[3][:, :], ones_fr[:, :], r_istd[0:1, 512:1024], start=True, stop=True)
                C.inc(i, "tE", 1)

        emit_bcast_mi()
        tE_ln1bc = C.val("tE")

        vE_app = []

        @V
        def _(v):
            C.wait(v, "tE", tE_ln1bc)
            for c in range(DC):
                xbuf = xa if c % 2 == 0 else xb
                C.wait(v, "dS" if c % 2 == 0 else "dS2", 16 * (8 + c // 2 + 1))
                i = None
                for hf in range(2):
                    v.tensor_tensor(ftmp[:, _ts(hf, 512)], xbuf[:, _ts(hf, 512)], PS[hf][:, :], ALU.subtract)
                    i = v.tensor_tensor(lnxown[:, c, _ts(hf, 512)], ftmp[:, _ts(hf, 512)], PS[2 + hf][:, :], ALU.mult)
                vE_app.append(C.inc(i, "vE", 1))

        @Sy
        def _(sy):
            for c in range(DC):
                if c >= 2:
                    C.wait(sy, "vE", vE_app[c - 2])
                else:
                    C.wait(sy, "tE", tE_x[DC - 2 + c])
                i = sy.dma_start(out=(xa if c % 2 == 0 else xb), in_=xT[_ts(c, 128), :])
                C.inc(i, "dS" if c % 2 == 0 else "dS2", 16)

        cC_lnxq = {}

        @G
        def _(g):
            C.wait(g, "vE", vE_app[-1])
            for q in range(4):
                i = g.dma_start(out=lnx_in4[q].rearrange("(c p) t -> p c t", p=128),
                                in_=lnxown[:, :, _ts(q, 256)])
                C.inc(i, "dG", 16)
                C.wait(g, "dG", C.val("dG"))
                i = g.collective_compute("AllGather", ALU.bypass, replica_groups=grp4,
                                         ins=[lnx_in4[q].ap()], outs=[lnx_out4[q].ap()])
                cC_lnxq[q] = C.inc(i, "cC", 1)

        qchunks = []
        for mi, wsrc in ((0, w1T), (1, w3T), (2, w2T)):
            if mi < 2:
                for r in range(2):
                    for cc in range(8):
                        qchunks.append((wsrc, r * 128, 128, cc * 688, 688, mi))
            else:
                for r in range(6):
                    nr = 128 if r < 5 else HID_SH - 640
                    for (c0, ncol) in ((0, 688), (688, 688), (1376, 672)):
                        qchunks.append((wsrc, r * 128, nr, c0, ncol, mi))
        NQCH = len(qchunks)
        dQ_q1p = [C.val("dQ"), C.val("dQ2")]

        QEMIT = []
        QA_END = [0]
        vE_qred = []
        qacc_par = {0: 0, 1: 0, 2: 0}
        _first = {0: True, 1: True, 2: True}
        _par = {0: 0, 1: 0, 2: 0}
        sE_qs = {}
        vE_qt = {}
        cC_t = {}
        dG23 = {}

        for j, ch in enumerate(qchunks):
            def q_ld1(sc, j=j, ch=ch):
                wsrc, r0, nr, c0, ncol, mi = ch
                if j >= 2:
                    C.wait(sc, "vE", vE_qred[j - 2])
                i = sc.dma_start(out=qw[j % 2][0:nr, 0:ncol], in_=wsrc[r0:r0 + nr, c0:c0 + ncol])
                C.inc(i, "dQ" if j % 2 == 0 else "dQ2", 16)
            QEMIT.append((Sc, q_ld1))

            def q_red(v, j=j, ch=ch):
                wsrc, r0, nr, c0, ncol, mi = ch
                C.wait(v, "dQ" if j % 2 == 0 else "dQ2", dQ_q1p[j % 2] + 16 * (j // 2 + 1))
                buf = qw[j % 2]
                if _first[mi]:
                    i = v.reduce_sum(st1[:, mi * 2:mi * 2 + 1], buf[0:nr, 0:ncol],
                                     axis=mybir.AxisListType.X, apply_absolute_value=True)
                    _first[mi] = False
                else:
                    v.reduce_sum(st1[0:nr, 6:7], buf[0:nr, 0:ncol],
                                 axis=mybir.AxisListType.X, apply_absolute_value=True)
                    o = _par[mi]
                    _par[mi] ^= 1
                    n2 = _par[mi]
                    v.tensor_copy(st1[:, mi * 2 + n2:mi * 2 + n2 + 1], st1[:, mi * 2 + o:mi * 2 + o + 1])
                    i = v.tensor_tensor(st1[0:nr, mi * 2 + n2:mi * 2 + n2 + 1],
                                        st1[0:nr, mi * 2 + o:mi * 2 + o + 1],
                                        st1[0:nr, 6:7], ALU.add)
                vE_qred.append(C.inc(i, "vE", 1))
                qacc_par[mi] = _par[mi]
            QEMIT.append((V, q_red))

        def q_sum(t):
            C.wait(t, "vE", vE_qred[-1])
            i = None
            for mi in range(3):
                acc = st1[:, mi * 2 + qacc_par[mi]:mi * 2 + qacc_par[mi] + 1]
                i = t.matmul(PS[4][0:1, mi:mi + 1], ones_fc[:, :], acc, start=True, stop=True)
            C.inc(i, "tE", 1)
        QEMIT.append((T, q_sum))

        def q_cp(v):
            C.wait(v, "tE", C.val("tE"))
            i = v.tensor_copy(r_svec[:, 0:3], PS[4][0:1, 0:3])
            C.inc(i, "vE", 1)
        QEMIT.append((V, q_cp))

        def q_ar(g):
            C.wait(g, "vE", C.val("vE"))
            i = g.dma_start(out=ar_in[:, :], in_=r_svec[:, :])
            C.inc(i, "dG", 16)
            C.wait(g, "dG", C.val("dG"))
            i = g.collective_compute("AllReduce", ALU.add, replica_groups=grp8,
                                     ins=[ar_in.ap()], outs=[ar_out.ap()])
            C.inc(i, "cC", 1)
            C.wait(g, "cC", C.val("cC"))
            i = g.dma_start(out=r_sld[:, :], in_=ar_out[:, :])
            C.inc(i, "dG", 16)
        QEMIT.append((G, q_ar))

        def q_smul(sc):
            C.wait(sc, "dG", C.val("dG"))
            i = sc.mul(r_misc[:, 0:3], r_sld[:, 0:3], 0.5 / W_N)
            C.inc(i, "sE", 1)
        QEMIT.append((Sc, q_smul))

        def q_bc(t):
            C.wait(t, "sE", C.val("sE"))
            i = t.matmul(PS[5][:, 0:3], ones_fr[:, :], r_misc[0:1, 0:3], start=True, stop=True)
            C.inc(i, "tE", 1)
        QEMIT.append((T, q_bc))

        vE_sh_h = [0]

        def q_sh(v):
            C.wait(v, "tE", C.val("tE"))
            v.tensor_copy(st1[:, 8:11], PS[5][:, 0:3])
            i = v.tensor_scalar_mul(st1[:, 11:14], PS[5][:, 0:3], -1.0)
            vE_sh_h[0] = C.inc(i, "vE", 1)
        QEMIT.append((V, q_sh))
        QA_END[0] = len(QEMIT)

        dQ_q2p = [None, None]

        def q_p2init(sc):
            dQ_q2p[0] = C.val("dQ")
            dQ_q2p[1] = C.val("dQ2")
            C.wait(sc, "vE", vE_sh_h[0])
            for j in range(2):
                wsrc, r0, nr, c0, ncol, mi = qchunks[j]
                i = sc.dma_start(out=qw[j % 2][0:nr, 0:ncol], in_=wsrc[r0:r0 + nr, c0:c0 + ncol])
                C.inc(i, "dQ" if j % 2 == 0 else "dQ2", 16)
        QEMIT.append((Sc, q_p2init))

        dsts = {0: t1_in, 1: t3_in, 2: t2_in}
        lastj = {0: 15, 1: 31, 2: NQCH - 1}

        _written = set()

        def emit_write(sc, j):
            if j in _written:
                return
            _written.add(j)
            wsrc, r0, nr, c0, ncol, mi = qchunks[j]
            C.wait(sc, "vE", vE_qt[j])
            sem = "dG2" if j % 2 == 0 else "dG3"
            i = sc.dma_start(out=dsts[mi][r0:r0 + nr, c0:c0 + ncol],
                             in_=qts_s[0:nr, j % 2, 0:ncol])
            dG23[j] = (sem, C.inc(i, sem, 16))

        for j, ch in enumerate(qchunks):
            def q_sgn(sc, j=j, ch=ch):
                wsrc, r0, nr, c0, ncol, mi = ch
                if j >= 1:
                    emit_write(sc, j - 1)
                C.wait(sc, "dQ" if j % 2 == 0 else "dQ2", dQ_q2p[j % 2] + 16 * (j // 2 + 1))
                if j >= 1:
                    C.wait(sc, "vE", vE_qt[j - 1])
                sc.activation(qsg_s[0:nr, 0, 0:ncol], qw[j % 2][0:nr, 0:ncol], AF.Sign,
                              bias=st1[0:nr, 11 + mi:12 + mi], scale=1.0)
                i = sc.activation(qsg_s[0:nr, 1, 0:ncol], qw[j % 2][0:nr, 0:ncol], AF.Sign,
                                  bias=st1[0:nr, 8 + mi:9 + mi], scale=1.0)
                sE_qs[j] = C.inc(i, "sE", 1)
                if j + 2 < NQCH:
                    C.wait(sc, "sE", sE_qs[j])
                    w2s, r02, nr2, c02, nc2, _ = qchunks[j + 2]
                    i = sc.dma_start(out=qw[j % 2][0:nr2, 0:nc2], in_=w2s[r02:r02 + nr2, c02:c02 + nc2])
                    C.inc(i, "dQ" if j % 2 == 0 else "dQ2", 16)
            QEMIT.append((Sc, q_sgn))

            def q_tt(v, j=j, ch=ch):
                wsrc, r0, nr, c0, ncol, mi = ch
                C.wait(v, "sE", sE_qs[j])
                if j >= 2:
                    C.wait(v, dG23[j - 2][0], dG23[j - 2][1])
                v.tensor_tensor(qsg_s[0:nr, 2, 0:ncol], qsg_s[0:nr, 0, 0:ncol],
                                qsg_s[0:nr, 1, 0:ncol], ALU.add)
                i = v.tensor_scalar_mul(qts_s[0:nr, j % 2, 0:ncol], qsg_s[0:nr, 2, 0:ncol],
                                        st1[0:nr, 8 + mi:9 + mi])
                vE_qt[j] = C.inc(i, "vE", 1)
            QEMIT.append((V, q_tt))

            if j == lastj[0] or j == lastj[1] or j == lastj[2]:
                def q_ag(g, j=j):
                    mi = {lastj[0]: 0, lastj[1]: 1, lastj[2]: 2}[j]
                    C.wait(g, "dG2", C.val("dG2"))
                    C.wait(g, "dG3", C.val("dG3"))
                    src, dst = {0: (t1_in, t1_out), 1: (t3_in, t3_out), 2: (t2_in, t2_out)}[mi]
                    i = g.collective_compute("AllGather", ALU.bypass, replica_groups=grp8,
                                             ins=[src.ap()], outs=[dst.ap()])
                    cC_t[mi] = C.inc(i, "cC", 1)

                def q_wlast(sc, j=j):
                    emit_write(sc, j)
                QEMIT.append((Sc, q_wlast))
                QEMIT.append((G, q_ag))

        _emitted_last = set()

        def pop_quant(k):
            for _ in range(k):
                if not QEMIT:
                    return
                reg, fn = QEMIT.pop(0)
                QA_END[0] = max(0, QA_END[0] - 1)
                reg(fn)

        def drain_quant_psum_part():
            while QA_END[0] > 0:
                pop_quant(1)

        evac_done = {}
        qkv_done = {}
        dS_chunk = {}
        last_exp_use = {0: 0, 1: 0}
        last_pv = {0: 0, 1: 0}
        last_z_use = {0: 0, 1: 0}
        last_den = [0]
        last_bc = [0]
        vt_use = {0: 0, 1: 0}
        z_done = [0]
        GROUPS = [(0, 0), (1, 0), (2, 0), (0, 1), (1, 1), (2, 1)]

        JORD = [0, 2, 4, 6, 1, 3, 5, 7]
        for pk in range(2):
            for jp in range(8):
                j = JORD[jp]
                jg = pk * 8 + jp

                def sy_mini(sy, j=j, jg=jg):
                    C.wait(sy, "cC", cC_lnxq[2 * (j % 2) + 1])
                    if jg >= 2:
                        C.wait(sy, "tE", qkv_done[jg - 2])
                    blkr = (j // 2) * D
                    sem = "dS" if jg % 2 == 0 else "dS2"
                    for qq in range(2):
                        src = lnx_out4[2 * (j % 2) + qq][blkr:blkr + D, :].rearrange("(c p) t -> p c t", p=128)
                        for hcf in range(2):
                            i = sy.dma_start(out=lnch[:, jg % 2, 8 * hcf:8 * hcf + 8, _ts(qq, 256)],
                                             in_=src[:, 8 * hcf:8 * hcf + 8, :])
                            C.inc(i, sem, 16)
                    dS_chunk[jg] = (sem, C.val(sem))
                Sy(sy_mini)

                for g_i, (m, h) in enumerate(GROUPS):
                    Gg = jg * 6 + g_i

                    def t_mini(t, m=m, h=h, j=j, jg=jg, Gg=Gg, pk=pk):
                        bank = PS[Gg % 2]
                        C.wait(t, dS_chunk[jg][0], dS_chunk[jg][1])
                        C.wait(t, "dG", dG_wqkv[pk])
                        if Gg >= 2 and (Gg - 2) in evac_done:
                            C.wait(t, "vE", evac_done[Gg - 2])
                        i = None
                        for cc in range(DC):
                            i = t.matmul(bank[:, :], wqkv[:, m, h, cc, :], lnch[:, jg % 2, cc, :],
                                         start=(cc == 0), stop=(cc == DC - 1))
                        qkv_done[jg] = C.inc(i, "tE", 1)
                    T(t_mini)

                    if m < 2:
                        def v_mini(v, m=m, h=h, j=j, Gg=Gg):
                            bank = PS[Gg % 2]
                            C.wait(v, "tE", C.val("tE"))
                            i = v.tensor_copy((qt if m == 0 else kt)[:, h, _ts(j, 512)], bank[:, :])
                            evac_done[Gg] = C.inc(i, "vE", 1)
                        V(v_mini)
                        pop_quant(2)
                    else:
                        def v_mini1(v, Gg=Gg):
                            bank = PS[Gg % 2]
                            C.wait(v, "tE", C.val("tE"))
                            i = v.tensor_copy(vtmp_s[:, :], bank[:, :])
                            C.inc(i, "vE", 1)
                        V(v_mini1)

                        vpar = jg % 2

                        def t_mini2(t, vpar=vpar, jg=jg):
                            C.wait(t, "vE", C.val("vE"))
                            if vt_use[vpar]:
                                C.wait(t, "vE", vt_use[vpar])
                            i = None
                            for q in range(4):
                                i = t.transpose(PT[vpar][:, _ts(q, 128)], vtmp_s[:, _ts(q, 128)], id_s[:, :])
                            qkv_done[jg] = C.inc(i, "tE", 1)
                        T(t_mini2)

                        def v_mini2(v, h=h, j=j, Gg=Gg, vpar=vpar):
                            C.wait(v, "tE", C.val("tE"))
                            dst = vt[:, h, 4 * j:4 * j + 4, :].rearrange("p k e -> p (k e)")
                            i = v.tensor_copy(dst, PT[vpar][:, 0:512])
                            vt_use[vpar] = C.inc(i, "vE", 1)
                            evac_done[Gg] = C.val("vE")
                        V(v_mini2)
                        pop_quant(2)

            if pk == 0:
                def g_wq1(g):
                    C.wait(g, "tE", qkv_done[7])
                    for m in range(3):
                        i = g.dma_start(out=wqkv[:, m, :, :, :], in_=wqkv_in[:, m, 2:4, :, :])
                        C.inc(i, "dG", 16)
                    dG_wqkv[1] = C.val("dG")
                G(g_wq1)

            drain_quant_psum_part()
            vE_pass_evacs = evac_done[(pk * 8 + 7) * 6 + 5]

            for hl in range(2):
                hglob = 2 * pk + hl
                for qc in range(NQ):
                    nk = 4 * qc + 4
                    tS = {}

                    def t_pre(t, hl=hl, qc=qc, nk=nk, tS=tS, vE_pass_evacs=vE_pass_evacs):
                        C.wait(t, "vE", vE_pass_evacs)
                        if last_den[0]:
                            C.wait(t, "vE", last_den[0])
                        if last_z_use[qc % 2]:
                            C.wait(t, "vE", last_z_use[qc % 2])
                        for kc in range(min(2, nk)):
                            if last_exp_use[kc % 2]:
                                C.wait(t, "sE", last_exp_use[kc % 2])
                            i = t.matmul(PS[kc % 2][:, :], kt[:, hl, _ts(kc, 128)],
                                         qt[:, hl, _ts(qc, 512)], start=True, stop=True)
                            tS[kc] = C.inc(i, "tE", 1)
                    T(t_pre)

                    for kc in range(nk):
                        diag = kc >= nk - 4
                        eK = [0]
                        mK = [0]

                        def s_exp(s, kc=kc, tS=tS, eK=eK):
                            C.wait(s, "tE", tS[kc])
                            if last_pv[kc % 2]:
                                C.wait(s, "tE", last_pv[kc % 2])
                            i = s.activation(exp_s[:, kc % 2, :], PS[kc % 2][:, :], AF.Exp, scale=QSCL)
                            eK[0] = C.inc(i, "sE", 1)
                            last_exp_use[kc % 2] = eK[0]
                        Sc(s_exp)

                        if diag:
                            def v_mask(v, kc=kc, nk=nk, eK=eK, mK=mK):
                                C.wait(v, "sE", eK[0])
                                if last_pv[kc % 2]:
                                    C.wait(v, "tE", last_pv[kc % 2])
                                jj = kc - (nk - 4)
                                i = v.tensor_tensor(expm_s[:, kc % 2, :], exp_s[:, kc % 2, :],
                                                    mask_s[:, jj, :], ALU.mult)
                                mK[0] = C.inc(i, "vE", 1)
                            V(v_mask)

                        def t_pv(t, hl=hl, qc=qc, kc=kc, nk=nk, diag=diag, eK=eK, mK=mK, tS=tS):
                            if diag:
                                C.wait(t, "vE", mK[0])
                                src = expm_s[:, kc % 2, :]
                            else:
                                C.wait(t, "sE", eK[0])
                                src = exp_s[:, kc % 2, :]
                            i = t.matmul(PS[2 + qc % 2][:, :], vt[:, hl, kc, :], src,
                                         start=(kc == 0), stop=(kc == nk - 1))
                            C.inc(i, "tE", 1)
                            i = t.matmul(PS[4][0:1, :], ones_bc[:, :], src,
                                         start=(kc == 0), stop=(kc == nk - 1))
                            last_pv[kc % 2] = C.inc(i, "tE", 1)
                            if kc + 2 < nk:
                                C.wait(t, "sE", last_exp_use[kc % 2])
                                i = t.matmul(PS[kc % 2][:, :], kt[:, hl, _ts(kc + 2, 128)],
                                             qt[:, hl, _ts(qc, 512)], start=True, stop=True)
                                tS[kc + 2] = C.inc(i, "tE", 1)
                        T(t_pv)

                    def v_recip(v):
                        C.wait(v, "tE", C.val("tE"))
                        i = v.reciprocal(inv_s[:, :], PS[4][0:1, :])
                        last_den[0] = C.inc(i, "vE", 1)
                    V(v_recip)

                    def t_bc(t):
                        C.wait(t, "vE", last_den[0])
                        if last_bc[0]:
                            C.wait(t, "vE", last_bc[0])
                        i = t.matmul(PS[5][:, :], ones_fr[:, :], inv_s[0:1, :], start=True, stop=True)
                        C.inc(i, "tE", 1)
                    T(t_bc)

                    def v_zmul(v, hglob=hglob, qc=qc):
                        C.wait(v, "tE", C.val("tE"))
                        v.tensor_copy(bcz_s[:, :], PS[5][:, :])
                        i = v.tensor_tensor(z_v[:, hglob, _ts(qc, 512)], PS[2 + qc % 2][:, :],
                                            bcz_s[:, :], ALU.mult)
                        last_z_use[qc % 2] = C.inc(i, "vE", 1)
                        last_bc[0] = C.val("vE")
                        z_done[0] = C.val("vE")
                    V(v_zmul)
                    pop_quant(3)

        vE_zall = z_done[0]

        pop_quant(10 ** 9)

        dG_wo = [0]

        @G
        def _(g):
            C.wait(g, "tE", qkv_done[15])
            i = g.dma_start(out=wo_v[:, :, :, :], in_=wo_in[:, :, :, :])
            dG_wo[0] = C.inc(i, "dG", 16)

        p2_evac = {}
        st_p2 = {}
        cC_rs = {}

        for dc in range(DC):
            for grp, t5s in ((0, (0, 1, 2, 3, 4, 5)), (1, (6, 7))):
                def t_p2(t, dc=dc, grp=grp, t5s=t5s):
                    C.wait(t, "dG", dG_wo[0])
                    C.wait(t, "vE", vE_zall)
                    prev = (dc, 0) if grp == 1 else (dc - 1, 1)
                    if prev in p2_evac:
                        C.wait(t, "vE", p2_evac[prev])
                    i = None
                    for he in range(4):
                        for bi, t5 in enumerate(t5s):
                            i = t.matmul(PS[bi][:, :], wo_v[:, he, dc, :],
                                         z_v[:, he, _ts(t5, 512)], start=(he == 0), stop=(he == 3))
                    C.inc(i, "tE", 1)
                T(t_p2)

                def v_p2(v, dc=dc, grp=grp, t5s=t5s):
                    C.wait(v, "tE", C.val("tE"))
                    if grp == 0 and dc >= 1:
                        C.wait(v, *st_p2[dc - 1])
                    i = None
                    for bi, t5 in enumerate(t5s):
                        i = v.tensor_copy(p2stg[:, t5, :], PS[bi][:, :])
                    p2_evac[(dc, grp)] = C.inc(i, "vE", 1)
                V(v_p2)

            def sy_p2(sy, dc=dc):
                k = dc // 2
                C.wait(sy, "vE", p2_evac[(dc, 1)])
                if dc % 2 == 0 and k >= 2:
                    C.wait(sy, "cC", cC_rs[k - 2])
                rsv = rs_in8[k].rearrange("s (c p) (h t) -> p c s h t", p=128, t=512)
                i = sy.dma_start(out=rsv[:, dc % 2, :, :, :],
                                 in_=p2stg.rearrange("p (s h) t -> p s h t", s=4))
                st_p2[dc] = ("dR" if k % 2 == 0 else "dR2", C.inc(i, "dR" if k % 2 == 0 else "dR2", 16))
            Sy(sy_p2)

            if dc % 2 == 1:
                def g_rs(g, k=dc // 2):
                    C.wait(g, *st_p2[2 * k + 1])
                    i = g.collective_compute("ReduceScatter", ALU.add, replica_groups=grp4,
                                             ins=[rs_in8[k].ap()], outs=[rs_out8[k].ap()])
                    cC_rs[k] = C.inc(i, "cC", 1)
                G(g_rs)

        at_ld = {}
        tE_asq = {}

        def sy_at(sy, k):
            C.wait(sy, "cC", cC_rs[k])
            if k >= 2:
                C.wait(sy, "tE", tE_asq[2 * (k - 2) + 1])
            sem = "dS" if k % 2 == 0 else "dS2"
            i0 = sy.dma_start(out=attn_t[:, 2 * k, :], in_=rs_out8[k][0:128, :])
            C.inc(i0, sem, 16)
            i1 = sy.dma_start(out=attn_t[:, 2 * k + 1, :], in_=rs_out8[k][128:256, :])
            C.inc(i1, sem, 16)
            at_ld[k] = (sem, C.val(sem))

        for c in range(DC):
            if c % 2 == 0:
                Sy(lambda sy, k=c // 2: sy_at(sy, k))
            def s_asq(s, c=c):
                C.wait(s, *at_ld[c // 2])
                if c >= 1:
                    C.wait(s, "tE", tE_asq[c - 1])
                i = s.activation(sqb[:, :], attn_t[:, c, :], AF.Square)
                C.inc(i, "sE", 1)
            Sc(s_asq)

            def t_asq(t, c=c):
                C.wait(t, *at_ld[c // 2])
                t.matmul(PS[4][0:1, :], ones_bc[:, :], attn_t[:, c, 0:512], start=(c == 0), stop=(c == DC - 1))
                t.matmul(PS[5][0:1, :], ones_bc[:, :], attn_t[:, c, 512:1024], start=(c == 0), stop=(c == DC - 1))
                C.wait(t, "sE", C.val("sE"))
                t.matmul(PS[0][0:1, :], ones_bc[:, :], sqb[:, 0:512], start=(c == 0), stop=(c == DC - 1))
                i = t.matmul(PS[1][0:1, :], ones_bc[:, :], sqb[:, 512:1024], start=(c == 0), stop=(c == DC - 1))
                tE_asq[c] = C.inc(i, "tE", 1)
            T(t_asq)

        emit_rowchain([(PS[4], slice(0, 512)), (PS[5], slice(0, 512))],
                      [(PS[0], slice(0, 512)), (PS[1], slice(0, 512))], 1024)
        emit_bcast_mi()
        tE_abc = C.val("tE")


        x2ld = {}

        @Sc
        def _(s):
            C.wait(s, "vE", vE_app[-1])
            for c in range(2):
                sem = "dQ" if c % 2 == 0 else "dQ2"
                i = s.dma_start(out=(xa if c % 2 == 0 else xb), in_=xT[_ts(c, 128), :])
                x2ld[c] = (sem, C.inc(i, sem, 16))

        vE_x1 = {}
        dS_x1w = {}
        for c in range(DC):
            def v_x1(v, c=c):
                xbuf = xa if c % 2 == 0 else xb
                C.wait(v, "tE", tE_abc)
                C.wait(v, *at_ld[c // 2])
                C.wait(v, x2ld[c][0], x2ld[c][1])
                if c >= 1:
                    C.wait(v, "dW", dS_x1w[c - 1])
                i = None
                for hf in range(2):
                    v.tensor_tensor(ftmp[:, _ts(hf, 512)], attn_t[:, c, _ts(hf, 512)], PS[hf][:, :], ALU.subtract)
                    v.tensor_tensor(qx[:, _ts(hf, 512)], ftmp[:, _ts(hf, 512)], PS[2 + hf][:, :], ALU.mult)
                    i = v.tensor_tensor(fscr[:, _ts(hf, 512)], qx[:, _ts(hf, 512)],
                                        xbuf[:, _ts(hf, 512)], ALU.add)
                vE_x1[c] = C.inc(i, "vE", 1)
            V(v_x1)

            def sy_x1(sy, c=c):
                C.wait(sy, "vE", vE_x1[c])
                i = sy.dma_start(out=x1_scr[_ts(c, 128), :], in_=fscr[:, :])
                dS_x1w[c] = C.inc(i, "dW", 16)
            Sy(sy_x1)

            def s_x1(s, c=c):
                if c + 2 < DC:
                    C.wait(s, "vE", vE_x1[c])
                    sem = "dQ" if c % 2 == 0 else "dQ2"
                    i = s.dma_start(out=(xa if c % 2 == 0 else xb), in_=xT[_ts(c + 2, 128), :])
                    x2ld[c + 2] = (sem, C.inc(i, sem, 16))
            Sc(s_x1)

        stld = {}

        @Sc
        def _(s):
            C.wait(s, "vE", vE_x1[DC - 1])
            for c in range(2):
                C.wait(s, "dW", dS_x1w[c])
                sem = "dQ" if c % 2 == 0 else "dQ2"
                i = s.dma_start(out=(xa if c % 2 == 0 else xb), in_=x1_scr[_ts(c, 128), :])
                stld[c] = (sem, C.inc(i, sem, 16))

        sE_st = {}
        tE_st = {}
        for c in range(DC):
            def s_stA(s, c=c):
                C.wait(s, stld[c][0], stld[c][1])
                if c >= 1:
                    C.wait(s, "tE", tE_st[c - 1])
                i = s.activation(qx[:, :], (xa if c % 2 == 0 else xb), AF.Square)
                sE_st[c] = C.inc(i, "sE", 1)
            Sc(s_stA)

            def t_st(t, c=c):
                xbuf = xa if c % 2 == 0 else xb
                if c == 0:
                    C.wait(t, "vE", vE_x1[DC - 1])
                C.wait(t, stld[c][0], stld[c][1])
                t.matmul(PS[4][0:1, :], ones_fc[:, :], xbuf[:, 0:512], start=(c == 0), stop=(c == DC - 1))
                t.matmul(PS[5][0:1, :], ones_fc[:, :], xbuf[:, 512:1024], start=(c == 0), stop=(c == DC - 1))
                C.wait(t, "sE", sE_st[c])
                t.matmul(PS[0][0:1, :], ones_fc[:, :], qx[:, 0:512], start=(c == 0), stop=(c == DC - 1))
                i = t.matmul(PS[1][0:1, :], ones_fc[:, :], qx[:, 512:1024], start=(c == 0), stop=(c == DC - 1))
                tE_st[c] = C.inc(i, "tE", 1)
            T(t_st)

            def s_stB(s, c=c):
                if c + 2 < DC:
                    C.wait(s, "tE", tE_st[c])
                    C.wait(s, "dW", dS_x1w[c + 2])
                    sem = "dQ" if c % 2 == 0 else "dQ2"
                    i = s.dma_start(out=(xa if c % 2 == 0 else xb), in_=x1_scr[_ts(c + 2, 128), :])
                    stld[c + 2] = (sem, C.inc(i, sem, 16))
            Sc(s_stB)

        emit_rowchain([(PS[4], slice(0, 512)), (PS[5], slice(0, 512))],
                      [(PS[0], slice(0, 512)), (PS[1], slice(0, 512))], 1024)
        emit_bcast_mi()
        tE_l2bc = C.val("tE")

        l2ld = {}
        tE_pb = C.val("tE")

        @Sc
        def _(s):
            C.wait(s, "tE", tE_pb)
            for c in range(2):
                sem = "dQ" if c % 2 == 0 else "dQ2"
                i = s.dma_start(out=(xa if c % 2 == 0 else xb), in_=x1_scr[_ts(c, 128), :])
                l2ld[c] = (sem, C.inc(i, sem, 16))

        vE_l2 = {}
        for c in range(DC):
            def v_l2(v, c=c):
                xbuf = xa if c % 2 == 0 else xb
                C.wait(v, "tE", tE_l2bc)
                C.wait(v, l2ld[c][0], l2ld[c][1])
                i = None
                for hf in range(2):
                    v.tensor_tensor(fscr[:, _ts(hf, 512)], xbuf[:, _ts(hf, 512)], PS[hf][:, :], ALU.subtract)
                    i = v.tensor_tensor(ln2xt[:, c, _ts(hf, 512)], fscr[:, _ts(hf, 512)], PS[2 + hf][:, :], ALU.mult)
                vE_l2[c] = C.inc(i, "vE", 1)
            V(v_l2)

            def s_l2(s, c=c):
                if c + 2 < DC:
                    C.wait(s, "vE", vE_l2[c])
                    sem = "dQ" if c % 2 == 0 else "dQ2"
                    i = s.dma_start(out=(xa if c % 2 == 0 else xb), in_=x1_scr[_ts(c + 2, 128), :])
                    l2ld[c + 2] = (sem, C.inc(i, sem, 16))
            Sc(s_l2)
        vE_ln2 = C.val("vE")

        HCP = [(2 * i, 2) for i in range(21)] + [(42, 1)]
        out_store = [0]
        vE_out_prev = [0]
        gt_tap_gate = [0]
        gt_done = {}
        sE_sil = {}
        h_mm = {}
        w2_done = {}
        fstg_evac = {}

        def megabarrier():
            for eng, name in ((T, "tensor"),):
                def bar_t(t):
                    C.wait(t, "vE", C.val("vE"))
                    C.wait(t, "sE", C.val("sE"))
                    C.wait(t, "dS", C.val("dS"))
                    C.wait(t, "dQ", C.val("dQ"))
                    C.wait(t, "dG", C.val("dG"))
                    i = t.matmul(PS[5][0:1, 0:1], ones_fc[0:1, :], ones_fc[0:1, 0:1], start=True, stop=True)
                    C.inc(i, "tE", 1)
                T(bar_t)
            def bar_v(v):
                C.wait(v, "tE", C.val("tE"))
                i = v.tensor_copy(st1[:, 15:16], st1[:, 14:15])
                C.inc(i, "vE", 1)
            V(bar_v)
            def bar_s(s):
                C.wait(s, "vE", C.val("vE"))
                i = s.mul(st1[:, 15:16], st1[:, 15:16], 1.0)
                C.inc(i, "sE", 1)
            Sc(bar_s)
            def bar_sy(sy):
                C.wait(sy, "sE", C.val("sE"))
            Sy(bar_sy)
            def bar_g(g):
                C.wait(g, "sE", C.val("sE"))
            G(bar_g)

        MEGABAR = False
        for th in range(2):
            tsl = _ts(th, 512)
            if MEGABAR:
                megabarrier()
            dS_fb = {}
            for pi, (hc0, nn) in enumerate(HCP):
                def sy_fb(sy, pi=pi, hc0=hc0, nn=nn, th=th):
                    C.wait(sy, "cC", cC_t[1])
                    if th == 0 and pi < 2:
                        C.wait(sy, "vE", vE_ln2)
                    if pi >= 2:
                        C.wait(sy, "tE", h_mm[(th, pi - 2, 1)])
                    elif th == 1 and pi < 2:
                        C.wait(sy, "tE", h_mm[(0, len(HCP) - 2 + pi, 1)])
                        C.wait(sy, "tE", w2_done[(0, 3)])
                    sem = "dS" if pi % 2 == 0 else "dS2"
                    for m, tsrc in ((0, t1_out), (1, t3_out)):
                        src = tsrc[:, hc0 * 128:(hc0 + nn) * 128].rearrange("(c p) h -> p c h", p=128)
                        for q in range(4):
                            i = sy.dma_start(out=fbuf[:, m * 2 + pi % 2, _ts(q, 4), 0:nn * 128],
                                             in_=src[:, _ts(q, 4), :])
                            C.inc(i, sem, 16)
                    dS_fb[pi] = (sem, C.val(sem))
                Sy(sy_fb)

                for li in range(nn):
                    hc = hc0 + li

                    def t_h(t, pi=pi, li=li, hc=hc, th=th):
                        C.wait(t, dS_fb[pi][0], dS_fb[pi][1])
                        if th == 1 and hc < 2 and gt_tap_gate[0]:
                            C.wait(t, "dG", gt_tap_gate[0])
                        if th == 0 and hc < 2:
                            C.wait(t, "vE", vE_ln2)
                        if th == 1 and hc < 2 and (0, 3) in fstg_evac:
                            C.wait(t, "vE", fstg_evac[(0, 3)])
                        if hc >= 2:
                            C.wait(t, "vE", gt_done[(th, hc - 2)])
                        elif th == 1:
                            C.wait(t, "vE", gt_done[(0, NHC - 2 + hc)])
                        i1 = None
                        i3 = None
                        for cc in range(DC):
                            i1 = t.matmul(PS[hc % 2][:, :], fbuf[:, 0 + pi % 2, cc, _ts(li, 128)],
                                          ln2xt[:, cc, _ts(th, 512)], start=(cc == 0), stop=(cc == DC - 1))
                        h_mm[(th, pi, 0)] = C.inc(i1, "tE", 1)
                        for cc in range(DC):
                            i3 = t.matmul(PS[2 + hc % 2][:, :], fbuf[:, 2 + pi % 2, cc, _ts(li, 128)],
                                          ln2xt[:, cc, _ts(th, 512)], start=(cc == 0), stop=(cc == DC - 1))
                        h_mm[(th, pi, 1)] = C.inc(i3, "tE", 1)
                    T(t_h)

                    def s_sil(s, pi=pi, hc=hc, th=th):
                        C.wait(s, "tE", h_mm[(th, pi, 1)])
                        if hc >= 1:
                            C.wait(s, "vE", gt_done[(th, hc - 1)])
                        elif th == 1:
                            C.wait(s, "vE", gt_done[(0, NHC - 1)])
                        i = s.activation(sil_s[:, :], PS[hc % 2][:, :], AF.Silu)
                        sE_sil[(th, hc)] = C.inc(i, "sE", 1)
                    Sc(s_sil)

                    def v_g(v, hc=hc, th=th):
                        C.wait(v, "sE", sE_sil[(th, hc)])
                        C.wait(v, "tE", C.val("tE"))
                        i = v.tensor_tensor(gt_v[:, hc, :], sil_s[:, :], PS[2 + hc % 2][:, :], ALU.mult)
                        gt_done[(th, hc)] = C.inc(i, "vE", 1)
                    V(v_g)

            if th == 0 and "gt" in taps:
                tpg = tapt("gt", [128, NHC, 512], BF16)

                @G
                def _(g):
                    C.wait(g, "vE", gt_done[(0, NHC - 1)])
                    i = g.dma_start(out=tpg[:, :, :], in_=gt_v[:, :, :])
                    gt_tap_gate[0] = C.inc(i, "dG", 16)
            if th == 0 and "fb" in taps:
                tpf = tapt("fb", [128, 4, 16, 256], BF16)

                @G
                def _(g):
                    C.wait(g, "vE", gt_done[(0, NHC - 1)])
                    i = g.dma_start(out=tpf[:, :, :, :], in_=fbuf[:, :, :, :])
                    gt_tap_gate[0] = C.inc(i, "dG", 16)

            if MEGABAR:
                megabarrier()
            dQ_sl = {}
            for dcg in range(4):
                def s_slab(s, dcg=dcg, th=th):
                    C.wait(s, "cC", cC_t[2])
                    if dcg == 0:
                        if th == 0:
                            C.wait(s, "vE", vE_ln2)
                        C.wait(s, "tE", h_mm[(th, len(HCP) - 1, 1)])
                    if (th, dcg - 1) in w2_done:
                        C.wait(s, "tE", w2_done[(th, dcg - 1)])
                    elif th == 1 and dcg == 0:
                        C.wait(s, "tE", w2_done[(0, 3)])
                    src = t2_out[:, _ts(dcg, 512)]
                    sem = "dQ" if dcg % 2 == 0 else "dQ2"
                    for q in range(5):
                        h0 = q * 9
                        h1 = min(NHC, h0 + 9)
                        if h0 >= NHC:
                            break
                        i = s.dma_start(out=slab[:, h0:h1, :],
                                        in_=src[h0 * 128:h1 * 128, :].rearrange("(h p) d -> p h d", p=128))
                        C.inc(i, sem, 16)
                    dQ_sl[dcg] = (sem, C.val(sem))
                Sc(s_slab)

                def t_w2(t, dcg=dcg, th=th):
                    C.wait(t, dQ_sl[dcg][0], dQ_sl[dcg][1])
                    C.wait(t, "vE", gt_done[(th, NHC - 1)])
                    if th == 1 and dcg == 0 and vE_out_prev[0]:
                        C.wait(t, "vE", vE_out_prev[0])
                    if (th, dcg - 1) in fstg_evac:
                        C.wait(t, "vE", fstg_evac[(th, dcg - 1)])
                    elif th == 1 and dcg == 0:
                        C.wait(t, "vE", fstg_evac[(0, 3)])
                    i = None
                    for hc in range(NHC):
                        for dcl in range(4):
                            i = t.matmul(PS[dcl][:, :], slab[:, hc, _ts(dcl, 128)],
                                         gt_v[:, hc, :], start=(hc == 0), stop=(hc == NHC - 1))
                    w2_done[(th, dcg)] = C.inc(i, "tE", 1)
                T(t_w2)

                def v_w2(v, dcg=dcg, th=th):
                    C.wait(v, "tE", w2_done[(th, dcg)])
                    i = None
                    for dcl in range(4):
                        i = v.tensor_copy(fstg[:, dcg * 4 + dcl, :], PS[dcl][:, :])
                    fstg_evac[(th, dcg)] = C.inc(i, "vE", 1)
                V(v_w2)

            tE_fsq = {}
            for c in range(DC):
                def s_fsq(s, c=c, th=th):
                    if th == 1 and c == 0 and vE_out_prev[0]:
                        C.wait(s, "vE", vE_out_prev[0])
                    C.wait(s, "vE", fstg_evac[(th, c // 4)])
                    if c >= 1:
                        C.wait(s, "tE", tE_fsq[c - 1])
                    i = s.activation(fsq_v[:, :], fstg[:, c, :], AF.Square)
                    C.inc(i, "sE", 1)
                Sc(s_fsq)

                def t_fsq(t, c=c, th=th):
                    if th == 1 and c == 0 and vE_out_prev[0]:
                        C.wait(t, "vE", vE_out_prev[0])
                    C.wait(t, "vE", fstg_evac[(th, c // 4)])
                    t.matmul(PS[4][0:1, :], ones_bc[:, :], fstg[:, c, :], start=(c == 0), stop=(c == DC - 1))
                    C.wait(t, "sE", C.val("sE"))
                    i = t.matmul(PS[5][0:1, :], ones_bc[:, :], fsq_v[:, :], start=(c == 0), stop=(c == DC - 1))
                    tE_fsq[c] = C.inc(i, "tE", 1)
                T(t_fsq)

            emit_rowchain([(PS[4], slice(0, 512))], [(PS[5], slice(0, 512))], 512)

            def t_fbc(t):
                C.wait(t, "sE", C.val("sE"))
                t.matmul(PS[4][:, :], ones_fr[:, :], r_m[0:1, 0:512], start=True, stop=True)
                i = t.matmul(PS[5][:, :], ones_fr[:, :], r_istd[0:1, 0:512], start=True, stop=True)
                C.inc(i, "tE", 1)
            T(t_fbc)
            tE_fbc = C.val("tE")

            x1ld = {}

            @Sc
            def _(s, th=th):
                if th == 1 and vE_out_prev[0]:
                    C.wait(s, "vE", vE_out_prev[0])
                C.wait(s, "tE", w2_done[(th, 3)])
                for c in range(2):
                    sem = "dQ" if c % 2 == 0 else "dQ2"
                    i = s.dma_start(out=x1pb[c % 2], in_=x1_scr[_ts(c, 128), _ts(th, 512)])
                    x1ld[c] = (sem, C.inc(i, sem, 16))

            vE_o = {}
            dS_o = {}
            for c in range(DC):
                def v_out(v, c=c, th=th):
                    C.wait(v, "tE", tE_fbc)
                    C.wait(v, x1ld[c][0], x1ld[c][1])
                    if c >= 1:
                        C.wait(v, "dW", dS_o[c - 1])
                    v.tensor_tensor(fscr[:, 0:512], fstg[:, c, :], PS[4][:, :], ALU.subtract)
                    v.tensor_tensor(ftmp[:, 0:512], fscr[:, 0:512], PS[5][:, :], ALU.mult)
                    i = v.tensor_tensor(fscr[:, 0:512], ftmp[:, 0:512], x1pb[c % 2], ALU.add)
                    vE_o[c] = C.inc(i, "vE", 1)
                V(v_out)

                def sy_out(sy, c=c, th=th):
                    C.wait(sy, "vE", vE_o[c])
                    i = sy.dma_start(out=outT[_ts(c, 128), _ts(th, 512)], in_=fscr[:, 0:512])
                    dS_o[c] = C.inc(i, "dW", 16)
                    out_store[0] = dS_o[c]
                Sy(sy_out)

                def s_xr(s, c=c, th=th):
                    if c + 2 < DC:
                        C.wait(s, "vE", vE_o[c])
                        sem = "dQ" if c % 2 == 0 else "dQ2"
                        i = s.dma_start(out=x1pb[c % 2], in_=x1_scr[_ts(c + 2, 128), _ts(th, 512)])
                        x1ld[c + 2] = (sem, C.inc(i, sem, 16))
                Sc(s_xr)

            vE_out_prev[0] = vE_o[DC - 1]

        if "lnx" in taps:
            tp = tapt("lnx", [4 * D, TC], BF16)

            @G
            def _(g):
                C.wait(g, "cC", cC_lnx)
                i = g.dma_start(out=tp[:, :], in_=lnx_out.ap())
                C.inc(i, "dG", 16)
        if "t1" in taps:
            tp = tapt("t1", [D, HIDP], BF16)

            @G
            def _(g):
                C.wait(g, "cC", cC_t[0])
                i = g.dma_start(out=tp[:, :], in_=t1_out.ap())
                C.inc(i, "dG", 16)
        if "t3" in taps:
            tp = tapt("t3", [D, HIDP], BF16)

            @G
            def _(g):
                C.wait(g, "cC", cC_t[1])
                i = g.dma_start(out=tp[:, :], in_=t3_out.ap())
                C.inc(i, "dG", 16)
        if "t3in" in taps:
            tp = tapt("t3in", [D_SH, HIDP], BF16)

            @G
            def _(g):
                C.wait(g, "cC", cC_t[1])
                i = g.dma_start(out=tp[:, :], in_=t3_in.ap())
                C.inc(i, "dG", 16)
        if "t2" in taps:
            tp = tapt("t2", [HIDP, D], BF16)

            @G
            def _(g):
                C.wait(g, "cC", cC_t[2])
                i = g.dma_start(out=tp[:, :], in_=t2_out.ap())
                C.inc(i, "dG", 16)
        if "ln2" in taps:
            tp = tapt("ln2", [128, 16, TC], BF16)

            @G
            def _(g):
                C.wait(g, "vE", vE_ln2)
                i = g.dma_start(out=tp[:, :, :], in_=ln2xt[:, :, :])
                C.inc(i, "dG", 16)
        if "z" in taps:
            tp = tapt("z", [128, HL, S], BF16)

            @G
            def _(g):
                C.wait(g, "vE", vE_zall)
                i = g.dma_start(out=tp[:, :, :], in_=z_v[:, :, :])
                C.inc(i, "dG", 16)
        if "rs" in taps:
            tp = tapt("rs", [D, TC], BF16)

            @G
            def _(g):
                C.wait(g, "cC", cC_rs[7])
                for k in range(8):
                    i = g.dma_start(out=tp[_ts(k, 256), :], in_=rs_out8[k].ap())
                    C.inc(i, "dG", 16)
        if "x1" in taps:
            tp = tapt("x1", [D, TC], F32)

            @G
            def _(g):
                C.wait(g, "dW", dS_x1w[DC - 1])
                i = g.dma_start(out=tp[:, :], in_=x1_scr.ap())
                C.inc(i, "dG", 16)
        if "qkv" in taps:
            tpq = tapt("qkv", [128, 3, 2, S], BF16)

            @G
            def _(g):
                C.wait(g, "vE", evac_done[47])
                i = g.dma_start(out=tpq[:, 0, :, :], in_=qt[:, :, :])
                C.inc(i, "dG", 16)
                i = g.dma_start(out=tpq[:, 1, :, :], in_=kt[:, :, :])
                C.inc(i, "dG", 16)
                i = g.dma_start(out=tpq[:, 2, :, :],
                                in_=vt.rearrange("p h k e -> p h (k e)"))
                C.inc(i, "dG", 16)

        if "t1b" in taps:
            tp = tapt("t1b", [D, HIDP], BF16)

            @G
            def _(g):
                C.wait(g, "dW", out_store[0])
                i = g.dma_start(out=tp[:, :], in_=t1_out.ap())
                C.inc(i, "dG", 16)
        if "ln2b" in taps:
            tp = tapt("ln2b", [128, 16, TC], BF16)

            @G
            def _(g):
                C.wait(g, "dW", out_store[0])
                i = g.dma_start(out=tp[:, :, :], in_=ln2xt[:, :, :])
                C.inc(i, "dG", 16)

        @Sy
        def _(sy):
            for nm in ("dS", "dS2", "dW", "dQ", "dQ2", "dG", "dG2", "dG3"):
                C.wait(sy, nm, C.val(nm))

    return tap_t



def _mk_masks():
    kl = np.arange(128)[:, None]
    ql = np.arange(512)[None, :]
    return np.stack([(128 * j + kl <= ql) for j in range(4)]).astype(ml_dtypes.bfloat16)


def make_core_inputs(inp):
    x = np.asarray(inp["L0_attn_x"], dtype=np.float32).reshape(B * S, D)
    wq = np.asarray(inp["L0_attn_W_Q"], dtype=np.float32)
    wk = np.asarray(inp["L0_attn_W_K"], dtype=np.float32)
    wv = np.asarray(inp["L0_attn_W_V"], dtype=np.float32)
    wo = np.asarray(inp["L0_attn_W_O"], dtype=np.float32)
    w1 = np.asarray(inp["L0_ffn_w1"], dtype=np.float32)
    w3 = np.asarray(inp["L0_ffn_w3"], dtype=np.float32)
    w2 = np.asarray(inp["L0_ffn_w2"], dtype=np.float32)

    w1p = np.zeros((HIDP, D), np.float32); w1p[:HID] = w1
    w3p = np.zeros((HIDP, D), np.float32); w3p[:HID] = w3
    w2p = np.zeros((D, HIDP), np.float32); w2p[:, :HID] = w2
    w1Tf = np.ascontiguousarray(w1p.T)
    w3Tf = np.ascontiguousarray(w3p.T)
    w2Tf = np.ascontiguousarray(w2p.T)
    masks = _mk_masks()
    ident = np.eye(128, dtype=ml_dtypes.bfloat16)

    maps = []
    for c in range(NCORE):
        h0 = 4 * (c % 4)
        wqkv_host = np.empty((128, 3, HL, DC, 128), np.float32)
        for m, w in enumerate((wq, wk, wv)):
            for h in range(HL):
                wqkv_host[:, m, h] = w[h0 + h].T.reshape(DC, 128, 128).transpose(1, 0, 2)
        wo_host = np.empty((128, HL, DC, 128), np.float32)
        for h in range(HL):
            wo_host[:, h] = wo[h0 + h].T.reshape(128, DC, 128)
        maps.append({
            "xT": np.ascontiguousarray(x[c * TC:(c + 1) * TC, :].T),
            "wqkv_in": wqkv_host,
            "wo_in": wo_host,
            "w1T": np.ascontiguousarray(w1Tf[c * D_SH:(c + 1) * D_SH, :]),
            "w3T": np.ascontiguousarray(w3Tf[c * D_SH:(c + 1) * D_SH, :]),
            "w2T": np.ascontiguousarray(w2Tf[c * HID_SH:(c + 1) * HID_SH, :]),
            "masks": masks,
            "ident": ident,
        })
    return maps


def kernel(**inputs):
    nc = bass.Bass("TRN2", target_bir_lowering=False)
    build_program(nc)
    maps = make_core_inputs(inputs)
    res = run_bass_kernel_spmd(nc, maps, core_ids=list(range(NCORE)))
    parts = [np.ascontiguousarray(res.results[c]["outT"].T) for c in range(NCORE)]
    return np.concatenate(parts, axis=0).reshape(B, S, D).astype(np.float32)


# version 12
# speedup vs baseline: 1.0933x; 1.0411x over previous
import math
import numpy as np
import ml_dtypes
from contextlib import ExitStack

import concourse.bass as bass
import concourse.mybir as mybir
from concourse.bass_utils import run_bass_kernel_spmd

F32 = mybir.dt.float32
BF16 = mybir.dt.bfloat16
AF = mybir.ActivationFunctionType
ALU = mybir.AluOpType

NCORE = 8
B, S, D, H, E = 2, 4096, 2048, 16, 128
HL = 4
TC = 1024
DC = D // 128
HID = 5461
HIDP = 5504
NHC = HIDP // 128
HID_SH = HIDP // NCORE
D_SH = D // NCORE
NQ = S // 512
QSCL = float(1.0 / math.sqrt(128.0))
W_N = float(HID * D)
LN_EPS = 1e-5

PB_SZ = 70656
OFF_QT = 0
OFF_KT = 8192
OFF_VT = 16384
OFF_WQKV = 24576
OFF_LNCH = 36864
OFF_Z = 53248
OFF_LN2XT = 0
OFF_ATTN = 16384
OFF_WO = 32768
OFF_P2STG = 40960
OFF_SQB = 45056
OFF_GT = 16384
OFF_FBUF = 38400
OFF_SLAB = 38400
OFF_FSTG = 60416
OFF_SIL2 = 68608

PF_SZ = 6656
PFO_XA, PFO_XB = 0, 1024
PFO_SCR = 2048
PFO_TMP = 3072
PFO_QW = 4096
PFO_X1A, PFO_X1B = 5472, 5984


def _ts(i, n):
    return slice(i * n, (i + 1) * n)


class Ctx:
    def __init__(self):
        self.h = {}
        self.n = {}

    def add(self, name, handle):
        self.h[name] = handle
        self.n[name] = 0

    def inc(self, instr, name, k=1):
        instr.then_inc(self.h[name], k)
        self.n[name] += k
        return self.n[name]

    def val(self, name):
        return self.n[name]

    def wait(self, eng, name, v=None):
        vv = self.n[name] if v is None else v
        if vv > 0:
            eng.wait_ge(self.h[name], vv)


def build_program(nc, taps=()):
    taps = set(taps)
    tap_t = {}

    xT = nc.dram_tensor("xT", [D, TC], F32, kind="ExternalInput").ap()
    wqkv_in = nc.dram_tensor("wqkv_in", [128, 3, HL, DC, 128], F32, kind="ExternalInput").ap()
    wo_in = nc.dram_tensor("wo_in", [128, HL, DC, 128], F32, kind="ExternalInput").ap()
    w1T = nc.dram_tensor("w1T", [D_SH, HIDP], F32, kind="ExternalInput").ap()
    w3T = nc.dram_tensor("w3T", [D_SH, HIDP], F32, kind="ExternalInput").ap()
    w2T = nc.dram_tensor("w2T", [HID_SH, D], F32, kind="ExternalInput").ap()
    masks = nc.dram_tensor("masks", [4, 128, 512], BF16, kind="ExternalInput").ap()
    ident = nc.dram_tensor("ident", [128, 128], BF16, kind="ExternalInput").ap()
    outT = nc.dram_tensor("outT", [D, TC], F32, kind="ExternalOutput").ap()

    lnx_in4 = [nc.dram_tensor("lnx_in%d" % q, [D, 256], BF16) for q in range(4)]
    lnx_out4 = [nc.dram_tensor("lnx_out%d" % q, [4 * D, 256], BF16) for q in range(4)]
    ar_in = nc.dram_tensor("ar_in", [1, 8], F32)
    ar_out = nc.dram_tensor("ar_out", [1, 8], F32)
    t1_in = nc.dram_tensor("t1_in", [D_SH, HIDP], BF16)
    t1_out = nc.dram_tensor("t1_out", [D, HIDP], BF16, addr_space="Shared")
    t3_in = nc.dram_tensor("t3_in", [D_SH, HIDP], BF16)
    t3_out = nc.dram_tensor("t3_out", [D, HIDP], BF16, addr_space="Shared")
    t2_in = nc.dram_tensor("t2_in", [HID_SH, D], BF16)
    t2_out = nc.dram_tensor("t2_out", [HIDP, D], BF16, addr_space="Shared")
    rs_in8 = [nc.dram_tensor("rs_in%d" % k, [4, 256, TC], BF16) for k in range(8)]
    rs_out8 = [nc.dram_tensor("rs_out%d" % k, [256, TC], BF16) for k in range(8)]
    x1_scr = nc.dram_tensor("x1_scr", [D, TC], F32)
    warm_in = nc.dram_tensor("warm_in", [1, 16], F32)
    warm_out = nc.dram_tensor("warm_out", [4, 16], F32)
    warm_out8 = nc.dram_tensor("warm_out8", [8, 16], F32, addr_space="Shared")

    def tapt(name, shape, dtype):
        if name in taps and name not in tap_t:
            tap_t[name] = nc.dram_tensor("tap_" + name, shape, dtype, kind="ExternalOutput").ap()
        return tap_t.get(name)

    grp4 = [[0, 1, 2, 3], [4, 5, 6, 7]]
    grp8 = [list(range(8))]

    with ExitStack() as ectx:
        pb = ectx.enter_context(nc.sbuf_tensor("pb", [128, PB_SZ], BF16))
        pf = ectx.enter_context(nc.sbuf_tensor("pf", [128, PF_SZ], F32))
        ones_fc = ectx.enter_context(nc.sbuf_tensor("ones_fc", [128, 1], F32))
        ones_fr = ectx.enter_context(nc.sbuf_tensor("ones_fr", [1, 128], F32))
        ones_bc = ectx.enter_context(nc.sbuf_tensor("ones_bc", [128, 1], BF16))
        mask_s = ectx.enter_context(nc.sbuf_tensor("mask_s", [128, 4, 512], BF16))
        id_s = ectx.enter_context(nc.sbuf_tensor("id_s", [128, 128], BF16))
        rows = ectx.enter_context(nc.sbuf_tensor("rows", [1, 4608], F32))
        st1 = ectx.enter_context(nc.sbuf_tensor("st1", [128, 16], F32))
        exp_s = ectx.enter_context(nc.sbuf_tensor("exp_s", [128, 2, 512], BF16))
        expm_s = ectx.enter_context(nc.sbuf_tensor("expm_s", [128, 2, 512], BF16))
        vtmp_s = ectx.enter_context(nc.sbuf_tensor("vtmp_s", [128, 512], BF16))
        inv_s = ectx.enter_context(nc.sbuf_tensor("inv_s", [1, 512], F32))
        sil_s = ectx.enter_context(nc.sbuf_tensor("sil_s", [128, 512], BF16))
        qsg_s = ectx.enter_context(nc.sbuf_tensor("qsg_s", [128, 3, 688], BF16))
        qts_s = ectx.enter_context(nc.sbuf_tensor("qts_s", [128, 2, 688], BF16))

        PS = [ectx.enter_context(nc.psum_tensor("ps%d" % i, [128, 512], F32)) for i in range(6)]
        PT = [ectx.enter_context(nc.psum_tensor("pt%d" % i, [128, 1024], BF16)) for i in range(2)]

        C = Ctx()
        for nm in ("dS", "dS2", "dW", "dQ", "dQ2", "dG", "dG2", "dG3", "dR", "dR2", "tE", "vE", "sE", "cC"):
            C.add(nm, ectx.enter_context(nc.semaphore(nm)))
        block = ectx.enter_context(nc.Block())

        lnxown = pb[:, 0:16384].rearrange("p (c t) -> p c t", c=16)
        qt = pb[:, OFF_QT:OFF_QT + 8192].rearrange("p (h t) -> p h t", h=2)
        kt = pb[:, OFF_KT:OFF_KT + 8192].rearrange("p (h t) -> p h t", h=2)
        vt = pb[:, OFF_VT:OFF_VT + 8192].rearrange("p (h k e) -> p h k e", h=2, k=32)
        wqkv = pb[:, OFF_WQKV:OFF_WQKV + 12288].rearrange("p (m h c e) -> p m h c e", m=3, h=2, c=DC)
        lnch = pb[:, OFF_LNCH:OFF_LNCH + 16384].rearrange("p (b c t) -> p b c t", b=2, c=16)
        z_v = pb[:, OFF_Z:OFF_Z + 16384].rearrange("p (h t) -> p h t", h=HL)
        ln2xt = pb[:, OFF_LN2XT:OFF_LN2XT + 16384].rearrange("p (c t) -> p c t", c=16)
        attn_t = pb[:, OFF_ATTN:OFF_ATTN + 16384].rearrange("p (c t) -> p c t", c=16)
        wo_v = pb[:, OFF_WO:OFF_WO + 8192].rearrange("p (h c e) -> p h c e", h=HL, c=DC)
        p2stg = pb[:, OFF_P2STG:OFF_P2STG + 4096].rearrange("p (t c) -> p t c", t=8)
        sqb = pb[:, OFF_SQB:OFF_SQB + 1024]
        gt_v = pb[:, OFF_GT:OFF_GT + NHC * 512].rearrange("p (h t) -> p h t", h=NHC)
        fbuf = pb[:, OFF_FBUF:OFF_FBUF + 16384].rearrange("p (b c h) -> p b c h", b=4, c=16)
        slab = pb[:, OFF_SLAB:OFF_SLAB + 2 * 11 * 512].rearrange("p (b h d) -> p b h d", b=2, h=11)
        fstg = pb[:, OFF_FSTG:OFF_FSTG + 8192].rearrange("p (c t) -> p c t", c=16)
        fsq_v = pb[:, OFF_SIL2:OFF_SIL2 + 512]

        xa = pf[:, PFO_XA:PFO_XA + 1024]
        xb = pf[:, PFO_XB:PFO_XB + 1024]
        fscr = pf[:, PFO_SCR:PFO_SCR + 1024]
        ftmp = pf[:, PFO_TMP:PFO_TMP + 1024]
        qw = [pf[:, PFO_QW:PFO_QW + 688], pf[:, PFO_QW + 688:PFO_QW + 1376]]
        qx = pf[:, PFO_QW:PFO_QW + 1024]
        x1pb = [pf[:, PFO_X1A:PFO_X1A + 512], pf[:, PFO_X1B:PFO_X1B + 512]]
        bcz_s = pf[:, PFO_X1A:PFO_X1A + 512]
        m_bc = pf[:, PFO_X1A:PFO_X1A + 1024]

        r_sum = rows[:, 0:1024]
        r_var = rows[:, 1024:2048]
        r_m = rows[:, 2048:3072]
        r_istd = rows[:, 3072:4096]
        r_misc = rows[:, 4096:4592]
        r_svec = rows[:, 4592:4600]
        r_sld = rows[:, 4600:4608]

        def T(f): block.tensor(f)
        def V(f): block.vector(f)
        def Sc(f): block.scalar(f)
        def Sy(f): block.sync(f)
        def G(f): block.gpsimd(f)

        dG_wqkv = {}

        @G
        def _(g):
            i = g.collective_compute("AllGather", ALU.bypass, replica_groups=grp4,
                                     ins=[warm_in.ap()], outs=[warm_out.ap()])
            C.inc(i, "cC", 1)
            i = g.collective_compute("AllGather", ALU.bypass, replica_groups=grp8,
                                     ins=[warm_in.ap()], outs=[warm_out8.ap()])
            C.inc(i, "cC", 1)
            g.memset(ones_fc[:, :], 1.0)
            g.memset(ones_fr[:, :], 1.0)
            g.memset(ones_bc[:, :], 1.0)
            g.memset(r_svec[:, :], 0.0)
            i = g.dma_start(out=mask_s[:, :, :], in_=masks.rearrange("j p t -> p j t"))
            C.inc(i, "dG", 16)
            i = g.dma_start(out=id_s[:, :], in_=ident[:, :])
            C.inc(i, "dG", 16)
            for m in range(3):
                i = g.dma_start(out=wqkv[:, m, :, :, :], in_=wqkv_in[:, m, 0:2, :, :])
                C.inc(i, "dG", 16)
            dG_wqkv[0] = C.val("dG")
        dG_init = C.val("dG")

        sE_sq = []
        tE_x = []

        for c in range(DC):
            def s_p0(s, c=c):
                C.wait(s, "dS" if c % 2 == 0 else "dS2", 16 * (c // 2 + 1))
                if c >= 1:
                    C.wait(s, "tE", tE_x[c - 1])
                i = s.activation(fscr[:, :], (xa if c % 2 == 0 else xb), AF.Square)
                sE_sq.append(C.inc(i, "sE", 1))
            Sc(s_p0)

            def t_p0(t, c=c):
                if c == 0:
                    C.wait(t, "dG", dG_init)
                xbuf = xa if c % 2 == 0 else xb
                C.wait(t, "dS" if c % 2 == 0 else "dS2", 16 * (c // 2 + 1))
                t.matmul(PS[4][0:1, :], ones_fc[:, :], xbuf[:, 0:512], start=(c == 0), stop=(c == DC - 1))
                t.matmul(PS[5][0:1, :], ones_fc[:, :], xbuf[:, 512:1024], start=(c == 0), stop=(c == DC - 1))
                C.wait(t, "sE", sE_sq[c])
                t.matmul(PS[0][0:1, :], ones_fc[:, :], fscr[:, 0:512], start=(c == 0), stop=(c == DC - 1))
                i = t.matmul(PS[1][0:1, :], ones_fc[:, :], fscr[:, 512:1024], start=(c == 0), stop=(c == DC - 1))
                tE_x.append(C.inc(i, "tE", 1))
            T(t_p0)

        @Sy
        def _(sy):
            for c in range(DC):
                if c >= 2:
                    C.wait(sy, "tE", tE_x[c - 2])
                i = sy.dma_start(out=(xa if c % 2 == 0 else xb), in_=xT[_ts(c, 128), :])
                C.inc(i, "dS" if c % 2 == 0 else "dS2", 16)

        def emit_rowchain(sum_srcs, sq_srcs, width):
            @V
            def _(v):
                C.wait(v, "tE", C.val("tE"))
                i = None
                for j, (bk, sl) in enumerate(sum_srcs):
                    i = v.tensor_copy(r_sum[:, _ts(j, 512)], bk[0:1, sl] if sl is not None else bk)
                for j, (bk, sl) in enumerate(sq_srcs):
                    i = v.tensor_copy(r_var[:, _ts(j, 512)], bk[0:1, sl] if sl is not None else bk)
                C.inc(i, "vE", 1)

            @Sc
            def _(s):
                C.wait(s, "vE", C.val("vE"))
                s.mul(r_m[:, 0:width], r_sum[:, 0:width], 1.0 / D)
                i = s.mul(r_istd[:, 0:width], r_var[:, 0:width], 1.0 / D)
                C.inc(i, "sE", 1)

            @V
            def _(v):
                C.wait(v, "sE", C.val("sE"))
                v.tensor_tensor(r_sum[:, 0:width], r_m[:, 0:width], r_m[:, 0:width], ALU.mult)
                v.tensor_tensor(r_var[:, 0:width], r_istd[:, 0:width], r_sum[:, 0:width], ALU.subtract)
                i = v.tensor_scalar_add(r_istd[:, 0:width], r_var[:, 0:width], LN_EPS)
                C.inc(i, "vE", 1)

            @Sc
            def _(s):
                C.wait(s, "vE", C.val("vE"))
                i = s.activation(r_sum[:, 0:width], r_istd[:, 0:width], AF.Ln)
                C.inc(i, "sE", 1)
                C.wait(s, "sE", C.val("sE"))
                i = s.activation(r_istd[:, 0:width], r_sum[:, 0:width], AF.Exp, scale=-0.5)
                C.inc(i, "sE", 1)

        emit_rowchain([(PS[4], slice(0, 512)), (PS[5], slice(0, 512))],
                      [(PS[0], slice(0, 512)), (PS[1], slice(0, 512))], 1024)

        def emit_bcast_mi():
            @T
            def _(t):
                C.wait(t, "sE", C.val("sE"))
                t.matmul(PS[0][:, :], ones_fr[:, :], r_m[0:1, 0:512], start=True, stop=True)
                t.matmul(PS[1][:, :], ones_fr[:, :], r_m[0:1, 512:1024], start=True, stop=True)
                t.matmul(PS[2][:, :], ones_fr[:, :], r_istd[0:1, 0:512], start=True, stop=True)
                i = t.matmul(PS[3][:, :], ones_fr[:, :], r_istd[0:1, 512:1024], start=True, stop=True)
                C.inc(i, "tE", 1)

        emit_bcast_mi()
        tE_ln1bc = C.val("tE")

        vE_app = []

        @V
        def _(v):
            C.wait(v, "tE", tE_ln1bc)
            for c in range(DC):
                xbuf = xa if c % 2 == 0 else xb
                C.wait(v, "dS" if c % 2 == 0 else "dS2", 16 * (8 + c // 2 + 1))
                i = None
                for hf in range(2):
                    v.tensor_tensor(ftmp[:, _ts(hf, 512)], xbuf[:, _ts(hf, 512)], PS[hf][:, :], ALU.subtract)
                    i = v.tensor_tensor(lnxown[:, c, _ts(hf, 512)], ftmp[:, _ts(hf, 512)], PS[2 + hf][:, :], ALU.mult)
                vE_app.append(C.inc(i, "vE", 1))

        @Sy
        def _(sy):
            for c in range(DC):
                if c >= 2:
                    C.wait(sy, "vE", vE_app[c - 2])
                else:
                    C.wait(sy, "tE", tE_x[DC - 2 + c])
                i = sy.dma_start(out=(xa if c % 2 == 0 else xb), in_=xT[_ts(c, 128), :])
                C.inc(i, "dS" if c % 2 == 0 else "dS2", 16)

        cC_lnxq = {}

        @G
        def _(g):
            C.wait(g, "vE", vE_app[-1])
            for q in range(4):
                i = g.dma_start(out=lnx_in4[q].rearrange("(c p) t -> p c t", p=128),
                                in_=lnxown[:, :, _ts(q, 256)])
                C.inc(i, "dG", 16)
                C.wait(g, "dG", C.val("dG"))
                i = g.collective_compute("AllGather", ALU.bypass, replica_groups=grp4,
                                         ins=[lnx_in4[q].ap()], outs=[lnx_out4[q].ap()])
                cC_lnxq[q] = C.inc(i, "cC", 1)

        qchunks = []
        for mi, wsrc in ((0, w1T), (1, w3T), (2, w2T)):
            if mi < 2:
                for r in range(2):
                    for cc in range(8):
                        qchunks.append((wsrc, r * 128, 128, cc * 688, 688, mi))
            else:
                for r in range(6):
                    nr = 128 if r < 5 else HID_SH - 640
                    for (c0, ncol) in ((0, 688), (688, 688), (1376, 672)):
                        qchunks.append((wsrc, r * 128, nr, c0, ncol, mi))
        NQCH = len(qchunks)
        dQ_q1p = [C.val("dQ"), C.val("dQ2")]

        QEMIT = []
        QA_END = [0]
        vE_qred = []
        qacc_par = {0: 0, 1: 0, 2: 0}
        _first = {0: True, 1: True, 2: True}
        _par = {0: 0, 1: 0, 2: 0}
        sE_qs = {}
        vE_qt = {}
        cC_t = {}
        dG23 = {}

        for j, ch in enumerate(qchunks):
            def q_ld1(sc, j=j, ch=ch):
                wsrc, r0, nr, c0, ncol, mi = ch
                if j >= 2:
                    C.wait(sc, "vE", vE_qred[j - 2])
                i = sc.dma_start(out=qw[j % 2][0:nr, 0:ncol], in_=wsrc[r0:r0 + nr, c0:c0 + ncol])
                C.inc(i, "dQ" if j % 2 == 0 else "dQ2", 16)
            QEMIT.append((Sc, q_ld1))

            def q_red(v, j=j, ch=ch):
                wsrc, r0, nr, c0, ncol, mi = ch
                C.wait(v, "dQ" if j % 2 == 0 else "dQ2", dQ_q1p[j % 2] + 16 * (j // 2 + 1))
                buf = qw[j % 2]
                if _first[mi]:
                    i = v.reduce_sum(st1[:, mi * 2:mi * 2 + 1], buf[0:nr, 0:ncol],
                                     axis=mybir.AxisListType.X, apply_absolute_value=True)
                    _first[mi] = False
                else:
                    v.reduce_sum(st1[0:nr, 6:7], buf[0:nr, 0:ncol],
                                 axis=mybir.AxisListType.X, apply_absolute_value=True)
                    o = _par[mi]
                    _par[mi] ^= 1
                    n2 = _par[mi]
                    v.tensor_copy(st1[:, mi * 2 + n2:mi * 2 + n2 + 1], st1[:, mi * 2 + o:mi * 2 + o + 1])
                    i = v.tensor_tensor(st1[0:nr, mi * 2 + n2:mi * 2 + n2 + 1],
                                        st1[0:nr, mi * 2 + o:mi * 2 + o + 1],
                                        st1[0:nr, 6:7], ALU.add)
                vE_qred.append(C.inc(i, "vE", 1))
                qacc_par[mi] = _par[mi]
            QEMIT.append((V, q_red))

        def q_sum(t):
            C.wait(t, "vE", vE_qred[-1])
            i = None
            for mi in range(3):
                acc = st1[:, mi * 2 + qacc_par[mi]:mi * 2 + qacc_par[mi] + 1]
                i = t.matmul(PS[4][0:1, mi:mi + 1], ones_fc[:, :], acc, start=True, stop=True)
            C.inc(i, "tE", 1)
        QEMIT.append((T, q_sum))

        def q_cp(v):
            C.wait(v, "tE", C.val("tE"))
            i = v.tensor_copy(r_svec[:, 0:3], PS[4][0:1, 0:3])
            C.inc(i, "vE", 1)
        QEMIT.append((V, q_cp))

        def q_ar(g):
            C.wait(g, "vE", C.val("vE"))
            i = g.dma_start(out=ar_in[:, :], in_=r_svec[:, :])
            C.inc(i, "dG", 16)
            C.wait(g, "dG", C.val("dG"))
            i = g.collective_compute("AllReduce", ALU.add, replica_groups=grp8,
                                     ins=[ar_in.ap()], outs=[ar_out.ap()])
            C.inc(i, "cC", 1)
            C.wait(g, "cC", C.val("cC"))
            i = g.dma_start(out=r_sld[:, :], in_=ar_out[:, :])
            C.inc(i, "dG", 16)
        QEMIT.append((G, q_ar))

        def q_smul(sc):
            C.wait(sc, "dG", C.val("dG"))
            i = sc.mul(r_misc[:, 0:3], r_sld[:, 0:3], 0.5 / W_N)
            C.inc(i, "sE", 1)
        QEMIT.append((Sc, q_smul))

        def q_bc(t):
            C.wait(t, "sE", C.val("sE"))
            i = t.matmul(PS[5][:, 0:3], ones_fr[:, :], r_misc[0:1, 0:3], start=True, stop=True)
            C.inc(i, "tE", 1)
        QEMIT.append((T, q_bc))

        vE_sh_h = [0]

        def q_sh(v):
            C.wait(v, "tE", C.val("tE"))
            v.tensor_copy(st1[:, 8:11], PS[5][:, 0:3])
            i = v.tensor_scalar_mul(st1[:, 11:14], PS[5][:, 0:3], -1.0)
            vE_sh_h[0] = C.inc(i, "vE", 1)
        QEMIT.append((V, q_sh))
        QA_END[0] = len(QEMIT)

        dQ_q2p = [None, None]

        def q_p2init(sc):
            dQ_q2p[0] = C.val("dQ")
            dQ_q2p[1] = C.val("dQ2")
            C.wait(sc, "vE", vE_sh_h[0])
            for j in range(2):
                wsrc, r0, nr, c0, ncol, mi = qchunks[j]
                i = sc.dma_start(out=qw[j % 2][0:nr, 0:ncol], in_=wsrc[r0:r0 + nr, c0:c0 + ncol])
                C.inc(i, "dQ" if j % 2 == 0 else "dQ2", 16)
        QEMIT.append((Sc, q_p2init))

        dsts = {0: t1_in, 1: t3_in, 2: t2_in}
        lastj = {0: 15, 1: 31, 2: NQCH - 1}

        _written = set()

        def emit_write(sc, j):
            if j in _written:
                return
            _written.add(j)
            wsrc, r0, nr, c0, ncol, mi = qchunks[j]
            C.wait(sc, "vE", vE_qt[j])
            sem = "dG2" if j % 2 == 0 else "dG3"
            i = sc.dma_start(out=dsts[mi][r0:r0 + nr, c0:c0 + ncol],
                             in_=qts_s[0:nr, j % 2, 0:ncol])
            dG23[j] = (sem, C.inc(i, sem, 16))

        for j, ch in enumerate(qchunks):
            def q_sgn(sc, j=j, ch=ch):
                wsrc, r0, nr, c0, ncol, mi = ch
                if j >= 1:
                    emit_write(sc, j - 1)
                C.wait(sc, "dQ" if j % 2 == 0 else "dQ2", dQ_q2p[j % 2] + 16 * (j // 2 + 1))
                if j >= 1:
                    C.wait(sc, "vE", vE_qt[j - 1])
                sc.activation(qsg_s[0:nr, 0, 0:ncol], qw[j % 2][0:nr, 0:ncol], AF.Sign,
                              bias=st1[0:nr, 11 + mi:12 + mi], scale=1.0)
                i = sc.activation(qsg_s[0:nr, 1, 0:ncol], qw[j % 2][0:nr, 0:ncol], AF.Sign,
                                  bias=st1[0:nr, 8 + mi:9 + mi], scale=1.0)
                sE_qs[j] = C.inc(i, "sE", 1)
                if j + 2 < NQCH:
                    C.wait(sc, "sE", sE_qs[j])
                    w2s, r02, nr2, c02, nc2, _ = qchunks[j + 2]
                    i = sc.dma_start(out=qw[j % 2][0:nr2, 0:nc2], in_=w2s[r02:r02 + nr2, c02:c02 + nc2])
                    C.inc(i, "dQ" if j % 2 == 0 else "dQ2", 16)
            QEMIT.append((Sc, q_sgn))

            def q_tt(v, j=j, ch=ch):
                wsrc, r0, nr, c0, ncol, mi = ch
                C.wait(v, "sE", sE_qs[j])
                if j >= 2:
                    C.wait(v, dG23[j - 2][0], dG23[j - 2][1])
                v.tensor_tensor(qsg_s[0:nr, 2, 0:ncol], qsg_s[0:nr, 0, 0:ncol],
                                qsg_s[0:nr, 1, 0:ncol], ALU.add)
                i = v.tensor_scalar_mul(qts_s[0:nr, j % 2, 0:ncol], qsg_s[0:nr, 2, 0:ncol],
                                        st1[0:nr, 8 + mi:9 + mi])
                vE_qt[j] = C.inc(i, "vE", 1)
            QEMIT.append((V, q_tt))

            if j == lastj[0] or j == lastj[1] or j == lastj[2]:
                def q_ag(g, j=j):
                    mi = {lastj[0]: 0, lastj[1]: 1, lastj[2]: 2}[j]
                    C.wait(g, "dG2", C.val("dG2"))
                    C.wait(g, "dG3", C.val("dG3"))
                    src, dst = {0: (t1_in, t1_out), 1: (t3_in, t3_out), 2: (t2_in, t2_out)}[mi]
                    i = g.collective_compute("AllGather", ALU.bypass, replica_groups=grp8,
                                             ins=[src.ap()], outs=[dst.ap()])
                    cC_t[mi] = C.inc(i, "cC", 1)

                def q_wlast(sc, j=j):
                    emit_write(sc, j)
                QEMIT.append((Sc, q_wlast))
                QEMIT.append((G, q_ag))

        _emitted_last = set()

        def pop_quant(k):
            for _ in range(k):
                if not QEMIT:
                    return
                reg, fn = QEMIT.pop(0)
                QA_END[0] = max(0, QA_END[0] - 1)
                reg(fn)

        def drain_quant_psum_part():
            while QA_END[0] > 0:
                pop_quant(1)

        evac_done = {}
        qkv_done = {}
        dS_chunk = {}
        last_exp_use = {0: 0, 1: 0}
        last_pv = {0: 0, 1: 0}
        last_z_use = {0: 0, 1: 0}
        last_den = [0]
        last_bc = [0]
        vt_use = {0: 0, 1: 0}
        z_done = [0]
        GROUPS = [(0, 0), (1, 0), (2, 0), (0, 1), (1, 1), (2, 1)]

        JORD = [0, 2, 4, 6, 1, 3, 5, 7]
        for pk in range(2):
            for jp in range(8):
                j = JORD[jp]
                jg = pk * 8 + jp

                def sy_mini(sy, j=j, jg=jg):
                    C.wait(sy, "cC", cC_lnxq[2 * (j % 2) + 1])
                    if jg >= 2:
                        C.wait(sy, "tE", qkv_done[jg - 2])
                    blkr = (j // 2) * D
                    sem = "dS" if jg % 2 == 0 else "dS2"
                    for qq in range(2):
                        src = lnx_out4[2 * (j % 2) + qq][blkr:blkr + D, :].rearrange("(c p) t -> p c t", p=128)
                        for hcf in range(2):
                            i = sy.dma_start(out=lnch[:, jg % 2, 8 * hcf:8 * hcf + 8, _ts(qq, 256)],
                                             in_=src[:, 8 * hcf:8 * hcf + 8, :])
                            C.inc(i, sem, 16)
                    dS_chunk[jg] = (sem, C.val(sem))
                Sy(sy_mini)

                for g_i, (m, h) in enumerate(GROUPS):
                    Gg = jg * 6 + g_i

                    def t_mini(t, m=m, h=h, j=j, jg=jg, Gg=Gg, pk=pk):
                        bank = PS[Gg % 2]
                        C.wait(t, dS_chunk[jg][0], dS_chunk[jg][1])
                        C.wait(t, "dG", dG_wqkv[pk])
                        if Gg >= 2 and (Gg - 2) in evac_done:
                            C.wait(t, "vE", evac_done[Gg - 2])
                        i = None
                        for cc in range(DC):
                            i = t.matmul(bank[:, :], wqkv[:, m, h, cc, :], lnch[:, jg % 2, cc, :],
                                         start=(cc == 0), stop=(cc == DC - 1))
                        qkv_done[jg] = C.inc(i, "tE", 1)
                    T(t_mini)

                    if m < 2:
                        def v_mini(v, m=m, h=h, j=j, Gg=Gg):
                            bank = PS[Gg % 2]
                            C.wait(v, "tE", C.val("tE"))
                            i = v.tensor_copy((qt if m == 0 else kt)[:, h, _ts(j, 512)], bank[:, :])
                            evac_done[Gg] = C.inc(i, "vE", 1)
                        V(v_mini)
                        pop_quant(2)
                    else:
                        def v_mini1(v, Gg=Gg):
                            bank = PS[Gg % 2]
                            C.wait(v, "tE", C.val("tE"))
                            i = v.tensor_copy(vtmp_s[:, :], bank[:, :])
                            C.inc(i, "vE", 1)
                        V(v_mini1)

                        vpar = jg % 2

                        def t_mini2(t, vpar=vpar, jg=jg):
                            C.wait(t, "vE", C.val("vE"))
                            if vt_use[vpar]:
                                C.wait(t, "vE", vt_use[vpar])
                            i = None
                            for q in range(4):
                                i = t.transpose(PT[vpar][:, _ts(q, 128)], vtmp_s[:, _ts(q, 128)], id_s[:, :])
                            qkv_done[jg] = C.inc(i, "tE", 1)
                        T(t_mini2)

                        def v_mini2(v, h=h, j=j, Gg=Gg, vpar=vpar):
                            C.wait(v, "tE", C.val("tE"))
                            dst = vt[:, h, 4 * j:4 * j + 4, :].rearrange("p k e -> p (k e)")
                            i = v.tensor_copy(dst, PT[vpar][:, 0:512])
                            vt_use[vpar] = C.inc(i, "vE", 1)
                            evac_done[Gg] = C.val("vE")
                        V(v_mini2)
                        pop_quant(2)

            if pk == 0:
                def g_wq1(g):
                    C.wait(g, "tE", qkv_done[7])
                    for m in range(3):
                        i = g.dma_start(out=wqkv[:, m, :, :, :], in_=wqkv_in[:, m, 2:4, :, :])
                        C.inc(i, "dG", 16)
                    dG_wqkv[1] = C.val("dG")
                G(g_wq1)

            drain_quant_psum_part()
            vE_pass_evacs = evac_done[(pk * 8 + 7) * 6 + 5]

            for hl in range(2):
                hglob = 2 * pk + hl
                for qc in range(NQ):
                    nk = 4 * qc + 4
                    tS = {}

                    def t_pre(t, hl=hl, qc=qc, nk=nk, tS=tS, vE_pass_evacs=vE_pass_evacs):
                        C.wait(t, "vE", vE_pass_evacs)
                        if last_den[0]:
                            C.wait(t, "vE", last_den[0])
                        if last_z_use[qc % 2]:
                            C.wait(t, "vE", last_z_use[qc % 2])
                        for kc in range(min(2, nk)):
                            if last_exp_use[kc % 2]:
                                C.wait(t, "sE", last_exp_use[kc % 2])
                            i = t.matmul(PS[kc % 2][:, :], kt[:, hl, _ts(kc, 128)],
                                         qt[:, hl, _ts(qc, 512)], start=True, stop=True)
                            tS[kc] = C.inc(i, "tE", 1)
                    T(t_pre)

                    for kc in range(nk):
                        diag = kc >= nk - 4
                        eK = [0]
                        mK = [0]

                        def s_exp(s, kc=kc, tS=tS, eK=eK):
                            C.wait(s, "tE", tS[kc])
                            if last_pv[kc % 2]:
                                C.wait(s, "tE", last_pv[kc % 2])
                            i = s.activation(exp_s[:, kc % 2, :], PS[kc % 2][:, :], AF.Exp, scale=QSCL)
                            eK[0] = C.inc(i, "sE", 1)
                            last_exp_use[kc % 2] = eK[0]
                        Sc(s_exp)

                        if diag:
                            def v_mask(v, kc=kc, nk=nk, eK=eK, mK=mK):
                                C.wait(v, "sE", eK[0])
                                if last_pv[kc % 2]:
                                    C.wait(v, "tE", last_pv[kc % 2])
                                jj = kc - (nk - 4)
                                i = v.tensor_tensor(expm_s[:, kc % 2, :], exp_s[:, kc % 2, :],
                                                    mask_s[:, jj, :], ALU.mult)
                                mK[0] = C.inc(i, "vE", 1)
                            V(v_mask)

                        def t_pv(t, hl=hl, qc=qc, kc=kc, nk=nk, diag=diag, eK=eK, mK=mK, tS=tS):
                            if diag:
                                C.wait(t, "vE", mK[0])
                                src = expm_s[:, kc % 2, :]
                            else:
                                C.wait(t, "sE", eK[0])
                                src = exp_s[:, kc % 2, :]
                            i = t.matmul(PS[2 + qc % 2][:, :], vt[:, hl, kc, :], src,
                                         start=(kc == 0), stop=(kc == nk - 1))
                            C.inc(i, "tE", 1)
                            i = t.matmul(PS[4][0:1, :], ones_bc[:, :], src,
                                         start=(kc == 0), stop=(kc == nk - 1))
                            last_pv[kc % 2] = C.inc(i, "tE", 1)
                            if kc + 2 < nk:
                                C.wait(t, "sE", last_exp_use[kc % 2])
                                i = t.matmul(PS[kc % 2][:, :], kt[:, hl, _ts(kc + 2, 128)],
                                             qt[:, hl, _ts(qc, 512)], start=True, stop=True)
                                tS[kc + 2] = C.inc(i, "tE", 1)
                        T(t_pv)

                    def v_recip(v):
                        C.wait(v, "tE", C.val("tE"))
                        i = v.reciprocal(inv_s[:, :], PS[4][0:1, :])
                        last_den[0] = C.inc(i, "vE", 1)
                    V(v_recip)

                    def t_bc(t):
                        C.wait(t, "vE", last_den[0])
                        if last_bc[0]:
                            C.wait(t, "vE", last_bc[0])
                        i = t.matmul(PS[5][:, :], ones_fr[:, :], inv_s[0:1, :], start=True, stop=True)
                        C.inc(i, "tE", 1)
                    T(t_bc)

                    def v_zmul(v, hglob=hglob, qc=qc):
                        C.wait(v, "tE", C.val("tE"))
                        v.tensor_copy(bcz_s[:, :], PS[5][:, :])
                        i = v.tensor_tensor(z_v[:, hglob, _ts(qc, 512)], PS[2 + qc % 2][:, :],
                                            bcz_s[:, :], ALU.mult)
                        last_z_use[qc % 2] = C.inc(i, "vE", 1)
                        last_bc[0] = C.val("vE")
                        z_done[0] = C.val("vE")
                    V(v_zmul)
                    pop_quant(3)

        vE_zall = z_done[0]

        pop_quant(10 ** 9)

        dG_wo = [0]

        @G
        def _(g):
            C.wait(g, "tE", qkv_done[15])
            i = g.dma_start(out=wo_v[:, :, :, :], in_=wo_in[:, :, :, :])
            dG_wo[0] = C.inc(i, "dG", 16)

        p2_evac = {}
        st_p2 = {}
        cC_rs = {}

        for dc in range(DC):
            for grp, t5s in ((0, (0, 1, 2, 3, 4, 5)), (1, (6, 7))):
                def t_p2(t, dc=dc, grp=grp, t5s=t5s):
                    C.wait(t, "dG", dG_wo[0])
                    C.wait(t, "vE", vE_zall)
                    prev = (dc, 0) if grp == 1 else (dc - 1, 1)
                    if prev in p2_evac:
                        C.wait(t, "vE", p2_evac[prev])
                    i = None
                    for he in range(4):
                        for bi, t5 in enumerate(t5s):
                            i = t.matmul(PS[bi][:, :], wo_v[:, he, dc, :],
                                         z_v[:, he, _ts(t5, 512)], start=(he == 0), stop=(he == 3))
                    C.inc(i, "tE", 1)
                T(t_p2)

                def v_p2(v, dc=dc, grp=grp, t5s=t5s):
                    C.wait(v, "tE", C.val("tE"))
                    if grp == 0 and dc >= 1:
                        C.wait(v, *st_p2[dc - 1])
                    i = None
                    for bi, t5 in enumerate(t5s):
                        i = v.tensor_copy(p2stg[:, t5, :], PS[bi][:, :])
                    p2_evac[(dc, grp)] = C.inc(i, "vE", 1)
                V(v_p2)

            def sy_p2(sy, dc=dc):
                k = dc // 2
                C.wait(sy, "vE", p2_evac[(dc, 1)])
                if dc % 2 == 0 and k >= 2:
                    C.wait(sy, "cC", cC_rs[k - 2])
                rsv = rs_in8[k].rearrange("s (c p) (h t) -> p c s h t", p=128, t=512)
                i = sy.dma_start(out=rsv[:, dc % 2, :, :, :],
                                 in_=p2stg.rearrange("p (s h) t -> p s h t", s=4))
                st_p2[dc] = ("dR" if k % 2 == 0 else "dR2", C.inc(i, "dR" if k % 2 == 0 else "dR2", 16))
            Sy(sy_p2)

            if dc % 2 == 1:
                def g_rs(g, k=dc // 2):
                    C.wait(g, *st_p2[2 * k + 1])
                    i = g.collective_compute("ReduceScatter", ALU.add, replica_groups=grp4,
                                             ins=[rs_in8[k].ap()], outs=[rs_out8[k].ap()])
                    cC_rs[k] = C.inc(i, "cC", 1)
                G(g_rs)

        at_ld = {}
        tE_asq = {}

        def sy_at(sy, k):
            C.wait(sy, "cC", cC_rs[k])
            if k >= 2:
                C.wait(sy, "tE", tE_asq[2 * (k - 2) + 1])
            sem = "dS" if k % 2 == 0 else "dS2"
            i0 = sy.dma_start(out=attn_t[:, 2 * k, :], in_=rs_out8[k][0:128, :])
            C.inc(i0, sem, 16)
            i1 = sy.dma_start(out=attn_t[:, 2 * k + 1, :], in_=rs_out8[k][128:256, :])
            C.inc(i1, sem, 16)
            at_ld[k] = (sem, C.val(sem))

        for c in range(DC):
            if c % 2 == 0:
                Sy(lambda sy, k=c // 2: sy_at(sy, k))
            def s_asq(s, c=c):
                C.wait(s, *at_ld[c // 2])
                if c >= 1:
                    C.wait(s, "tE", tE_asq[c - 1])
                i = s.activation(sqb[:, :], attn_t[:, c, :], AF.Square)
                C.inc(i, "sE", 1)
            Sc(s_asq)

            def t_asq(t, c=c):
                C.wait(t, *at_ld[c // 2])
                t.matmul(PS[4][0:1, :], ones_bc[:, :], attn_t[:, c, 0:512], start=(c == 0), stop=(c == DC - 1))
                t.matmul(PS[5][0:1, :], ones_bc[:, :], attn_t[:, c, 512:1024], start=(c == 0), stop=(c == DC - 1))
                C.wait(t, "sE", C.val("sE"))
                t.matmul(PS[0][0:1, :], ones_bc[:, :], sqb[:, 0:512], start=(c == 0), stop=(c == DC - 1))
                i = t.matmul(PS[1][0:1, :], ones_bc[:, :], sqb[:, 512:1024], start=(c == 0), stop=(c == DC - 1))
                tE_asq[c] = C.inc(i, "tE", 1)
            T(t_asq)

        emit_rowchain([(PS[4], slice(0, 512)), (PS[5], slice(0, 512))],
                      [(PS[0], slice(0, 512)), (PS[1], slice(0, 512))], 1024)
        emit_bcast_mi()
        tE_abc = C.val("tE")

        @V
        def _(v):
            C.wait(v, "tE", tE_abc)
            v.tensor_copy(m_bc[:, 0:512], PS[0][:, :])
            i = v.tensor_copy(m_bc[:, 512:1024], PS[1][:, :])
            C.inc(i, "vE", 1)


        x2ld = {}

        @Sc
        def _(s):
            C.wait(s, "vE", vE_app[-1])
            for c in range(2):
                sem = "dQ" if c % 2 == 0 else "dQ2"
                i = s.dma_start(out=(xa if c % 2 == 0 else xb), in_=xT[_ts(c, 128), :])
                x2ld[c] = (sem, C.inc(i, sem, 16))

        vE_x1 = {}
        dS_x1w = {}
        sE_xsq = {}
        tE_xst = {}
        for c in range(DC):
            def v_x1(v, c=c):
                xbuf = xa if c % 2 == 0 else xb
                C.wait(v, "tE", tE_abc)
                C.wait(v, *at_ld[c // 2])
                C.wait(v, x2ld[c][0], x2ld[c][1])
                if c >= 1:
                    C.wait(v, "dW", dS_x1w[c - 1])
                    C.wait(v, "tE", tE_xst[c - 1])
                i = None
                for hf in range(2):
                    v.tensor_tensor(ftmp[:, _ts(hf, 512)], attn_t[:, c, _ts(hf, 512)], m_bc[:, _ts(hf, 512)], ALU.subtract)
                    v.tensor_tensor(qx[:, _ts(hf, 512)], ftmp[:, _ts(hf, 512)], PS[2 + hf][:, :], ALU.mult)
                    i = v.tensor_tensor(fscr[:, _ts(hf, 512)], qx[:, _ts(hf, 512)],
                                        xbuf[:, _ts(hf, 512)], ALU.add)
                vE_x1[c] = C.inc(i, "vE", 1)
            V(v_x1)

            def sy_x1(sy, c=c):
                C.wait(sy, "vE", vE_x1[c])
                i = sy.dma_start(out=x1_scr[_ts(c, 128), :], in_=fscr[:, :])
                dS_x1w[c] = C.inc(i, "dW", 16)
            Sy(sy_x1)

            def s_xsq(s, c=c):
                C.wait(s, "vE", vE_x1[c])
                if c >= 1:
                    C.wait(s, "tE", tE_xst[c - 1])
                i = s.activation(sqb[:, :], fscr[:, :], AF.Square)
                sE_xsq[c] = C.inc(i, "sE", 1)
            Sc(s_xsq)

            def t_xst(t, c=c):
                C.wait(t, "vE", vE_x1[c])
                t.matmul(PS[4][0:1, :], ones_fc[:, :], fscr[:, 0:512], start=(c == 0), stop=(c == DC - 1))
                t.matmul(PS[5][0:1, :], ones_fc[:, :], fscr[:, 512:1024], start=(c == 0), stop=(c == DC - 1))
                C.wait(t, "sE", sE_xsq[c])
                t.matmul(PS[0][0:1, :], ones_bc[:, :], sqb[:, 0:512], start=(c == 0), stop=(c == DC - 1))
                i = t.matmul(PS[1][0:1, :], ones_bc[:, :], sqb[:, 512:1024], start=(c == 0), stop=(c == DC - 1))
                tE_xst[c] = C.inc(i, "tE", 1)
            T(t_xst)

            def s_x1(s, c=c):
                if c + 2 < DC:
                    C.wait(s, "vE", vE_x1[c])
                    sem = "dQ" if c % 2 == 0 else "dQ2"
                    i = s.dma_start(out=(xa if c % 2 == 0 else xb), in_=xT[_ts(c + 2, 128), :])
                    x2ld[c + 2] = (sem, C.inc(i, sem, 16))
            Sc(s_x1)

        emit_rowchain([(PS[4], slice(0, 512)), (PS[5], slice(0, 512))],
                      [(PS[0], slice(0, 512)), (PS[1], slice(0, 512))], 1024)
        emit_bcast_mi()
        tE_l2bc = C.val("tE")

        l2ld = {}
        tE_pb = C.val("tE")

        @Sc
        def _(s):
            C.wait(s, "tE", tE_pb)
            for c in range(2):
                sem = "dQ" if c % 2 == 0 else "dQ2"
                i = s.dma_start(out=(xa if c % 2 == 0 else xb), in_=x1_scr[_ts(c, 128), :])
                l2ld[c] = (sem, C.inc(i, sem, 16))

        vE_l2 = {}
        for c in range(DC):
            def v_l2(v, c=c):
                xbuf = xa if c % 2 == 0 else xb
                C.wait(v, "tE", tE_l2bc)
                C.wait(v, l2ld[c][0], l2ld[c][1])
                i = None
                for hf in range(2):
                    v.tensor_tensor(fscr[:, _ts(hf, 512)], xbuf[:, _ts(hf, 512)], PS[hf][:, :], ALU.subtract)
                    i = v.tensor_tensor(ln2xt[:, c, _ts(hf, 512)], fscr[:, _ts(hf, 512)], PS[2 + hf][:, :], ALU.mult)
                vE_l2[c] = C.inc(i, "vE", 1)
            V(v_l2)

            def s_l2(s, c=c):
                if c + 2 < DC:
                    C.wait(s, "vE", vE_l2[c])
                    sem = "dQ" if c % 2 == 0 else "dQ2"
                    i = s.dma_start(out=(xa if c % 2 == 0 else xb), in_=x1_scr[_ts(c + 2, 128), :])
                    l2ld[c + 2] = (sem, C.inc(i, sem, 16))
            Sc(s_l2)
        vE_ln2 = C.val("vE")

        HCP = [(2 * i, 2) for i in range(21)] + [(42, 1)]
        out_store = [0]
        vE_out_prev = [0]
        gt_tap_gate = [0]
        gt_done = {}
        sE_sil = {}
        h_mm = {}
        w2_done = {}
        w2sub_done = {}
        fstg_evac = {}

        def megabarrier():
            for eng, name in ((T, "tensor"),):
                def bar_t(t):
                    C.wait(t, "vE", C.val("vE"))
                    C.wait(t, "sE", C.val("sE"))
                    C.wait(t, "dS", C.val("dS"))
                    C.wait(t, "dQ", C.val("dQ"))
                    C.wait(t, "dG", C.val("dG"))
                    i = t.matmul(PS[5][0:1, 0:1], ones_fc[0:1, :], ones_fc[0:1, 0:1], start=True, stop=True)
                    C.inc(i, "tE", 1)
                T(bar_t)
            def bar_v(v):
                C.wait(v, "tE", C.val("tE"))
                i = v.tensor_copy(st1[:, 15:16], st1[:, 14:15])
                C.inc(i, "vE", 1)
            V(bar_v)
            def bar_s(s):
                C.wait(s, "vE", C.val("vE"))
                i = s.mul(st1[:, 15:16], st1[:, 15:16], 1.0)
                C.inc(i, "sE", 1)
            Sc(bar_s)
            def bar_sy(sy):
                C.wait(sy, "sE", C.val("sE"))
            Sy(bar_sy)
            def bar_g(g):
                C.wait(g, "sE", C.val("sE"))
            G(bar_g)

        MEGABAR = False
        for th in range(2):
            tsl = _ts(th, 512)
            if MEGABAR:
                megabarrier()
            dS_fb = {}
            for pi, (hc0, nn) in enumerate(HCP):
                def sy_fb(sy, pi=pi, hc0=hc0, nn=nn, th=th):
                    C.wait(sy, "cC", cC_t[1])
                    if th == 0 and pi < 2:
                        C.wait(sy, "vE", vE_ln2)
                    if pi >= 2:
                        C.wait(sy, "tE", h_mm[(th, pi - 2, 1)])
                    elif th == 1 and pi < 2:
                        C.wait(sy, "tE", h_mm[(0, len(HCP) - 2 + pi, 1)])
                        C.wait(sy, "tE", w2_done[(0, 3)])
                    sem = "dS" if pi % 2 == 0 else "dS2"
                    for m, tsrc in ((0, t1_out), (1, t3_out)):
                        src = tsrc[:, hc0 * 128:(hc0 + nn) * 128].rearrange("(c p) h -> p c h", p=128)
                        for q in range(4):
                            i = sy.dma_start(out=fbuf[:, m * 2 + pi % 2, _ts(q, 4), 0:nn * 128],
                                             in_=src[:, _ts(q, 4), :])
                            C.inc(i, sem, 16)
                    dS_fb[pi] = (sem, C.val(sem))
                Sy(sy_fb)

                for li in range(nn):
                    hc = hc0 + li

                    def t_h(t, pi=pi, li=li, hc=hc, th=th):
                        C.wait(t, dS_fb[pi][0], dS_fb[pi][1])
                        if th == 1 and hc < 2 and gt_tap_gate[0]:
                            C.wait(t, "dG", gt_tap_gate[0])
                        if th == 0 and hc < 2:
                            C.wait(t, "vE", vE_ln2)
                        if th == 1 and hc < 2 and (0, 3) in fstg_evac:
                            C.wait(t, "vE", fstg_evac[(0, 3)])
                        if hc >= 2:
                            C.wait(t, "vE", gt_done[(th, hc - 2)])
                        elif th == 1:
                            C.wait(t, "vE", gt_done[(0, NHC - 2 + hc)])
                        i1 = None
                        i3 = None
                        for cc in range(DC):
                            i1 = t.matmul(PS[hc % 2][:, :], fbuf[:, 0 + pi % 2, cc, _ts(li, 128)],
                                          ln2xt[:, cc, _ts(th, 512)], start=(cc == 0), stop=(cc == DC - 1))
                        h_mm[(th, pi, 0)] = C.inc(i1, "tE", 1)
                        for cc in range(DC):
                            i3 = t.matmul(PS[2 + hc % 2][:, :], fbuf[:, 2 + pi % 2, cc, _ts(li, 128)],
                                          ln2xt[:, cc, _ts(th, 512)], start=(cc == 0), stop=(cc == DC - 1))
                        h_mm[(th, pi, 1)] = C.inc(i3, "tE", 1)
                    T(t_h)

                    def s_sil(s, pi=pi, hc=hc, th=th):
                        C.wait(s, "tE", h_mm[(th, pi, 1)])
                        if hc >= 1:
                            C.wait(s, "vE", gt_done[(th, hc - 1)])
                        elif th == 1:
                            C.wait(s, "vE", gt_done[(0, NHC - 1)])
                        i = s.activation(sil_s[:, :], PS[hc % 2][:, :], AF.Silu)
                        sE_sil[(th, hc)] = C.inc(i, "sE", 1)
                    Sc(s_sil)

                    def v_g(v, hc=hc, th=th):
                        C.wait(v, "sE", sE_sil[(th, hc)])
                        C.wait(v, "tE", C.val("tE"))
                        i = v.tensor_tensor(gt_v[:, hc, :], sil_s[:, :], PS[2 + hc % 2][:, :], ALU.mult)
                        gt_done[(th, hc)] = C.inc(i, "vE", 1)
                    V(v_g)

            if th == 0 and "gt" in taps:
                tpg = tapt("gt", [128, NHC, 512], BF16)

                @G
                def _(g):
                    C.wait(g, "vE", gt_done[(0, NHC - 1)])
                    i = g.dma_start(out=tpg[:, :, :], in_=gt_v[:, :, :])
                    gt_tap_gate[0] = C.inc(i, "dG", 16)
            if th == 0 and "fb" in taps:
                tpf = tapt("fb", [128, 4, 16, 256], BF16)

                @G
                def _(g):
                    C.wait(g, "vE", gt_done[(0, NHC - 1)])
                    i = g.dma_start(out=tpf[:, :, :, :], in_=fbuf[:, :, :, :])
                    gt_tap_gate[0] = C.inc(i, "dG", 16)

            if MEGABAR:
                megabarrier()
            dQ_sl = {}
            SUBH = [(0, 11), (11, 11), (22, 11), (33, 10)]
            for g in range(16):
                dcg, si = g // 4, g % 4
                sh0, snh = SUBH[si]

                def s_slab(s, g=g, dcg=dcg, sh0=sh0, snh=snh, th=th):
                    if g == 0:
                        C.wait(s, "cC", cC_t[2])
                        if th == 0:
                            C.wait(s, "vE", vE_ln2)
                    if g <= 1:
                        C.wait(s, "tE", h_mm[(th, len(HCP) - 1, 1)])
                    if g >= 2:
                        C.wait(s, "tE", w2sub_done[(th, g - 2)])
                    elif th == 1:
                        C.wait(s, "tE", w2sub_done[(0, 14 + g)])
                    src = t2_out[:, _ts(dcg, 512)]
                    sem = "dQ" if g % 2 == 0 else "dQ2"
                    i = s.dma_start(out=slab[:, g % 2, 0:snh, :],
                                    in_=src[sh0 * 128:(sh0 + snh) * 128, :].rearrange(
                                        "(h p) d -> p h d", p=128))
                    C.inc(i, sem, 16)
                    dQ_sl[(th, g)] = (sem, C.val(sem))
                Sc(s_slab)

                def t_w2s(t, g=g, dcg=dcg, si=si, sh0=sh0, snh=snh, th=th):
                    if si == 0:
                        C.wait(t, "vE", gt_done[(th, NHC - 1)])
                        if th == 1 and dcg == 0 and vE_out_prev[0]:
                            C.wait(t, "vE", vE_out_prev[0])
                        if (th, dcg - 1) in fstg_evac:
                            C.wait(t, "vE", fstg_evac[(th, dcg - 1)])
                        elif th == 1 and dcg == 0:
                            C.wait(t, "vE", fstg_evac[(0, 3)])
                    C.wait(t, dQ_sl[(th, g)][0], dQ_sl[(th, g)][1])
                    i = None
                    for hl in range(snh):
                        hc = sh0 + hl
                        for dcl in range(4):
                            i = t.matmul(PS[dcl][:, :], slab[:, g % 2, hl, _ts(dcl, 128)],
                                         gt_v[:, hc, :], start=(hc == 0), stop=(hc == NHC - 1))
                    w2sub_done[(th, g)] = C.inc(i, "tE", 1)
                    if si == 3:
                        w2_done[(th, dcg)] = w2sub_done[(th, g)]
                T(t_w2s)

                if si == 3:
                    def v_w2(v, dcg=dcg, th=th):
                        C.wait(v, "tE", w2_done[(th, dcg)])
                        i = None
                        for dcl in range(4):
                            i = v.tensor_copy(fstg[:, dcg * 4 + dcl, :], PS[dcl][:, :])
                        fstg_evac[(th, dcg)] = C.inc(i, "vE", 1)
                    V(v_w2)

            tE_fsq = {}
            for c in range(DC):
                def s_fsq(s, c=c, th=th):
                    if th == 1 and c == 0 and vE_out_prev[0]:
                        C.wait(s, "vE", vE_out_prev[0])
                    C.wait(s, "vE", fstg_evac[(th, c // 4)])
                    if c >= 1:
                        C.wait(s, "tE", tE_fsq[c - 1])
                    i = s.activation(fsq_v[:, :], fstg[:, c, :], AF.Square)
                    C.inc(i, "sE", 1)
                Sc(s_fsq)

                def t_fsq(t, c=c, th=th):
                    if th == 1 and c == 0 and vE_out_prev[0]:
                        C.wait(t, "vE", vE_out_prev[0])
                    C.wait(t, "vE", fstg_evac[(th, c // 4)])
                    t.matmul(PS[4][0:1, :], ones_bc[:, :], fstg[:, c, :], start=(c == 0), stop=(c == DC - 1))
                    C.wait(t, "sE", C.val("sE"))
                    i = t.matmul(PS[5][0:1, :], ones_bc[:, :], fsq_v[:, :], start=(c == 0), stop=(c == DC - 1))
                    tE_fsq[c] = C.inc(i, "tE", 1)
                T(t_fsq)

            emit_rowchain([(PS[4], slice(0, 512))], [(PS[5], slice(0, 512))], 512)

            def t_fbc(t):
                C.wait(t, "sE", C.val("sE"))
                t.matmul(PS[4][:, :], ones_fr[:, :], r_m[0:1, 0:512], start=True, stop=True)
                i = t.matmul(PS[5][:, :], ones_fr[:, :], r_istd[0:1, 0:512], start=True, stop=True)
                C.inc(i, "tE", 1)
            T(t_fbc)
            tE_fbc = C.val("tE")

            x1ld = {}

            @Sc
            def _(s, th=th):
                if th == 1 and vE_out_prev[0]:
                    C.wait(s, "vE", vE_out_prev[0])
                C.wait(s, "tE", w2_done[(th, 3)])
                for c in range(2):
                    sem = "dQ" if c % 2 == 0 else "dQ2"
                    i = s.dma_start(out=x1pb[c % 2], in_=x1_scr[_ts(c, 128), _ts(th, 512)])
                    x1ld[c] = (sem, C.inc(i, sem, 16))

            vE_o = {}
            dS_o = {}
            for c in range(DC):
                def v_out(v, c=c, th=th):
                    C.wait(v, "tE", tE_fbc)
                    C.wait(v, x1ld[c][0], x1ld[c][1])
                    if c >= 1:
                        C.wait(v, "dW", dS_o[c - 1])
                    v.tensor_tensor(fscr[:, 0:512], fstg[:, c, :], PS[4][:, :], ALU.subtract)
                    v.tensor_tensor(ftmp[:, 0:512], fscr[:, 0:512], PS[5][:, :], ALU.mult)
                    i = v.tensor_tensor(fscr[:, 0:512], ftmp[:, 0:512], x1pb[c % 2], ALU.add)
                    vE_o[c] = C.inc(i, "vE", 1)
                V(v_out)

                def sy_out(sy, c=c, th=th):
                    C.wait(sy, "vE", vE_o[c])
                    i = sy.dma_start(out=outT[_ts(c, 128), _ts(th, 512)], in_=fscr[:, 0:512])
                    dS_o[c] = C.inc(i, "dW", 16)
                    out_store[0] = dS_o[c]
                Sy(sy_out)

                def s_xr(s, c=c, th=th):
                    if c + 2 < DC:
                        C.wait(s, "vE", vE_o[c])
                        sem = "dQ" if c % 2 == 0 else "dQ2"
                        i = s.dma_start(out=x1pb[c % 2], in_=x1_scr[_ts(c + 2, 128), _ts(th, 512)])
                        x1ld[c + 2] = (sem, C.inc(i, sem, 16))
                Sc(s_xr)

            vE_out_prev[0] = vE_o[DC - 1]

        if "w2p" in taps:
            tpw = tapt("w2p", [128, 16, 512], F32)

            @G
            def _(g):
                C.wait(g, "vE", fstg_evac[(0, 3)])
                i = g.dma_start(out=tpw[:, :, :], in_=fstg[:, :, :])
                C.inc(i, "dG", 16)
        if "slb" in taps:
            tps = tapt("slb", [128, 2, 11, 512], BF16)

            @G
            def _(g):
                C.wait(g, "tE", w2sub_done[(0, 1)])
                i = g.dma_start(out=tps[:, :, :, :], in_=slab[:, :, :, :])
                C.inc(i, "dG", 16)
        if "lnx" in taps:
            tp = tapt("lnx", [4 * D, TC], BF16)

            @G
            def _(g):
                C.wait(g, "cC", cC_lnx)
                i = g.dma_start(out=tp[:, :], in_=lnx_out.ap())
                C.inc(i, "dG", 16)
        if "t1" in taps:
            tp = tapt("t1", [D, HIDP], BF16)

            @G
            def _(g):
                C.wait(g, "cC", cC_t[0])
                i = g.dma_start(out=tp[:, :], in_=t1_out.ap())
                C.inc(i, "dG", 16)
        if "t3" in taps:
            tp = tapt("t3", [D, HIDP], BF16)

            @G
            def _(g):
                C.wait(g, "cC", cC_t[1])
                i = g.dma_start(out=tp[:, :], in_=t3_out.ap())
                C.inc(i, "dG", 16)
        if "t3in" in taps:
            tp = tapt("t3in", [D_SH, HIDP], BF16)

            @G
            def _(g):
                C.wait(g, "cC", cC_t[1])
                i = g.dma_start(out=tp[:, :], in_=t3_in.ap())
                C.inc(i, "dG", 16)
        if "t2" in taps:
            tp = tapt("t2", [HIDP, D], BF16)

            @G
            def _(g):
                C.wait(g, "cC", cC_t[2])
                i = g.dma_start(out=tp[:, :], in_=t2_out.ap())
                C.inc(i, "dG", 16)
        if "ln2" in taps:
            tp = tapt("ln2", [128, 16, TC], BF16)

            @G
            def _(g):
                C.wait(g, "vE", vE_ln2)
                i = g.dma_start(out=tp[:, :, :], in_=ln2xt[:, :, :])
                C.inc(i, "dG", 16)
        if "z" in taps:
            tp = tapt("z", [128, HL, S], BF16)

            @G
            def _(g):
                C.wait(g, "vE", vE_zall)
                i = g.dma_start(out=tp[:, :, :], in_=z_v[:, :, :])
                C.inc(i, "dG", 16)
        if "rs" in taps:
            tp = tapt("rs", [D, TC], BF16)

            @G
            def _(g):
                C.wait(g, "cC", cC_rs[7])
                for k in range(8):
                    i = g.dma_start(out=tp[_ts(k, 256), :], in_=rs_out8[k].ap())
                    C.inc(i, "dG", 16)
        if "x1" in taps:
            tp = tapt("x1", [D, TC], F32)

            @G
            def _(g):
                C.wait(g, "dW", dS_x1w[DC - 1])
                i = g.dma_start(out=tp[:, :], in_=x1_scr.ap())
                C.inc(i, "dG", 16)
        if "qkv" in taps:
            tpq = tapt("qkv", [128, 3, 2, S], BF16)

            @G
            def _(g):
                C.wait(g, "vE", evac_done[47])
                i = g.dma_start(out=tpq[:, 0, :, :], in_=qt[:, :, :])
                C.inc(i, "dG", 16)
                i = g.dma_start(out=tpq[:, 1, :, :], in_=kt[:, :, :])
                C.inc(i, "dG", 16)
                i = g.dma_start(out=tpq[:, 2, :, :],
                                in_=vt.rearrange("p h k e -> p h (k e)"))
                C.inc(i, "dG", 16)

        if "t1b" in taps:
            tp = tapt("t1b", [D, HIDP], BF16)

            @G
            def _(g):
                C.wait(g, "dW", out_store[0])
                i = g.dma_start(out=tp[:, :], in_=t1_out.ap())
                C.inc(i, "dG", 16)
        if "ln2b" in taps:
            tp = tapt("ln2b", [128, 16, TC], BF16)

            @G
            def _(g):
                C.wait(g, "dW", out_store[0])
                i = g.dma_start(out=tp[:, :, :], in_=ln2xt[:, :, :])
                C.inc(i, "dG", 16)

        @Sy
        def _(sy):
            for nm in ("dS", "dS2", "dW", "dQ", "dQ2", "dG", "dG2", "dG3"):
                C.wait(sy, nm, C.val(nm))

    return tap_t



def _mk_masks():
    kl = np.arange(128)[:, None]
    ql = np.arange(512)[None, :]
    return np.stack([(128 * j + kl <= ql) for j in range(4)]).astype(ml_dtypes.bfloat16)


def make_core_inputs(inp):
    x = np.asarray(inp["L0_attn_x"], dtype=np.float32).reshape(B * S, D)
    wq = np.asarray(inp["L0_attn_W_Q"], dtype=np.float32)
    wk = np.asarray(inp["L0_attn_W_K"], dtype=np.float32)
    wv = np.asarray(inp["L0_attn_W_V"], dtype=np.float32)
    wo = np.asarray(inp["L0_attn_W_O"], dtype=np.float32)
    w1 = np.asarray(inp["L0_ffn_w1"], dtype=np.float32)
    w3 = np.asarray(inp["L0_ffn_w3"], dtype=np.float32)
    w2 = np.asarray(inp["L0_ffn_w2"], dtype=np.float32)

    w1p = np.zeros((HIDP, D), np.float32); w1p[:HID] = w1
    w3p = np.zeros((HIDP, D), np.float32); w3p[:HID] = w3
    w2p = np.zeros((D, HIDP), np.float32); w2p[:, :HID] = w2
    w1Tf = np.ascontiguousarray(w1p.T)
    w3Tf = np.ascontiguousarray(w3p.T)
    w2Tf = np.ascontiguousarray(w2p.T)
    masks = _mk_masks()
    ident = np.eye(128, dtype=ml_dtypes.bfloat16)

    maps = []
    for c in range(NCORE):
        h0 = 4 * (c % 4)
        wqkv_host = np.empty((128, 3, HL, DC, 128), np.float32)
        for m, w in enumerate((wq, wk, wv)):
            for h in range(HL):
                wqkv_host[:, m, h] = w[h0 + h].T.reshape(DC, 128, 128).transpose(1, 0, 2)
        wo_host = np.empty((128, HL, DC, 128), np.float32)
        for h in range(HL):
            wo_host[:, h] = wo[h0 + h].T.reshape(128, DC, 128)
        maps.append({
            "xT": np.ascontiguousarray(x[c * TC:(c + 1) * TC, :].T),
            "wqkv_in": wqkv_host,
            "wo_in": wo_host,
            "w1T": np.ascontiguousarray(w1Tf[c * D_SH:(c + 1) * D_SH, :]),
            "w3T": np.ascontiguousarray(w3Tf[c * D_SH:(c + 1) * D_SH, :]),
            "w2T": np.ascontiguousarray(w2Tf[c * HID_SH:(c + 1) * HID_SH, :]),
            "masks": masks,
            "ident": ident,
        })
    return maps


def kernel(**inputs):
    nc = bass.Bass("TRN2", target_bir_lowering=False)
    build_program(nc)
    maps = make_core_inputs(inputs)
    res = run_bass_kernel_spmd(nc, maps, core_ids=list(range(NCORE)))
    parts = [np.ascontiguousarray(res.results[c]["outT"].T) for c in range(NCORE)]
    return np.concatenate(parts, axis=0).reshape(B, S, D).astype(np.float32)


# version 13 (final)
# speedup vs baseline: 1.0971x; 1.0034x over previous
import math
import numpy as np
import ml_dtypes
from contextlib import ExitStack

import concourse.bass as bass
import concourse.mybir as mybir
from concourse.bass_utils import run_bass_kernel_spmd

F32 = mybir.dt.float32
BF16 = mybir.dt.bfloat16
AF = mybir.ActivationFunctionType
ALU = mybir.AluOpType

NCORE = 8
B, S, D, H, E = 2, 4096, 2048, 16, 128
HL = 4
TC = 1024
DC = D // 128
HID = 5461
HIDP = 5504
NHC = HIDP // 128
HID_SH = HIDP // NCORE
D_SH = D // NCORE
NQ = S // 512
QSCL = float(1.0 / math.sqrt(128.0))
W_N = float(HID * D)
LN_EPS = 1e-5

PB_SZ = 70656
OFF_QT = 0
OFF_KT = 8192
OFF_VT = 16384
OFF_WQKV = 24576
OFF_LNCH = 36864
OFF_Z = 53248
OFF_LN2XT = 0
OFF_ATTN = 16384
OFF_WO = 32768
OFF_P2STG = 40960
OFF_SQB = 45056
OFF_GT = 16384
OFF_FBUF = 38400
OFF_SLAB = 38400
OFF_FSTG = 60416
OFF_SIL2 = 68608

PF_SZ = 6656
PFO_XA, PFO_XB = 0, 1024
PFO_SCR = 2048
PFO_TMP = 3072
PFO_QW = 4096
PFO_X1A, PFO_X1B = 5472, 5984


def _ts(i, n):
    return slice(i * n, (i + 1) * n)


class Ctx:
    def __init__(self):
        self.h = {}
        self.n = {}

    def add(self, name, handle):
        self.h[name] = handle
        self.n[name] = 0

    def inc(self, instr, name, k=1):
        instr.then_inc(self.h[name], k)
        self.n[name] += k
        return self.n[name]

    def val(self, name):
        return self.n[name]

    def wait(self, eng, name, v=None):
        vv = self.n[name] if v is None else v
        if vv > 0:
            eng.wait_ge(self.h[name], vv)


def build_program(nc, taps=()):
    taps = set(taps)
    tap_t = {}

    xT = nc.dram_tensor("xT", [D, TC], F32, kind="ExternalInput").ap()
    wqkv_in = nc.dram_tensor("wqkv_in", [128, 3, HL, DC, 128], F32, kind="ExternalInput").ap()
    wo_in = nc.dram_tensor("wo_in", [128, HL, DC, 128], F32, kind="ExternalInput").ap()
    w1T = nc.dram_tensor("w1T", [D_SH, HIDP], F32, kind="ExternalInput").ap()
    w3T = nc.dram_tensor("w3T", [D_SH, HIDP], F32, kind="ExternalInput").ap()
    w2T = nc.dram_tensor("w2T", [HID_SH, D], F32, kind="ExternalInput").ap()
    masks = nc.dram_tensor("masks", [4, 128, 512], BF16, kind="ExternalInput").ap()
    ident = nc.dram_tensor("ident", [128, 128], BF16, kind="ExternalInput").ap()
    outT = nc.dram_tensor("outT", [D, TC], F32, kind="ExternalOutput").ap()

    lnx_in4 = [nc.dram_tensor("lnx_in%d" % q, [D, 256], BF16) for q in range(4)]
    lnx_out4 = [nc.dram_tensor("lnx_out%d" % q, [4 * D, 256], BF16) for q in range(4)]
    ar_in = nc.dram_tensor("ar_in", [1, 8], F32)
    ar_out = nc.dram_tensor("ar_out", [1, 8], F32)
    t1_in = nc.dram_tensor("t1_in", [D_SH, HIDP], BF16)
    t1_out = nc.dram_tensor("t1_out", [D, HIDP], BF16, addr_space="Shared")
    t3_in = nc.dram_tensor("t3_in", [D_SH, HIDP], BF16)
    t3_out = nc.dram_tensor("t3_out", [D, HIDP], BF16, addr_space="Shared")
    t2_in = nc.dram_tensor("t2_in", [HID_SH, D], BF16)
    t2_out = nc.dram_tensor("t2_out", [HIDP, D], BF16, addr_space="Shared")
    rs_in8 = [nc.dram_tensor("rs_in%d" % k, [4, 256, TC], BF16) for k in range(8)]
    rs_out8 = [nc.dram_tensor("rs_out%d" % k, [256, TC], BF16) for k in range(8)]
    x1_scr = nc.dram_tensor("x1_scr", [D, TC], F32)
    warm_in = nc.dram_tensor("warm_in", [1, 16], F32)
    warm_out = nc.dram_tensor("warm_out", [4, 16], F32)
    warm_out8 = nc.dram_tensor("warm_out8", [8, 16], F32, addr_space="Shared")

    def tapt(name, shape, dtype):
        if name in taps and name not in tap_t:
            tap_t[name] = nc.dram_tensor("tap_" + name, shape, dtype, kind="ExternalOutput").ap()
        return tap_t.get(name)

    grp4 = [[0, 1, 2, 3], [4, 5, 6, 7]]
    grp8 = [list(range(8))]

    with ExitStack() as ectx:
        pb = ectx.enter_context(nc.sbuf_tensor("pb", [128, PB_SZ], BF16))
        pf = ectx.enter_context(nc.sbuf_tensor("pf", [128, PF_SZ], F32))
        ones_fc = ectx.enter_context(nc.sbuf_tensor("ones_fc", [128, 1], F32))
        ones_fr = ectx.enter_context(nc.sbuf_tensor("ones_fr", [1, 128], F32))
        ones_bc = ectx.enter_context(nc.sbuf_tensor("ones_bc", [128, 1], BF16))
        mask_s = ectx.enter_context(nc.sbuf_tensor("mask_s", [128, 4, 512], BF16))
        id_s = ectx.enter_context(nc.sbuf_tensor("id_s", [128, 128], BF16))
        rows = ectx.enter_context(nc.sbuf_tensor("rows", [1, 4608], F32))
        st1 = ectx.enter_context(nc.sbuf_tensor("st1", [128, 16], F32))
        exp_s = ectx.enter_context(nc.sbuf_tensor("exp_s", [128, 2, 512], BF16))
        expm_s = ectx.enter_context(nc.sbuf_tensor("expm_s", [128, 2, 512], BF16))
        vtmp_s = ectx.enter_context(nc.sbuf_tensor("vtmp_s", [128, 512], BF16))
        inv_s = ectx.enter_context(nc.sbuf_tensor("inv_s", [1, 512], F32))
        sil_s = ectx.enter_context(nc.sbuf_tensor("sil_s", [128, 512], BF16))
        qsg_s = ectx.enter_context(nc.sbuf_tensor("qsg_s", [128, 3, 688], BF16))
        qts_s = ectx.enter_context(nc.sbuf_tensor("qts_s", [128, 2, 688], BF16))

        PS = [ectx.enter_context(nc.psum_tensor("ps%d" % i, [128, 512], F32)) for i in range(6)]
        PT = [ectx.enter_context(nc.psum_tensor("pt%d" % i, [128, 1024], BF16)) for i in range(2)]

        C = Ctx()
        for nm in ("dS", "dS2", "dW", "dQ", "dQ2", "dG", "dG2", "dG3", "dR", "dR2", "dR3", "dR4", "tE", "vE", "sE", "cC"):
            C.add(nm, ectx.enter_context(nc.semaphore(nm)))
        block = ectx.enter_context(nc.Block())

        lnxown = pb[:, 0:16384].rearrange("p (c t) -> p c t", c=16)
        qt = pb[:, OFF_QT:OFF_QT + 8192].rearrange("p (h t) -> p h t", h=2)
        kt = pb[:, OFF_KT:OFF_KT + 8192].rearrange("p (h t) -> p h t", h=2)
        vt = pb[:, OFF_VT:OFF_VT + 8192].rearrange("p (h k e) -> p h k e", h=2, k=32)
        wqkv = pb[:, OFF_WQKV:OFF_WQKV + 12288].rearrange("p (m h c e) -> p m h c e", m=3, h=2, c=DC)
        lnch = pb[:, OFF_LNCH:OFF_LNCH + 16384].rearrange("p (b c t) -> p b c t", b=2, c=16)
        z_v = pb[:, OFF_Z:OFF_Z + 16384].rearrange("p (h t) -> p h t", h=HL)
        ln2xt = pb[:, OFF_LN2XT:OFF_LN2XT + 16384].rearrange("p (c t) -> p c t", c=16)
        attn_t = pb[:, OFF_ATTN:OFF_ATTN + 16384].rearrange("p (c t) -> p c t", c=16)
        wo_v = pb[:, OFF_WO:OFF_WO + 8192].rearrange("p (h c e) -> p h c e", h=HL, c=DC)
        p2stg = pb[:, OFF_P2STG:OFF_P2STG + 4096].rearrange("p (t c) -> p t c", t=8)
        sqb = pb[:, OFF_SQB:OFF_SQB + 1024]
        gt_v = pb[:, OFF_GT:OFF_GT + NHC * 512].rearrange("p (h t) -> p h t", h=NHC)
        fbuf = pb[:, OFF_FBUF:OFF_FBUF + 16384].rearrange("p (b c h) -> p b c h", b=4, c=16)
        slab = pb[:, OFF_SLAB:OFF_SLAB + 2 * 11 * 512].rearrange("p (b h d) -> p b h d", b=2, h=11)
        fstg = pb[:, OFF_FSTG:OFF_FSTG + 8192].rearrange("p (c t) -> p c t", c=16)
        fsq_v = pb[:, OFF_SIL2:OFF_SIL2 + 512]

        xa = pf[:, PFO_XA:PFO_XA + 1024]
        xb = pf[:, PFO_XB:PFO_XB + 1024]
        fscr = pf[:, PFO_SCR:PFO_SCR + 1024]
        ftmp = pf[:, PFO_TMP:PFO_TMP + 1024]
        qw = [pf[:, PFO_QW:PFO_QW + 688], pf[:, PFO_QW + 688:PFO_QW + 1376]]
        qx = pf[:, PFO_QW:PFO_QW + 1024]
        x1pb = [pf[:, PFO_X1A:PFO_X1A + 512], pf[:, PFO_X1B:PFO_X1B + 512]]
        bcz_s = pf[:, PFO_X1A:PFO_X1A + 512]
        m_bc = pf[:, PFO_X1A:PFO_X1A + 1024]

        r_sum = rows[:, 0:1024]
        r_var = rows[:, 1024:2048]
        r_m = rows[:, 2048:3072]
        r_istd = rows[:, 3072:4096]
        r_misc = rows[:, 4096:4592]
        r_svec = rows[:, 4592:4600]
        r_sld = rows[:, 4600:4608]

        def T(f): block.tensor(f)
        def V(f): block.vector(f)
        def Sc(f): block.scalar(f)
        def Sy(f): block.sync(f)
        def G(f): block.gpsimd(f)

        dG_wqkv = {}

        @G
        def _(g):
            i = g.collective_compute("AllGather", ALU.bypass, replica_groups=grp4,
                                     ins=[warm_in.ap()], outs=[warm_out.ap()])
            C.inc(i, "cC", 1)
            i = g.collective_compute("AllGather", ALU.bypass, replica_groups=grp8,
                                     ins=[warm_in.ap()], outs=[warm_out8.ap()])
            C.inc(i, "cC", 1)
            g.memset(ones_fc[:, :], 1.0)
            g.memset(ones_fr[:, :], 1.0)
            g.memset(ones_bc[:, :], 1.0)
            g.memset(r_svec[:, :], 0.0)
            i = g.dma_start(out=mask_s[:, :, :], in_=masks.rearrange("j p t -> p j t"))
            C.inc(i, "dG", 16)
            i = g.dma_start(out=id_s[:, :], in_=ident[:, :])
            C.inc(i, "dG", 16)
            for m in range(3):
                i = g.dma_start(out=wqkv[:, m, :, :, :], in_=wqkv_in[:, m, 0:2, :, :])
                C.inc(i, "dG", 16)
            dG_wqkv[0] = C.val("dG")
        dG_init = C.val("dG")

        sE_sq = []
        tE_x = []

        for c in range(DC):
            def s_p0(s, c=c):
                C.wait(s, "dS" if c % 2 == 0 else "dS2", 16 * (c // 2 + 1))
                if c >= 1:
                    C.wait(s, "tE", tE_x[c - 1])
                i = s.activation(fscr[:, :], (xa if c % 2 == 0 else xb), AF.Square)
                sE_sq.append(C.inc(i, "sE", 1))
            Sc(s_p0)

            def t_p0(t, c=c):
                if c == 0:
                    C.wait(t, "dG", dG_init)
                xbuf = xa if c % 2 == 0 else xb
                C.wait(t, "dS" if c % 2 == 0 else "dS2", 16 * (c // 2 + 1))
                t.matmul(PS[4][0:1, :], ones_fc[:, :], xbuf[:, 0:512], start=(c == 0), stop=(c == DC - 1))
                t.matmul(PS[5][0:1, :], ones_fc[:, :], xbuf[:, 512:1024], start=(c == 0), stop=(c == DC - 1))
                C.wait(t, "sE", sE_sq[c])
                t.matmul(PS[0][0:1, :], ones_fc[:, :], fscr[:, 0:512], start=(c == 0), stop=(c == DC - 1))
                i = t.matmul(PS[1][0:1, :], ones_fc[:, :], fscr[:, 512:1024], start=(c == 0), stop=(c == DC - 1))
                tE_x.append(C.inc(i, "tE", 1))
            T(t_p0)

        @Sy
        def _(sy):
            for c in range(DC):
                if c >= 2:
                    C.wait(sy, "tE", tE_x[c - 2])
                i = sy.dma_start(out=(xa if c % 2 == 0 else xb), in_=xT[_ts(c, 128), :])
                C.inc(i, "dS" if c % 2 == 0 else "dS2", 16)

        def emit_rowchain(sum_srcs, sq_srcs, width):
            @V
            def _(v):
                C.wait(v, "tE", C.val("tE"))
                i = None
                for j, (bk, sl) in enumerate(sum_srcs):
                    i = v.tensor_copy(r_sum[:, _ts(j, 512)], bk[0:1, sl] if sl is not None else bk)
                for j, (bk, sl) in enumerate(sq_srcs):
                    i = v.tensor_copy(r_var[:, _ts(j, 512)], bk[0:1, sl] if sl is not None else bk)
                C.inc(i, "vE", 1)

            @Sc
            def _(s):
                C.wait(s, "vE", C.val("vE"))
                s.mul(r_m[:, 0:width], r_sum[:, 0:width], 1.0 / D)
                i = s.mul(r_istd[:, 0:width], r_var[:, 0:width], 1.0 / D)
                C.inc(i, "sE", 1)

            @V
            def _(v):
                C.wait(v, "sE", C.val("sE"))
                v.tensor_tensor(r_sum[:, 0:width], r_m[:, 0:width], r_m[:, 0:width], ALU.mult)
                v.tensor_tensor(r_var[:, 0:width], r_istd[:, 0:width], r_sum[:, 0:width], ALU.subtract)
                i = v.tensor_scalar_add(r_istd[:, 0:width], r_var[:, 0:width], LN_EPS)
                C.inc(i, "vE", 1)

            @Sc
            def _(s):
                C.wait(s, "vE", C.val("vE"))
                i = s.activation(r_sum[:, 0:width], r_istd[:, 0:width], AF.Ln)
                C.inc(i, "sE", 1)
                C.wait(s, "sE", C.val("sE"))
                i = s.activation(r_istd[:, 0:width], r_sum[:, 0:width], AF.Exp, scale=-0.5)
                C.inc(i, "sE", 1)

        emit_rowchain([(PS[4], slice(0, 512)), (PS[5], slice(0, 512))],
                      [(PS[0], slice(0, 512)), (PS[1], slice(0, 512))], 1024)

        def emit_bcast_mi():
            @T
            def _(t):
                C.wait(t, "sE", C.val("sE"))
                t.matmul(PS[0][:, :], ones_fr[:, :], r_m[0:1, 0:512], start=True, stop=True)
                t.matmul(PS[1][:, :], ones_fr[:, :], r_m[0:1, 512:1024], start=True, stop=True)
                t.matmul(PS[2][:, :], ones_fr[:, :], r_istd[0:1, 0:512], start=True, stop=True)
                i = t.matmul(PS[3][:, :], ones_fr[:, :], r_istd[0:1, 512:1024], start=True, stop=True)
                C.inc(i, "tE", 1)

        emit_bcast_mi()
        tE_ln1bc = C.val("tE")

        vE_app = []

        @V
        def _(v):
            C.wait(v, "tE", tE_ln1bc)
            for c in range(DC):
                xbuf = xa if c % 2 == 0 else xb
                C.wait(v, "dS" if c % 2 == 0 else "dS2", 16 * (8 + c // 2 + 1))
                i = None
                for hf in range(2):
                    v.tensor_tensor(ftmp[:, _ts(hf, 512)], xbuf[:, _ts(hf, 512)], PS[hf][:, :], ALU.subtract)
                    i = v.tensor_tensor(lnxown[:, c, _ts(hf, 512)], ftmp[:, _ts(hf, 512)], PS[2 + hf][:, :], ALU.mult)
                vE_app.append(C.inc(i, "vE", 1))

        @Sy
        def _(sy):
            for c in range(DC):
                if c >= 2:
                    C.wait(sy, "vE", vE_app[c - 2])
                else:
                    C.wait(sy, "tE", tE_x[DC - 2 + c])
                i = sy.dma_start(out=(xa if c % 2 == 0 else xb), in_=xT[_ts(c, 128), :])
                C.inc(i, "dS" if c % 2 == 0 else "dS2", 16)

        cC_lnxq = {}

        @G
        def _(g):
            C.wait(g, "vE", vE_app[-1])
            for q in range(4):
                i = g.dma_start(out=lnx_in4[q].rearrange("(c p) t -> p c t", p=128),
                                in_=lnxown[:, :, _ts(q, 256)])
                C.inc(i, "dG", 16)
                C.wait(g, "dG", C.val("dG"))
                i = g.collective_compute("AllGather", ALU.bypass, replica_groups=grp4,
                                         ins=[lnx_in4[q].ap()], outs=[lnx_out4[q].ap()])
                cC_lnxq[q] = C.inc(i, "cC", 1)

        qchunks = []
        for mi, wsrc in ((0, w1T), (1, w3T), (2, w2T)):
            if mi < 2:
                for r in range(2):
                    for cc in range(8):
                        qchunks.append((wsrc, r * 128, 128, cc * 688, 688, mi))
            else:
                for r in range(6):
                    nr = 128 if r < 5 else HID_SH - 640
                    for (c0, ncol) in ((0, 688), (688, 688), (1376, 672)):
                        qchunks.append((wsrc, r * 128, nr, c0, ncol, mi))
        NQCH = len(qchunks)
        dQ_q1p = [C.val("dQ"), C.val("dQ2")]

        QEMIT = []
        QA_END = [0]
        vE_qred = []
        qacc_par = {0: 0, 1: 0, 2: 0}
        _first = {0: True, 1: True, 2: True}
        _par = {0: 0, 1: 0, 2: 0}
        sE_qs = {}
        vE_qt = {}
        cC_t = {}
        dG23 = {}

        for j, ch in enumerate(qchunks):
            def q_ld1(sc, j=j, ch=ch):
                wsrc, r0, nr, c0, ncol, mi = ch
                if j >= 2:
                    C.wait(sc, "vE", vE_qred[j - 2])
                i = sc.dma_start(out=qw[j % 2][0:nr, 0:ncol], in_=wsrc[r0:r0 + nr, c0:c0 + ncol])
                C.inc(i, "dQ" if j % 2 == 0 else "dQ2", 16)
            QEMIT.append((Sc, q_ld1))

            def q_red(v, j=j, ch=ch):
                wsrc, r0, nr, c0, ncol, mi = ch
                C.wait(v, "dQ" if j % 2 == 0 else "dQ2", dQ_q1p[j % 2] + 16 * (j // 2 + 1))
                buf = qw[j % 2]
                if _first[mi]:
                    i = v.reduce_sum(st1[:, mi * 2:mi * 2 + 1], buf[0:nr, 0:ncol],
                                     axis=mybir.AxisListType.X, apply_absolute_value=True)
                    _first[mi] = False
                else:
                    v.reduce_sum(st1[0:nr, 6:7], buf[0:nr, 0:ncol],
                                 axis=mybir.AxisListType.X, apply_absolute_value=True)
                    o = _par[mi]
                    _par[mi] ^= 1
                    n2 = _par[mi]
                    v.tensor_copy(st1[:, mi * 2 + n2:mi * 2 + n2 + 1], st1[:, mi * 2 + o:mi * 2 + o + 1])
                    i = v.tensor_tensor(st1[0:nr, mi * 2 + n2:mi * 2 + n2 + 1],
                                        st1[0:nr, mi * 2 + o:mi * 2 + o + 1],
                                        st1[0:nr, 6:7], ALU.add)
                vE_qred.append(C.inc(i, "vE", 1))
                qacc_par[mi] = _par[mi]
            QEMIT.append((V, q_red))

        def q_sum(t):
            C.wait(t, "vE", vE_qred[-1])
            i = None
            for mi in range(3):
                acc = st1[:, mi * 2 + qacc_par[mi]:mi * 2 + qacc_par[mi] + 1]
                i = t.matmul(PS[4][0:1, mi:mi + 1], ones_fc[:, :], acc, start=True, stop=True)
            C.inc(i, "tE", 1)
        QEMIT.append((T, q_sum))

        def q_cp(v):
            C.wait(v, "tE", C.val("tE"))
            i = v.tensor_copy(r_svec[:, 0:3], PS[4][0:1, 0:3])
            C.inc(i, "vE", 1)
        QEMIT.append((V, q_cp))

        def q_ar(g):
            C.wait(g, "vE", C.val("vE"))
            i = g.dma_start(out=ar_in[:, :], in_=r_svec[:, :])
            C.inc(i, "dG", 16)
            C.wait(g, "dG", C.val("dG"))
            i = g.collective_compute("AllReduce", ALU.add, replica_groups=grp8,
                                     ins=[ar_in.ap()], outs=[ar_out.ap()])
            C.inc(i, "cC", 1)
            C.wait(g, "cC", C.val("cC"))
            i = g.dma_start(out=r_sld[:, :], in_=ar_out[:, :])
            C.inc(i, "dG", 16)
        QEMIT.append((G, q_ar))

        def q_smul(sc):
            C.wait(sc, "dG", C.val("dG"))
            i = sc.mul(r_misc[:, 0:3], r_sld[:, 0:3], 0.5 / W_N)
            C.inc(i, "sE", 1)
        QEMIT.append((Sc, q_smul))

        def q_bc(t):
            C.wait(t, "sE", C.val("sE"))
            i = t.matmul(PS[5][:, 0:3], ones_fr[:, :], r_misc[0:1, 0:3], start=True, stop=True)
            C.inc(i, "tE", 1)
        QEMIT.append((T, q_bc))

        vE_sh_h = [0]

        def q_sh(v):
            C.wait(v, "tE", C.val("tE"))
            v.tensor_copy(st1[:, 8:11], PS[5][:, 0:3])
            i = v.tensor_scalar_mul(st1[:, 11:14], PS[5][:, 0:3], -1.0)
            vE_sh_h[0] = C.inc(i, "vE", 1)
        QEMIT.append((V, q_sh))
        QA_END[0] = len(QEMIT)

        dQ_q2p = [None, None]

        def q_p2init(sc):
            dQ_q2p[0] = C.val("dQ")
            dQ_q2p[1] = C.val("dQ2")
            C.wait(sc, "vE", vE_sh_h[0])
            for j in range(2):
                wsrc, r0, nr, c0, ncol, mi = qchunks[j]
                i = sc.dma_start(out=qw[j % 2][0:nr, 0:ncol], in_=wsrc[r0:r0 + nr, c0:c0 + ncol])
                C.inc(i, "dQ" if j % 2 == 0 else "dQ2", 16)
        QEMIT.append((Sc, q_p2init))

        dsts = {0: t1_in, 1: t3_in, 2: t2_in}
        lastj = {0: 15, 1: 31, 2: NQCH - 1}

        _written = set()

        def emit_write(sc, j):
            if j in _written:
                return
            _written.add(j)
            wsrc, r0, nr, c0, ncol, mi = qchunks[j]
            C.wait(sc, "vE", vE_qt[j])
            sem = "dG2" if j % 2 == 0 else "dG3"
            i = sc.dma_start(out=dsts[mi][r0:r0 + nr, c0:c0 + ncol],
                             in_=qts_s[0:nr, j % 2, 0:ncol])
            dG23[j] = (sem, C.inc(i, sem, 16))

        for j, ch in enumerate(qchunks):
            def q_sgn(sc, j=j, ch=ch):
                wsrc, r0, nr, c0, ncol, mi = ch
                if j >= 1:
                    emit_write(sc, j - 1)
                C.wait(sc, "dQ" if j % 2 == 0 else "dQ2", dQ_q2p[j % 2] + 16 * (j // 2 + 1))
                if j >= 1:
                    C.wait(sc, "vE", vE_qt[j - 1])
                sc.activation(qsg_s[0:nr, 0, 0:ncol], qw[j % 2][0:nr, 0:ncol], AF.Sign,
                              bias=st1[0:nr, 11 + mi:12 + mi], scale=1.0)
                i = sc.activation(qsg_s[0:nr, 1, 0:ncol], qw[j % 2][0:nr, 0:ncol], AF.Sign,
                                  bias=st1[0:nr, 8 + mi:9 + mi], scale=1.0)
                sE_qs[j] = C.inc(i, "sE", 1)
                if j + 2 < NQCH:
                    C.wait(sc, "sE", sE_qs[j])
                    w2s, r02, nr2, c02, nc2, _ = qchunks[j + 2]
                    i = sc.dma_start(out=qw[j % 2][0:nr2, 0:nc2], in_=w2s[r02:r02 + nr2, c02:c02 + nc2])
                    C.inc(i, "dQ" if j % 2 == 0 else "dQ2", 16)
            QEMIT.append((Sc, q_sgn))

            def q_tt(v, j=j, ch=ch):
                wsrc, r0, nr, c0, ncol, mi = ch
                C.wait(v, "sE", sE_qs[j])
                if j >= 2:
                    C.wait(v, dG23[j - 2][0], dG23[j - 2][1])
                v.tensor_tensor(qsg_s[0:nr, 2, 0:ncol], qsg_s[0:nr, 0, 0:ncol],
                                qsg_s[0:nr, 1, 0:ncol], ALU.add)
                i = v.tensor_scalar_mul(qts_s[0:nr, j % 2, 0:ncol], qsg_s[0:nr, 2, 0:ncol],
                                        st1[0:nr, 8 + mi:9 + mi])
                vE_qt[j] = C.inc(i, "vE", 1)
            QEMIT.append((V, q_tt))

            if j == lastj[0] or j == lastj[1] or j == lastj[2]:
                def q_ag(g, j=j):
                    mi = {lastj[0]: 0, lastj[1]: 1, lastj[2]: 2}[j]
                    C.wait(g, "dG2", C.val("dG2"))
                    C.wait(g, "dG3", C.val("dG3"))
                    src, dst = {0: (t1_in, t1_out), 1: (t3_in, t3_out), 2: (t2_in, t2_out)}[mi]
                    i = g.collective_compute("AllGather", ALU.bypass, replica_groups=grp8,
                                             ins=[src.ap()], outs=[dst.ap()])
                    cC_t[mi] = C.inc(i, "cC", 1)

                def q_wlast(sc, j=j):
                    emit_write(sc, j)
                QEMIT.append((Sc, q_wlast))
                QEMIT.append((G, q_ag))

        _emitted_last = set()

        def pop_quant(k):
            for _ in range(k):
                if not QEMIT:
                    return
                reg, fn = QEMIT.pop(0)
                QA_END[0] = max(0, QA_END[0] - 1)
                reg(fn)

        def drain_quant_psum_part():
            while QA_END[0] > 0:
                pop_quant(1)

        evac_done = {}
        qkv_done = {}
        dS_chunk = {}
        last_exp_use = {0: 0, 1: 0}
        last_pv = {0: 0, 1: 0}
        last_z_use = {0: 0, 1: 0}
        last_den = [0]
        last_bc = [0]
        vt_use = {0: 0, 1: 0}
        z_done = [0]
        GROUPS = [(0, 0), (1, 0), (2, 0), (0, 1), (1, 1), (2, 1)]

        JORD = [0, 2, 4, 6, 1, 3, 5, 7]
        for pk in range(2):
            for jp in range(8):
                j = JORD[jp]
                jg = pk * 8 + jp

                def sy_mini(sy, j=j, jg=jg):
                    C.wait(sy, "cC", cC_lnxq[2 * (j % 2) + 1])
                    if jg >= 2:
                        C.wait(sy, "tE", qkv_done[jg - 2])
                    blkr = (j // 2) * D
                    sem = "dS" if jg % 2 == 0 else "dS2"
                    for qq in range(2):
                        src = lnx_out4[2 * (j % 2) + qq][blkr:blkr + D, :].rearrange("(c p) t -> p c t", p=128)
                        for hcf in range(2):
                            i = sy.dma_start(out=lnch[:, jg % 2, 8 * hcf:8 * hcf + 8, _ts(qq, 256)],
                                             in_=src[:, 8 * hcf:8 * hcf + 8, :])
                            C.inc(i, sem, 16)
                    dS_chunk[jg] = (sem, C.val(sem))
                Sy(sy_mini)

                for g_i, (m, h) in enumerate(GROUPS):
                    Gg = jg * 6 + g_i

                    def t_mini(t, m=m, h=h, j=j, jg=jg, Gg=Gg, pk=pk):
                        bank = PS[Gg % 2]
                        C.wait(t, dS_chunk[jg][0], dS_chunk[jg][1])
                        C.wait(t, "dG", dG_wqkv[pk])
                        if Gg >= 2 and (Gg - 2) in evac_done:
                            C.wait(t, "vE", evac_done[Gg - 2])
                        i = None
                        for cc in range(DC):
                            i = t.matmul(bank[:, :], wqkv[:, m, h, cc, :], lnch[:, jg % 2, cc, :],
                                         start=(cc == 0), stop=(cc == DC - 1))
                        qkv_done[jg] = C.inc(i, "tE", 1)
                    T(t_mini)

                    if m < 2:
                        def v_mini(v, m=m, h=h, j=j, Gg=Gg):
                            bank = PS[Gg % 2]
                            C.wait(v, "tE", C.val("tE"))
                            i = v.tensor_copy((qt if m == 0 else kt)[:, h, _ts(j, 512)], bank[:, :])
                            evac_done[Gg] = C.inc(i, "vE", 1)
                        V(v_mini)
                        pop_quant(2)
                    else:
                        def v_mini1(v, Gg=Gg):
                            bank = PS[Gg % 2]
                            C.wait(v, "tE", C.val("tE"))
                            i = v.tensor_copy(vtmp_s[:, :], bank[:, :])
                            C.inc(i, "vE", 1)
                        V(v_mini1)

                        vpar = jg % 2

                        def t_mini2(t, vpar=vpar, jg=jg):
                            C.wait(t, "vE", C.val("vE"))
                            if vt_use[vpar]:
                                C.wait(t, "vE", vt_use[vpar])
                            i = None
                            for q in range(4):
                                i = t.transpose(PT[vpar][:, _ts(q, 128)], vtmp_s[:, _ts(q, 128)], id_s[:, :])
                            qkv_done[jg] = C.inc(i, "tE", 1)
                        T(t_mini2)

                        def v_mini2(v, h=h, j=j, Gg=Gg, vpar=vpar):
                            C.wait(v, "tE", C.val("tE"))
                            dst = vt[:, h, 4 * j:4 * j + 4, :].rearrange("p k e -> p (k e)")
                            i = v.tensor_copy(dst, PT[vpar][:, 0:512])
                            vt_use[vpar] = C.inc(i, "vE", 1)
                            evac_done[Gg] = C.val("vE")
                        V(v_mini2)
                        pop_quant(2)

            if pk == 0:
                def g_wq1(g):
                    C.wait(g, "tE", qkv_done[7])
                    for m in range(3):
                        i = g.dma_start(out=wqkv[:, m, :, :, :], in_=wqkv_in[:, m, 2:4, :, :])
                        C.inc(i, "dG", 16)
                    dG_wqkv[1] = C.val("dG")
                G(g_wq1)

            drain_quant_psum_part()
            vE_pass_evacs = evac_done[(pk * 8 + 7) * 6 + 5]

            for hl in range(2):
                hglob = 2 * pk + hl
                for qc in range(NQ):
                    nk = 4 * qc + 4
                    tS = {}

                    def t_pre(t, hl=hl, qc=qc, nk=nk, tS=tS, vE_pass_evacs=vE_pass_evacs):
                        C.wait(t, "vE", vE_pass_evacs)
                        if last_den[0]:
                            C.wait(t, "vE", last_den[0])
                        if last_z_use[qc % 2]:
                            C.wait(t, "vE", last_z_use[qc % 2])
                        for kc in range(min(2, nk)):
                            if last_exp_use[kc % 2]:
                                C.wait(t, "sE", last_exp_use[kc % 2])
                            i = t.matmul(PS[kc % 2][:, :], kt[:, hl, _ts(kc, 128)],
                                         qt[:, hl, _ts(qc, 512)], start=True, stop=True)
                            tS[kc] = C.inc(i, "tE", 1)
                    T(t_pre)

                    for kc in range(nk):
                        diag = kc >= nk - 4
                        eK = [0]
                        mK = [0]

                        def s_exp(s, kc=kc, tS=tS, eK=eK):
                            C.wait(s, "tE", tS[kc])
                            if last_pv[kc % 2]:
                                C.wait(s, "tE", last_pv[kc % 2])
                            i = s.activation(exp_s[:, kc % 2, :], PS[kc % 2][:, :], AF.Exp, scale=QSCL)
                            eK[0] = C.inc(i, "sE", 1)
                            last_exp_use[kc % 2] = eK[0]
                        Sc(s_exp)

                        if diag:
                            def v_mask(v, kc=kc, nk=nk, eK=eK, mK=mK):
                                C.wait(v, "sE", eK[0])
                                if last_pv[kc % 2]:
                                    C.wait(v, "tE", last_pv[kc % 2])
                                jj = kc - (nk - 4)
                                i = v.tensor_tensor(expm_s[:, kc % 2, :], exp_s[:, kc % 2, :],
                                                    mask_s[:, jj, :], ALU.mult)
                                mK[0] = C.inc(i, "vE", 1)
                            V(v_mask)

                        def t_pv(t, hl=hl, qc=qc, kc=kc, nk=nk, diag=diag, eK=eK, mK=mK, tS=tS):
                            if diag:
                                C.wait(t, "vE", mK[0])
                                src = expm_s[:, kc % 2, :]
                            else:
                                C.wait(t, "sE", eK[0])
                                src = exp_s[:, kc % 2, :]
                            i = t.matmul(PS[2 + qc % 2][:, :], vt[:, hl, kc, :], src,
                                         start=(kc == 0), stop=(kc == nk - 1))
                            C.inc(i, "tE", 1)
                            i = t.matmul(PS[4][0:1, :], ones_bc[:, :], src,
                                         start=(kc == 0), stop=(kc == nk - 1))
                            last_pv[kc % 2] = C.inc(i, "tE", 1)
                            if kc + 2 < nk:
                                C.wait(t, "sE", last_exp_use[kc % 2])
                                i = t.matmul(PS[kc % 2][:, :], kt[:, hl, _ts(kc + 2, 128)],
                                             qt[:, hl, _ts(qc, 512)], start=True, stop=True)
                                tS[kc + 2] = C.inc(i, "tE", 1)
                        T(t_pv)

                    def v_recip(v):
                        C.wait(v, "tE", C.val("tE"))
                        i = v.reciprocal(inv_s[:, :], PS[4][0:1, :])
                        last_den[0] = C.inc(i, "vE", 1)
                    V(v_recip)

                    def t_bc(t):
                        C.wait(t, "vE", last_den[0])
                        if last_bc[0]:
                            C.wait(t, "vE", last_bc[0])
                        i = t.matmul(PS[5][:, :], ones_fr[:, :], inv_s[0:1, :], start=True, stop=True)
                        C.inc(i, "tE", 1)
                    T(t_bc)

                    def v_zmul(v, hglob=hglob, qc=qc):
                        C.wait(v, "tE", C.val("tE"))
                        v.tensor_copy(bcz_s[:, :], PS[5][:, :])
                        i = v.tensor_tensor(z_v[:, hglob, _ts(qc, 512)], PS[2 + qc % 2][:, :],
                                            bcz_s[:, :], ALU.mult)
                        last_z_use[qc % 2] = C.inc(i, "vE", 1)
                        last_bc[0] = C.val("vE")
                        z_done[0] = C.val("vE")
                    V(v_zmul)
                    pop_quant(3)

        vE_zall = z_done[0]

        pop_quant(10 ** 9)

        dG_wo = [0]

        @G
        def _(g):
            C.wait(g, "tE", qkv_done[15])
            i = g.dma_start(out=wo_v[:, :, :, :], in_=wo_in[:, :, :, :])
            dG_wo[0] = C.inc(i, "dG", 16)

        p2_evac = {}
        st_p2 = {}
        cC_rs = {}

        for dc in range(DC):
            for grp, t5s in ((0, (0, 1, 2, 3, 4, 5)), (1, (6, 7))):
                def t_p2(t, dc=dc, grp=grp, t5s=t5s):
                    C.wait(t, "dG", dG_wo[0])
                    C.wait(t, "vE", vE_zall)
                    prev = (dc, 0) if grp == 1 else (dc - 1, 1)
                    if prev in p2_evac:
                        C.wait(t, "vE", p2_evac[prev])
                    i = None
                    for he in range(4):
                        for bi, t5 in enumerate(t5s):
                            i = t.matmul(PS[bi][:, :], wo_v[:, he, dc, :],
                                         z_v[:, he, _ts(t5, 512)], start=(he == 0), stop=(he == 3))
                    C.inc(i, "tE", 1)
                T(t_p2)

                def v_p2(v, dc=dc, grp=grp, t5s=t5s):
                    C.wait(v, "tE", C.val("tE"))
                    if grp == 0 and dc >= 1:
                        C.wait(v, *st_p2[dc - 1])
                    i = None
                    for bi, t5 in enumerate(t5s):
                        i = v.tensor_copy(p2stg[:, t5, :], PS[bi][:, :])
                    p2_evac[(dc, grp)] = C.inc(i, "vE", 1)
                V(v_p2)

            def sy_p2(sy, dc=dc):
                k = dc // 2
                C.wait(sy, "vE", p2_evac[(dc, 1)])
                if dc % 2 == 0 and k >= 4:
                    C.wait(sy, "cC", cC_rs[k - 4])
                rsv = rs_in8[k].rearrange("s (c p) (h t) -> p c s h t", p=128, t=512)
                i = sy.dma_start(out=rsv[:, dc % 2, :, :, :],
                                 in_=p2stg.rearrange("p (s h) t -> p s h t", s=4))
                sem = ("dR", "dR2", "dR3", "dR4")[k % 4]
                st_p2[dc] = (sem, C.inc(i, sem, 16))
            Sy(sy_p2)

            if dc % 2 == 1:
                def g_rs(g, k=dc // 2):
                    C.wait(g, *st_p2[2 * k + 1])
                    i = g.collective_compute("ReduceScatter", ALU.add, replica_groups=grp4,
                                             ins=[rs_in8[k].ap()], outs=[rs_out8[k].ap()])
                    cC_rs[k] = C.inc(i, "cC", 1)
                G(g_rs)

        at_ld = {}
        tE_asq = {}

        def sy_at(sy, k):
            C.wait(sy, "cC", cC_rs[k])
            if k >= 2:
                C.wait(sy, "tE", tE_asq[2 * (k - 2) + 1])
            sem = "dS" if k % 2 == 0 else "dS2"
            i0 = sy.dma_start(out=attn_t[:, 2 * k, :], in_=rs_out8[k][0:128, :])
            C.inc(i0, sem, 16)
            i1 = sy.dma_start(out=attn_t[:, 2 * k + 1, :], in_=rs_out8[k][128:256, :])
            C.inc(i1, sem, 16)
            at_ld[k] = (sem, C.val(sem))

        for c in range(DC):
            if c % 2 == 0:
                Sy(lambda sy, k=c // 2: sy_at(sy, k))
            def s_asq(s, c=c):
                C.wait(s, *at_ld[c // 2])
                if c >= 1:
                    C.wait(s, "tE", tE_asq[c - 1])
                i = s.activation(sqb[:, :], attn_t[:, c, :], AF.Square)
                C.inc(i, "sE", 1)
            Sc(s_asq)

            def t_asq(t, c=c):
                C.wait(t, *at_ld[c // 2])
                t.matmul(PS[4][0:1, :], ones_bc[:, :], attn_t[:, c, 0:512], start=(c == 0), stop=(c == DC - 1))
                t.matmul(PS[5][0:1, :], ones_bc[:, :], attn_t[:, c, 512:1024], start=(c == 0), stop=(c == DC - 1))
                C.wait(t, "sE", C.val("sE"))
                t.matmul(PS[0][0:1, :], ones_bc[:, :], sqb[:, 0:512], start=(c == 0), stop=(c == DC - 1))
                i = t.matmul(PS[1][0:1, :], ones_bc[:, :], sqb[:, 512:1024], start=(c == 0), stop=(c == DC - 1))
                tE_asq[c] = C.inc(i, "tE", 1)
            T(t_asq)

        emit_rowchain([(PS[4], slice(0, 512)), (PS[5], slice(0, 512))],
                      [(PS[0], slice(0, 512)), (PS[1], slice(0, 512))], 1024)
        emit_bcast_mi()
        tE_abc = C.val("tE")

        @V
        def _(v):
            C.wait(v, "tE", tE_abc)
            v.tensor_copy(m_bc[:, 0:512], PS[0][:, :])
            i = v.tensor_copy(m_bc[:, 512:1024], PS[1][:, :])
            C.inc(i, "vE", 1)


        x2ld = {}

        @Sc
        def _(s):
            C.wait(s, "vE", vE_app[-1])
            for c in range(2):
                sem = "dQ" if c % 2 == 0 else "dQ2"
                i = s.dma_start(out=(xa if c % 2 == 0 else xb), in_=xT[_ts(c, 128), :])
                x2ld[c] = (sem, C.inc(i, sem, 16))

        vE_x1 = {}
        dS_x1w = {}
        sE_xsq = {}
        tE_xst = {}
        for c in range(DC):
            def v_x1(v, c=c):
                xbuf = xa if c % 2 == 0 else xb
                C.wait(v, "tE", tE_abc)
                C.wait(v, *at_ld[c // 2])
                C.wait(v, x2ld[c][0], x2ld[c][1])
                if c >= 1:
                    C.wait(v, "dW", dS_x1w[c - 1])
                    C.wait(v, "tE", tE_xst[c - 1])
                i = None
                for hf in range(2):
                    v.tensor_tensor(ftmp[:, _ts(hf, 512)], attn_t[:, c, _ts(hf, 512)], m_bc[:, _ts(hf, 512)], ALU.subtract)
                    v.tensor_tensor(qx[:, _ts(hf, 512)], ftmp[:, _ts(hf, 512)], PS[2 + hf][:, :], ALU.mult)
                    i = v.tensor_tensor(fscr[:, _ts(hf, 512)], qx[:, _ts(hf, 512)],
                                        xbuf[:, _ts(hf, 512)], ALU.add)
                vE_x1[c] = C.inc(i, "vE", 1)
            V(v_x1)

            def sy_x1(sy, c=c):
                C.wait(sy, "vE", vE_x1[c])
                i = sy.dma_start(out=x1_scr[_ts(c, 128), :], in_=fscr[:, :])
                dS_x1w[c] = C.inc(i, "dW", 16)
            Sy(sy_x1)

            def s_xsq(s, c=c):
                C.wait(s, "vE", vE_x1[c])
                if c >= 1:
                    C.wait(s, "tE", tE_xst[c - 1])
                i = s.activation(sqb[:, :], fscr[:, :], AF.Square)
                sE_xsq[c] = C.inc(i, "sE", 1)
            Sc(s_xsq)

            def t_xst(t, c=c):
                C.wait(t, "vE", vE_x1[c])
                t.matmul(PS[4][0:1, :], ones_fc[:, :], fscr[:, 0:512], start=(c == 0), stop=(c == DC - 1))
                t.matmul(PS[5][0:1, :], ones_fc[:, :], fscr[:, 512:1024], start=(c == 0), stop=(c == DC - 1))
                C.wait(t, "sE", sE_xsq[c])
                t.matmul(PS[0][0:1, :], ones_bc[:, :], sqb[:, 0:512], start=(c == 0), stop=(c == DC - 1))
                i = t.matmul(PS[1][0:1, :], ones_bc[:, :], sqb[:, 512:1024], start=(c == 0), stop=(c == DC - 1))
                tE_xst[c] = C.inc(i, "tE", 1)
            T(t_xst)

            def s_x1(s, c=c):
                if c + 2 < DC:
                    C.wait(s, "vE", vE_x1[c])
                    sem = "dQ" if c % 2 == 0 else "dQ2"
                    i = s.dma_start(out=(xa if c % 2 == 0 else xb), in_=xT[_ts(c + 2, 128), :])
                    x2ld[c + 2] = (sem, C.inc(i, sem, 16))
            Sc(s_x1)

        emit_rowchain([(PS[4], slice(0, 512)), (PS[5], slice(0, 512))],
                      [(PS[0], slice(0, 512)), (PS[1], slice(0, 512))], 1024)
        emit_bcast_mi()
        tE_l2bc = C.val("tE")

        l2ld = {}
        tE_pb = C.val("tE")

        @Sc
        def _(s):
            C.wait(s, "tE", tE_pb)
            for c in range(2):
                sem = "dQ" if c % 2 == 0 else "dQ2"
                i = s.dma_start(out=(xa if c % 2 == 0 else xb), in_=x1_scr[_ts(c, 128), :])
                l2ld[c] = (sem, C.inc(i, sem, 16))

        vE_l2 = {}
        for c in range(DC):
            def v_l2(v, c=c):
                xbuf = xa if c % 2 == 0 else xb
                C.wait(v, "tE", tE_l2bc)
                C.wait(v, l2ld[c][0], l2ld[c][1])
                i = None
                for hf in range(2):
                    v.tensor_tensor(fscr[:, _ts(hf, 512)], xbuf[:, _ts(hf, 512)], PS[hf][:, :], ALU.subtract)
                    i = v.tensor_tensor(ln2xt[:, c, _ts(hf, 512)], fscr[:, _ts(hf, 512)], PS[2 + hf][:, :], ALU.mult)
                vE_l2[c] = C.inc(i, "vE", 1)
            V(v_l2)

            def s_l2(s, c=c):
                if c + 2 < DC:
                    C.wait(s, "vE", vE_l2[c])
                    sem = "dQ" if c % 2 == 0 else "dQ2"
                    i = s.dma_start(out=(xa if c % 2 == 0 else xb), in_=x1_scr[_ts(c + 2, 128), :])
                    l2ld[c + 2] = (sem, C.inc(i, sem, 16))
            Sc(s_l2)
        vE_ln2 = C.val("vE")

        HCP = [(2 * i, 2) for i in range(21)] + [(42, 1)]
        out_store = [0]
        vE_out_prev = [0]
        gt_tap_gate = [0]
        gt_done = {}
        sE_sil = {}
        h_mm = {}
        w2_done = {}
        w2sub_done = {}
        fstg_evac = {}

        def megabarrier():
            for eng, name in ((T, "tensor"),):
                def bar_t(t):
                    C.wait(t, "vE", C.val("vE"))
                    C.wait(t, "sE", C.val("sE"))
                    C.wait(t, "dS", C.val("dS"))
                    C.wait(t, "dQ", C.val("dQ"))
                    C.wait(t, "dG", C.val("dG"))
                    i = t.matmul(PS[5][0:1, 0:1], ones_fc[0:1, :], ones_fc[0:1, 0:1], start=True, stop=True)
                    C.inc(i, "tE", 1)
                T(bar_t)
            def bar_v(v):
                C.wait(v, "tE", C.val("tE"))
                i = v.tensor_copy(st1[:, 15:16], st1[:, 14:15])
                C.inc(i, "vE", 1)
            V(bar_v)
            def bar_s(s):
                C.wait(s, "vE", C.val("vE"))
                i = s.mul(st1[:, 15:16], st1[:, 15:16], 1.0)
                C.inc(i, "sE", 1)
            Sc(bar_s)
            def bar_sy(sy):
                C.wait(sy, "sE", C.val("sE"))
            Sy(bar_sy)
            def bar_g(g):
                C.wait(g, "sE", C.val("sE"))
            G(bar_g)

        MEGABAR = False
        for th in range(2):
            tsl = _ts(th, 512)
            if MEGABAR:
                megabarrier()
            dS_fb = {}
            for pi, (hc0, nn) in enumerate(HCP):
                def sy_fb(sy, pi=pi, hc0=hc0, nn=nn, th=th):
                    C.wait(sy, "cC", cC_t[1])
                    if th == 0 and pi < 2:
                        C.wait(sy, "vE", vE_ln2)
                    if pi >= 2:
                        C.wait(sy, "tE", h_mm[(th, pi - 2, 1)])
                    elif th == 1 and pi < 2:
                        C.wait(sy, "tE", h_mm[(0, len(HCP) - 2 + pi, 1)])
                        C.wait(sy, "tE", w2_done[(0, 3)])
                    sem = "dS" if pi % 2 == 0 else "dS2"
                    for m, tsrc in ((0, t1_out), (1, t3_out)):
                        src = tsrc[:, hc0 * 128:(hc0 + nn) * 128].rearrange("(c p) h -> p c h", p=128)
                        for q in range(4):
                            i = sy.dma_start(out=fbuf[:, m * 2 + pi % 2, _ts(q, 4), 0:nn * 128],
                                             in_=src[:, _ts(q, 4), :])
                            C.inc(i, sem, 16)
                    dS_fb[pi] = (sem, C.val(sem))
                Sy(sy_fb)

                for li in range(nn):
                    hc = hc0 + li

                    def t_h(t, pi=pi, li=li, hc=hc, th=th):
                        C.wait(t, dS_fb[pi][0], dS_fb[pi][1])
                        if th == 1 and hc < 2 and gt_tap_gate[0]:
                            C.wait(t, "dG", gt_tap_gate[0])
                        if th == 0 and hc < 2:
                            C.wait(t, "vE", vE_ln2)
                        if th == 1 and hc < 2 and (0, 3) in fstg_evac:
                            C.wait(t, "vE", fstg_evac[(0, 3)])
                        if hc >= 2:
                            C.wait(t, "vE", gt_done[(th, hc - 2)])
                        elif th == 1:
                            C.wait(t, "vE", gt_done[(0, NHC - 2 + hc)])
                        i1 = None
                        i3 = None
                        for cc in range(DC):
                            i1 = t.matmul(PS[hc % 2][:, :], fbuf[:, 0 + pi % 2, cc, _ts(li, 128)],
                                          ln2xt[:, cc, _ts(th, 512)], start=(cc == 0), stop=(cc == DC - 1))
                        h_mm[(th, pi, 0)] = C.inc(i1, "tE", 1)
                        for cc in range(DC):
                            i3 = t.matmul(PS[2 + hc % 2][:, :], fbuf[:, 2 + pi % 2, cc, _ts(li, 128)],
                                          ln2xt[:, cc, _ts(th, 512)], start=(cc == 0), stop=(cc == DC - 1))
                        h_mm[(th, pi, 1)] = C.inc(i3, "tE", 1)
                    T(t_h)

                    def s_sil(s, pi=pi, hc=hc, th=th):
                        C.wait(s, "tE", h_mm[(th, pi, 1)])
                        if hc >= 1:
                            C.wait(s, "vE", gt_done[(th, hc - 1)])
                        elif th == 1:
                            C.wait(s, "vE", gt_done[(0, NHC - 1)])
                        i = s.activation(sil_s[:, :], PS[hc % 2][:, :], AF.Silu)
                        sE_sil[(th, hc)] = C.inc(i, "sE", 1)
                    Sc(s_sil)

                    def v_g(v, hc=hc, th=th):
                        C.wait(v, "sE", sE_sil[(th, hc)])
                        C.wait(v, "tE", C.val("tE"))
                        i = v.tensor_tensor(gt_v[:, hc, :], sil_s[:, :], PS[2 + hc % 2][:, :], ALU.mult)
                        gt_done[(th, hc)] = C.inc(i, "vE", 1)
                    V(v_g)

            if th == 0 and "gt" in taps:
                tpg = tapt("gt", [128, NHC, 512], BF16)

                @G
                def _(g):
                    C.wait(g, "vE", gt_done[(0, NHC - 1)])
                    i = g.dma_start(out=tpg[:, :, :], in_=gt_v[:, :, :])
                    gt_tap_gate[0] = C.inc(i, "dG", 16)
            if th == 0 and "fb" in taps:
                tpf = tapt("fb", [128, 4, 16, 256], BF16)

                @G
                def _(g):
                    C.wait(g, "vE", gt_done[(0, NHC - 1)])
                    i = g.dma_start(out=tpf[:, :, :, :], in_=fbuf[:, :, :, :])
                    gt_tap_gate[0] = C.inc(i, "dG", 16)

            if MEGABAR:
                megabarrier()
            dQ_sl = {}
            SUBH = [(0, 11), (11, 11), (22, 11), (33, 10)]
            for g in range(16):
                dcg, si = g // 4, g % 4
                sh0, snh = SUBH[si]

                def s_slab(s, g=g, dcg=dcg, sh0=sh0, snh=snh, th=th):
                    if g == 0:
                        C.wait(s, "cC", cC_t[2])
                        if th == 0:
                            C.wait(s, "vE", vE_ln2)
                    if g <= 1:
                        C.wait(s, "tE", h_mm[(th, len(HCP) - 1, 1)])
                    if g >= 2:
                        C.wait(s, "tE", w2sub_done[(th, g - 2)])
                    elif th == 1:
                        C.wait(s, "tE", w2sub_done[(0, 14 + g)])
                    src = t2_out[:, _ts(dcg, 512)]
                    sem = "dQ" if g % 2 == 0 else "dQ2"
                    i = s.dma_start(out=slab[:, g % 2, 0:snh, :],
                                    in_=src[sh0 * 128:(sh0 + snh) * 128, :].rearrange(
                                        "(h p) d -> p h d", p=128))
                    C.inc(i, sem, 16)
                    dQ_sl[(th, g)] = (sem, C.val(sem))
                Sc(s_slab)

                def t_w2s(t, g=g, dcg=dcg, si=si, sh0=sh0, snh=snh, th=th):
                    if si == 0:
                        C.wait(t, "vE", gt_done[(th, NHC - 1)])
                        if th == 1 and dcg == 0 and vE_out_prev[0]:
                            C.wait(t, "vE", vE_out_prev[0])
                        if (th, dcg - 1) in fstg_evac:
                            C.wait(t, "vE", fstg_evac[(th, dcg - 1)])
                        elif th == 1 and dcg == 0:
                            C.wait(t, "vE", fstg_evac[(0, 3)])
                    C.wait(t, dQ_sl[(th, g)][0], dQ_sl[(th, g)][1])
                    i = None
                    for hl in range(snh):
                        hc = sh0 + hl
                        for dcl in range(4):
                            i = t.matmul(PS[dcl][:, :], slab[:, g % 2, hl, _ts(dcl, 128)],
                                         gt_v[:, hc, :], start=(hc == 0), stop=(hc == NHC - 1))
                    w2sub_done[(th, g)] = C.inc(i, "tE", 1)
                    if si == 3:
                        w2_done[(th, dcg)] = w2sub_done[(th, g)]
                T(t_w2s)

                if si == 3:
                    def v_w2(v, dcg=dcg, th=th):
                        C.wait(v, "tE", w2_done[(th, dcg)])
                        i = None
                        for dcl in range(4):
                            i = v.tensor_copy(fstg[:, dcg * 4 + dcl, :], PS[dcl][:, :])
                        fstg_evac[(th, dcg)] = C.inc(i, "vE", 1)
                    V(v_w2)

            tE_fsq = {}
            for c in range(DC):
                def s_fsq(s, c=c, th=th):
                    if th == 1 and c == 0 and vE_out_prev[0]:
                        C.wait(s, "vE", vE_out_prev[0])
                    C.wait(s, "vE", fstg_evac[(th, c // 4)])
                    if c >= 1:
                        C.wait(s, "tE", tE_fsq[c - 1])
                    i = s.activation(fsq_v[:, :], fstg[:, c, :], AF.Square)
                    C.inc(i, "sE", 1)
                Sc(s_fsq)

                def t_fsq(t, c=c, th=th):
                    if th == 1 and c == 0 and vE_out_prev[0]:
                        C.wait(t, "vE", vE_out_prev[0])
                    C.wait(t, "vE", fstg_evac[(th, c // 4)])
                    t.matmul(PS[4][0:1, :], ones_bc[:, :], fstg[:, c, :], start=(c == 0), stop=(c == DC - 1))
                    C.wait(t, "sE", C.val("sE"))
                    i = t.matmul(PS[5][0:1, :], ones_bc[:, :], fsq_v[:, :], start=(c == 0), stop=(c == DC - 1))
                    tE_fsq[c] = C.inc(i, "tE", 1)
                T(t_fsq)

            emit_rowchain([(PS[4], slice(0, 512))], [(PS[5], slice(0, 512))], 512)

            def t_fbc(t):
                C.wait(t, "sE", C.val("sE"))
                t.matmul(PS[4][:, :], ones_fr[:, :], r_m[0:1, 0:512], start=True, stop=True)
                i = t.matmul(PS[5][:, :], ones_fr[:, :], r_istd[0:1, 0:512], start=True, stop=True)
                C.inc(i, "tE", 1)
            T(t_fbc)
            tE_fbc = C.val("tE")

            x1ld = {}

            @Sc
            def _(s, th=th):
                if th == 1 and vE_out_prev[0]:
                    C.wait(s, "vE", vE_out_prev[0])
                C.wait(s, "tE", w2_done[(th, 3)])
                for c in range(2):
                    sem = "dQ" if c % 2 == 0 else "dQ2"
                    i = s.dma_start(out=x1pb[c % 2], in_=x1_scr[_ts(c, 128), _ts(th, 512)])
                    x1ld[c] = (sem, C.inc(i, sem, 16))

            vE_o = {}
            dS_o = {}
            for c in range(DC):
                def v_out(v, c=c, th=th):
                    C.wait(v, "tE", tE_fbc)
                    C.wait(v, x1ld[c][0], x1ld[c][1])
                    if c >= 1:
                        C.wait(v, "dW", dS_o[c - 1])
                    v.tensor_tensor(fscr[:, 0:512], fstg[:, c, :], PS[4][:, :], ALU.subtract)
                    v.tensor_tensor(ftmp[:, 0:512], fscr[:, 0:512], PS[5][:, :], ALU.mult)
                    i = v.tensor_tensor(fscr[:, 0:512], ftmp[:, 0:512], x1pb[c % 2], ALU.add)
                    vE_o[c] = C.inc(i, "vE", 1)
                V(v_out)

                def sy_out(sy, c=c, th=th):
                    C.wait(sy, "vE", vE_o[c])
                    i = sy.dma_start(out=outT[_ts(c, 128), _ts(th, 512)], in_=fscr[:, 0:512])
                    dS_o[c] = C.inc(i, "dW", 16)
                    out_store[0] = dS_o[c]
                Sy(sy_out)

                def s_xr(s, c=c, th=th):
                    if c + 2 < DC:
                        C.wait(s, "vE", vE_o[c])
                        sem = "dQ" if c % 2 == 0 else "dQ2"
                        i = s.dma_start(out=x1pb[c % 2], in_=x1_scr[_ts(c + 2, 128), _ts(th, 512)])
                        x1ld[c + 2] = (sem, C.inc(i, sem, 16))
                Sc(s_xr)

            vE_out_prev[0] = vE_o[DC - 1]

        if "w2p" in taps:
            tpw = tapt("w2p", [128, 16, 512], F32)

            @G
            def _(g):
                C.wait(g, "vE", fstg_evac[(0, 3)])
                i = g.dma_start(out=tpw[:, :, :], in_=fstg[:, :, :])
                C.inc(i, "dG", 16)
        if "slb" in taps:
            tps = tapt("slb", [128, 2, 11, 512], BF16)

            @G
            def _(g):
                C.wait(g, "tE", w2sub_done[(0, 1)])
                i = g.dma_start(out=tps[:, :, :, :], in_=slab[:, :, :, :])
                C.inc(i, "dG", 16)
        if "lnx" in taps:
            tp = tapt("lnx", [4 * D, TC], BF16)

            @G
            def _(g):
                C.wait(g, "cC", cC_lnx)
                i = g.dma_start(out=tp[:, :], in_=lnx_out.ap())
                C.inc(i, "dG", 16)
        if "t1" in taps:
            tp = tapt("t1", [D, HIDP], BF16)

            @G
            def _(g):
                C.wait(g, "cC", cC_t[0])
                i = g.dma_start(out=tp[:, :], in_=t1_out.ap())
                C.inc(i, "dG", 16)
        if "t3" in taps:
            tp = tapt("t3", [D, HIDP], BF16)

            @G
            def _(g):
                C.wait(g, "cC", cC_t[1])
                i = g.dma_start(out=tp[:, :], in_=t3_out.ap())
                C.inc(i, "dG", 16)
        if "t3in" in taps:
            tp = tapt("t3in", [D_SH, HIDP], BF16)

            @G
            def _(g):
                C.wait(g, "cC", cC_t[1])
                i = g.dma_start(out=tp[:, :], in_=t3_in.ap())
                C.inc(i, "dG", 16)
        if "t2" in taps:
            tp = tapt("t2", [HIDP, D], BF16)

            @G
            def _(g):
                C.wait(g, "cC", cC_t[2])
                i = g.dma_start(out=tp[:, :], in_=t2_out.ap())
                C.inc(i, "dG", 16)
        if "ln2" in taps:
            tp = tapt("ln2", [128, 16, TC], BF16)

            @G
            def _(g):
                C.wait(g, "vE", vE_ln2)
                i = g.dma_start(out=tp[:, :, :], in_=ln2xt[:, :, :])
                C.inc(i, "dG", 16)
        if "z" in taps:
            tp = tapt("z", [128, HL, S], BF16)

            @G
            def _(g):
                C.wait(g, "vE", vE_zall)
                i = g.dma_start(out=tp[:, :, :], in_=z_v[:, :, :])
                C.inc(i, "dG", 16)
        if "rs" in taps:
            tp = tapt("rs", [D, TC], BF16)

            @G
            def _(g):
                C.wait(g, "cC", cC_rs[7])
                for k in range(8):
                    i = g.dma_start(out=tp[_ts(k, 256), :], in_=rs_out8[k].ap())
                    C.inc(i, "dG", 16)
        if "x1" in taps:
            tp = tapt("x1", [D, TC], F32)

            @G
            def _(g):
                C.wait(g, "dW", dS_x1w[DC - 1])
                i = g.dma_start(out=tp[:, :], in_=x1_scr.ap())
                C.inc(i, "dG", 16)
        if "qkv" in taps:
            tpq = tapt("qkv", [128, 3, 2, S], BF16)

            @G
            def _(g):
                C.wait(g, "vE", evac_done[47])
                i = g.dma_start(out=tpq[:, 0, :, :], in_=qt[:, :, :])
                C.inc(i, "dG", 16)
                i = g.dma_start(out=tpq[:, 1, :, :], in_=kt[:, :, :])
                C.inc(i, "dG", 16)
                i = g.dma_start(out=tpq[:, 2, :, :],
                                in_=vt.rearrange("p h k e -> p h (k e)"))
                C.inc(i, "dG", 16)

        if "t1b" in taps:
            tp = tapt("t1b", [D, HIDP], BF16)

            @G
            def _(g):
                C.wait(g, "dW", out_store[0])
                i = g.dma_start(out=tp[:, :], in_=t1_out.ap())
                C.inc(i, "dG", 16)
        if "ln2b" in taps:
            tp = tapt("ln2b", [128, 16, TC], BF16)

            @G
            def _(g):
                C.wait(g, "dW", out_store[0])
                i = g.dma_start(out=tp[:, :, :], in_=ln2xt[:, :, :])
                C.inc(i, "dG", 16)

        @Sy
        def _(sy):
            for nm in ("dS", "dS2", "dW", "dQ", "dQ2", "dG", "dG2", "dG3"):
                C.wait(sy, nm, C.val(nm))

    return tap_t



def _mk_masks():
    kl = np.arange(128)[:, None]
    ql = np.arange(512)[None, :]
    return np.stack([(128 * j + kl <= ql) for j in range(4)]).astype(ml_dtypes.bfloat16)


def make_core_inputs(inp):
    x = np.asarray(inp["L0_attn_x"], dtype=np.float32).reshape(B * S, D)
    wq = np.asarray(inp["L0_attn_W_Q"], dtype=np.float32)
    wk = np.asarray(inp["L0_attn_W_K"], dtype=np.float32)
    wv = np.asarray(inp["L0_attn_W_V"], dtype=np.float32)
    wo = np.asarray(inp["L0_attn_W_O"], dtype=np.float32)
    w1 = np.asarray(inp["L0_ffn_w1"], dtype=np.float32)
    w3 = np.asarray(inp["L0_ffn_w3"], dtype=np.float32)
    w2 = np.asarray(inp["L0_ffn_w2"], dtype=np.float32)

    w1p = np.zeros((HIDP, D), np.float32); w1p[:HID] = w1
    w3p = np.zeros((HIDP, D), np.float32); w3p[:HID] = w3
    w2p = np.zeros((D, HIDP), np.float32); w2p[:, :HID] = w2
    w1Tf = np.ascontiguousarray(w1p.T)
    w3Tf = np.ascontiguousarray(w3p.T)
    w2Tf = np.ascontiguousarray(w2p.T)
    masks = _mk_masks()
    ident = np.eye(128, dtype=ml_dtypes.bfloat16)

    maps = []
    for c in range(NCORE):
        h0 = 4 * (c % 4)
        wqkv_host = np.empty((128, 3, HL, DC, 128), np.float32)
        for m, w in enumerate((wq, wk, wv)):
            for h in range(HL):
                wqkv_host[:, m, h] = w[h0 + h].T.reshape(DC, 128, 128).transpose(1, 0, 2)
        wo_host = np.empty((128, HL, DC, 128), np.float32)
        for h in range(HL):
            wo_host[:, h] = wo[h0 + h].T.reshape(128, DC, 128)
        maps.append({
            "xT": np.ascontiguousarray(x[c * TC:(c + 1) * TC, :].T),
            "wqkv_in": wqkv_host,
            "wo_in": wo_host,
            "w1T": np.ascontiguousarray(w1Tf[c * D_SH:(c + 1) * D_SH, :]),
            "w3T": np.ascontiguousarray(w3Tf[c * D_SH:(c + 1) * D_SH, :]),
            "w2T": np.ascontiguousarray(w2Tf[c * HID_SH:(c + 1) * HID_SH, :]),
            "masks": masks,
            "ident": ident,
        })
    return maps


def kernel(**inputs):
    nc = bass.Bass("TRN2", target_bir_lowering=False)
    build_program(nc)
    maps = make_core_inputs(inputs)
    res = run_bass_kernel_spmd(nc, maps, core_ids=list(range(NCORE)))
    parts = [np.ascontiguousarray(res.results[c]["outT"].T) for c in range(NCORE)]
    return np.concatenate(parts, axis=0).reshape(B, S, D).astype(np.float32)
